# version 04
# baseline (speedup 1.0000x reference)
import numpy as np
from contextlib import ExitStack

import concourse.bass as bass
import concourse.mybir as mybir
from concourse.bass_utils import run_bass_kernel_spmd
from concourse.bass_core_rust import IndirectOffsetOnAxis

F32 = mybir.dt.float32
BF16 = mybir.dt.bfloat16
F16 = mybir.dt.float16
I32 = mybir.dt.int32
AL = mybir.AluOpType
AF = mybir.ActivationFunctionType
AX = mybir.AxisListType

B, S, D = 4, 2048, 2048
T = B * S
TL = 1024
NTL = 8
NGT = 64
E, EPC, K = 32, 4, 4
I_FF, ISH = 1024, 2048
CAP, CPAD = 320, 384
SEG = EPC * CPAD
DUMMY_ROW = 8 * SEG
RES_ROWS = 8 * SEG + 128
NG, TKG = 8, 3
HPC, DH = 8, 128
NB = 16
EPS = 1e-5
ISQ = 0.08838834764831845
NEGBIG = -1.0e9

STAGE = 3
ASUB = 99


class Prog:
    ENGINES = ("sync", "scalar", "vector", "tensor", "gpsimd")

    def __init__(self, nc):
        self.nc = nc
        self.streams = {e: [] for e in self.ENGINES}
        self.ctx = ExitStack()
        self.sems = {}
        self.bnd = {}

    def sem(self, name):
        h = self.ctx.enter_context(self.nc.semaphore(name))
        self.sems[name] = [h, 0]
        return name

    def val(self, name):
        return self.sems[name][1]

    DMA_SEMS = {"s_sy": "sync", "s_sc": "scalar", "s_po": "gpsimd"}

    def op(self, engine, fn, wait=(), inc=None, inc_n=1, drain=False):
        nw = []
        for (n, v) in wait:
            if n in self.DMA_SEMS and v > 0:
                tot = self.sems[n][1]
                if tot not in self.bnd.setdefault(n, {0}):
                    qeng = self.DMA_SEMS[n]
                    h = self.sems[n][0]
                    self.streams[qeng].append(
                        (None, ((n, tot),), None, 0, False))
                    self.bnd[n].add(tot)
                nw.append((n, tot))
            else:
                nw.append((n, v))
        wait = nw
        if engine in ("sync", "scalar", "gpsimd") and wait:
            for sname, qeng in self.DMA_SEMS.items():
                if qeng == engine:
                    self.bnd.setdefault(sname, {0}).add(self.sems[sname][1])
        after = None
        if inc is not None:
            self.sems[inc][1] += inc_n
            after = self.sems[inc][1]
        self.streams[engine].append((fn, tuple(wait), inc, inc_n, drain))
        return after

    def build_block(self):
        blk = self.ctx.enter_context(self.nc.Block())
        for eng_name in self.ENGINES:
            stream = self.streams[eng_name]

            def make(stream):
                def emit(eng):
                    for fn, waits, inc, inc_n, drain in stream:
                        for sname, v in waits:
                            if v > 0:
                                eng.wait_ge(self.sems[sname][0], v)
                        inst = fn(eng) if fn is not None else None
                        if drain or (inc is not None and inst is None):
                            inst = eng.drain()
                        if inc is not None:
                            inst.then_inc(self.sems[inc][0], inc_n)
                return emit

            getattr(blk, eng_name)(make(stream))


def build_program():
    nc = bass.Bass("TRN2", target_bir_lowering=False)
    p = Prog(nc)
    EN = p.ctx.enter_context

    d_x0 = nc.dram_tensor("x0", [TL, D], F32, kind="ExternalInput").ap()
    d_wq = nc.dram_tensor("wq", [D, HPC * DH], F32, kind="ExternalInput").ap()
    d_wk = nc.dram_tensor("wk", [D, HPC * DH], F32, kind="ExternalInput").ap()
    d_wv = nc.dram_tensor("wv", [D, HPC * DH], F32, kind="ExternalInput").ap()
    d_bq = nc.dram_tensor("bq", [HPC * DH, 1], F32, kind="ExternalInput").ap()
    d_bk = nc.dram_tensor("bk", [HPC * DH, 1], F32, kind="ExternalInput").ap()
    d_bv = nc.dram_tensor("bv", [1, HPC * DH], F32, kind="ExternalInput").ap()
    d_rwT = nc.dram_tensor("rwT", [D, E], F32, kind="ExternalInput").ap()
    d_ebias = nc.dram_tensor("ebias", [128, E], F32, kind="ExternalInput").ap()
    d_wg = nc.dram_tensor("wg", [EPC, D, I_FF], F32, kind="ExternalInput").ap()
    d_wu = nc.dram_tensor("wu", [EPC, D, I_FF], F32, kind="ExternalInput").ap()
    d_wd = nc.dram_tensor("wd", [EPC, I_FF, D], F32, kind="ExternalInput").ap()
    d_sgT = nc.dram_tensor("sgT", [D, ISH], F32, kind="ExternalInput").ap()
    d_suT = nc.dram_tensor("suT", [D, ISH], F32, kind="ExternalInput").ap()
    d_sdT = nc.dram_tensor("sdT", [ISH, D], F32, kind="ExternalInput").ap()
    d_sel = nc.dram_tensor("sel", [128, EPC * E], F32, kind="ExternalInput").ap()
    d_cconst = nc.dram_tensor("cconst", [128, 4], F32, kind="ExternalInput").ap()
    d_rmask = nc.dram_tensor("rmask", [8, 1], F32, kind="ExternalInput").ap()
    d_out = nc.dram_tensor("xout", [TL, D], F32, kind="ExternalOutput").ap()

    d_xln_my = nc.dram_tensor("xln_my", [TL, D], BF16, kind="Internal").ap()
    d_xln_ag = nc.dram_tensor("xln_ag", [T, D], BF16, kind="Internal").ap()
    d_xba = nc.dram_tensor("xba", [S, D], BF16, kind="Internal").ap()
    d_scT_my = nc.dram_tensor("scT_my", [E, TL], F32, kind="Internal").ap()
    d_scT_ag = nc.dram_tensor("scT_ag", [8 * E, TL], F32, kind="Internal").ap()
    d_cnt_my = nc.dram_tensor("cnt_my", [1, E], F32, kind="Internal").ap()
    d_cnt_ag = nc.dram_tensor("cnt_ag", [8, E], F32, kind="Internal").ap()
    d_res_my = nc.dram_tensor("res_my", [SEG, D], BF16, kind="Internal").ap()
    d_res_ag = nc.dram_tensor("res_ag", [RES_ROWS, D], BF16, kind="Internal").ap()
    d_xp = nc.dram_tensor("xp_rows", [EPC, CPAD, D], BF16, kind="Internal").ap()
    d_shared = nc.dram_tensor("shared_out", [TL, D], BF16, kind="Internal").ap()
    d_x1 = nc.dram_tensor("x1", [TL, D], F32, kind="Internal").ap()
    d_x2 = nc.dram_tensor("x2", [TL, D], F32, kind="Internal").ap()
    d_qk = nc.dram_tensor("qk_spill", [2 * HPC * DH, S], BF16, kind="Internal").ap()
    d_ctx_my = nc.dram_tensor("ctx_my", [HPC * 2049, DH], BF16, kind="Internal").ap()
    d_ctx_pag = nc.dram_tensor("ctx_pag", [2 * HPC * 2049, DH], BF16,
                               kind="Internal").ap()
    d_xg = nc.dram_tensor("xg_rows", [HPC * 128, D], BF16, kind="Internal").ap()

    c_i384 = EN(nc.sbuf_tensor("c_i384", [128, CPAD], F16))
    c_i32 = EN(nc.sbuf_tensor("c_i32", [128, E], F32))
    c_i32m = EN(nc.sbuf_tensor("c_i32m", [128, E], F32))
    c_i8 = EN(nc.sbuf_tensor("c_i8", [128, NG], F32))
    c_i8m = EN(nc.sbuf_tensor("c_i8m", [128, NG], F32))
    c_i64 = EN(nc.sbuf_tensor("c_i64", [128, 64], F32))
    c_ut = EN(nc.sbuf_tensor("c_ut", [128, 128], F32))
    c_id16 = EN(nc.sbuf_tensor("c_id16", [128, 128], BF16))
    c_id32 = EN(nc.sbuf_tensor("c_id32", [128, 128], F32))
    c_ones_col = EN(nc.sbuf_tensor("c_ones_col", [128, 1], F32))
    c_ones_row = EN(nc.sbuf_tensor("c_ones_row", [1, 128], F32))
    c_ones16 = EN(nc.sbuf_tensor("c_ones16", [1, 128], BF16))
    c_ebias = EN(nc.sbuf_tensor("c_ebias", [128, E], F32))
    c_gatok = EN(nc.sbuf_tensor("c_gatok", [128, 16, 3], F32))
    c_rowb = EN(nc.sbuf_tensor("c_rowb", [128, 32], F32))
    c_bqkv = EN(nc.sbuf_tensor("c_bqkv", [128, 2 * HPC], F32))
    c_bv16 = EN(nc.sbuf_tensor("c_bv16", [1, HPC * DH], BF16))
    c_sel = EN(nc.sbuf_tensor("c_sel", [128, EPC, E], F32))
    c_cc = EN(nc.sbuf_tensor("c_cc", [128, 4], F32))
    c_rmask = EN(nc.sbuf_tensor("c_rmask", [8, 1], F32))
    c_cnt = EN(nc.sbuf_tensor("c_cnt", [8, E], F32))

    A = EN(nc.sbuf_tensor("A", [128, 16, 2048], BF16))
    Bf = EN(nc.sbuf_tensor("Bf", [128, 12, 1024], F32))
    Cc = EN(nc.sbuf_tensor("Cc", [128, 16, 1024], BF16))
    Dd = EN(nc.sbuf_tensor("Dd", [128, 16, 1024], BF16))
    a_oh = EN(nc.sbuf_tensor("a_oh", [128, 2, CPAD], F16))
    a_lhsT = EN(nc.sbuf_tensor("a_lhsT", [128, NGT, 3], F16))
    a_stat = EN(nc.sbuf_tensor("a_stat", [128, 16], F32))
    a_slotT = EN(nc.sbuf_tensor("a_slotT", [128, 12, 3], F32))
    a_wcol = EN(nc.sbuf_tensor("a_wcol", [128, 12], F32))
    a_idx = EN(nc.sbuf_tensor("a_idx", [128, 16], I32))
    a_carry = EN(nc.sbuf_tensor("a_carry", [1, E], F32))
    a_carry2 = EN(nc.sbuf_tensor("a_carry2", [1, E], F32))
    a_crow = EN(nc.sbuf_tensor("a_crow", [E, 1], F32))
    a_ek = EN(nc.sbuf_tensor("a_ek", [128, K, NGT], F32))
    a_wk = EN(nc.sbuf_tensor("a_wk", [128, K, NGT], F32))
    a_ws = EN(nc.sbuf_tensor("a_ws", [128, NGT], F32))
    a_sgmy = EN(nc.sbuf_tensor("a_sgmy", [128, NTL, K], I32))
    a_gidx = EN(nc.sbuf_tensor("a_gidx", [64, 8], F32))
    a_gi32 = EN(nc.sbuf_tensor("a_gi32", [128, 2], I32))
    a_sci = EN(nc.sbuf_tensor("a_sci", [128, HPC], I32))
    a_x32 = EN(nc.sbuf_tensor("a_x32", [128, 2, 2048], F32))
    a_qgT = EN(nc.sbuf_tensor("a_qgT", [128, 64], BF16))
    a_gsum = EN(nc.sbuf_tensor("a_gsum", [64, 8], F32))
    a_gctx = EN(nc.sbuf_tensor("a_gctx", [64, DH], BF16))
    a_y = EN(nc.sbuf_tensor("a_y", [128, 2, 512], BF16))

    ps = [EN(nc.psum_tensor(f"ps{i}", [128, 512], F32)) for i in range(8)]

    Bff = Bf.ap().rearrange("p a b -> p (a b)")

    def bfv(lo, hi):
        return Bff[:, lo * 1024:hi * 1024]

    Af = A.ap().rearrange("p a b -> p (a b)")
    Df = Dd.ap().rearrange("p a b -> p (a b)")
    Cf = Cc.ap().rearrange("p a b -> p (a b)")

    v_xlnT = Af[:, 0:16 * 1024].rearrange("p (a b) -> p a b", b=1024)
    v_shG = Af[:, 16 * 1024:24 * 1024].rearrange("p (a b) -> p a b", b=512)
    v_shU = Af[:, 24 * 1024:32 * 1024].rearrange("p (a b) -> p a b", b=512)
    v_shY = Af[:, 24 * 1024:32 * 1024].rearrange("p (a b) -> p a b", b=2048)
    f_scores = bfv(0, 2).rearrange("p (g e) -> p g e", e=E)
    a_sslot = bfv(0, 2)[0:3, 0:EPC * CPAD]
    a_lM = bfv(0, 1)[:, 0:256].rearrange("p (g e) -> p g e", e=E)
    a_lP = bfv(0, 1)[:, 256:512].rearrange("p (g e) -> p g e", e=E)
    a_lek = bfv(0, 1)[:, 512:544].rearrange("p (k g) -> p k g", g=NTL)
    a_lrk = bfv(0, 1)[:, 544:576].rearrange("p (k g) -> p k g", g=NTL)
    a_lws = bfv(0, 1)[:, 576:584]
    f_sc = bfv(2, 4).rearrange("p (g e) -> p g e", e=E)
    f_M = bfv(4, 6).rearrange("p (g e) -> p g e", e=E)
    a_scTb = bfv(4, 6)[0:E, 0:TL]
    f_Ww = bfv(6, 8).rearrange("p (g e) -> p g e", e=E)
    f_P = bfv(8, 10).rearrange("p (g e) -> p g e", e=E)
    f_oh = bfv(10, 12).rearrange("p (g e) -> p g e", e=E)
    f_scT3 = Bf[0:E, 4:12, :]
    f_sq = bfv(2, 4)
    f_xstage = bfv(6, 8)
    f_scmy = bfv(10, 11)[:, 0:256].rearrange("p (a b) -> p a b", b=E)
    f_g2 = bfv(10, 11)[:, 512:1024].rearrange("p (g n) -> p g n", n=NG)
    f_gm = bfv(11, 12)[:, 0:512].rearrange("p (g n) -> p g n", n=NG)
    f_t2 = bfv(11, 12)[:, 512:1024].rearrange("p (g n) -> p g n", n=NG)
    l_oh = bfv(10, 11)[:, 256:512].rearrange("p (g e) -> p g e", e=E)
    l_g2 = bfv(11, 12)[:, 0:64].rearrange("p (g n) -> p g n", n=NG)
    l_t2 = bfv(11, 12)[:, 64:128].rearrange("p (g n) -> p g n", n=NG)
    l_gm = bfv(11, 12)[:, 128:192].rearrange("p (g n) -> p g n", n=NG)
    l_sc = bfv(11, 12)[:, 768:1024].rearrange("p (g n) -> p g n", n=E) \
        if False else bfv(2, 3)[:, 0:256].rearrange("p (g e) -> p g e", e=E)

    v_xpT = Df[:, 0:6 * 1024].rearrange("p (a b) -> p a b", b=CPAD)
    v_eg = Df[:, 6 * 1024:9 * 1024].rearrange("p (a b) -> p a b", b=CPAD)[:, 0:8, :]
    v_eu = Df[:, 9 * 1024:12 * 1024].rearrange("p (a b) -> p a b", b=CPAD)[:, 0:8, :]
    v_gb = Df[:, 12 * 1024:16 * 1024].rearrange("p (a b) -> p a b", b=2048)
    v_comb = Df[:, 0:8 * 1024].rearrange("p (a b) -> p a b", b=2048)

    t_xgT = Af[:, 0:16 * 1024].rearrange("p (a b) -> p a b", b=1024)
    t_q0 = Df[:, 0:2048]
    t_k0 = Df[:, 2048:4096]
    t_q1 = Df[:, 4096:6144]
    t_k1 = Df[:, 6144:8192]
    t_whd = Df[:, 8192:10240].rearrange("p (a b) -> p a b", b=128)
    t_probs = Df[:, 14336:14720]
    t_probsT = Df[:, 14720:15104]
    t_gpT = Df[:, 10752:11776].rearrange("p (a b) -> p a b", b=64)
    t_ctxs = Df[:, 11776:12288].rearrange("p (a b) -> p a b", b=128)
    t_gq = Df[0:64, 12288:14336]
    t_asm = Df[:, 0:16 * 1024].rearrange("p (a b) -> p a b", b=1024)
    n_norm = bfv(0, 2)[0:8, :]
    n_work = bfv(2, 4)[0:8, :]
    n_r = bfv(4, 6)[0:8, :]
    n_m = bfv(6, 8)[0:8, :]
    n_p0 = bfv(8, 10)[0:1, 0:2048]
    a_rcm = bfv(8, 10)[:, 512:768].rearrange("p (g t n) -> p g t n", t=2, n=8)
    a_wtf = bfv(8, 10)[:, 768:832]
    a_pmy = bfv(10, 11)[:, 0:256].rearrange("p (g l) -> p g l", l=EPC)
    a_wmy = bfv(10, 11)[:, 256:512].rearrange("p (g l) -> p g l", l=EPC)
    c_rwT = bfv(8, 10)[:, 1024:1536].rearrange("p (a b) -> p a b", b=E)
    n_sq = bfv(10, 12)

    s_sy = p.sem("s_sy")
    s_sc = p.sem("s_sc")
    s_po = p.sem("s_po")
    s_te = p.sem("s_te")
    s_ve = p.sem("s_ve")
    s_ac = p.sem("s_ac")
    s_ge = p.sem("s_ge")
    s_cc = p.sem("s_cc")

    def dma(queue, out, in_, wait=(), slow=False):
        inc = {"sync": s_sy, "scalar": s_sc, "gpsimd": s_po}[queue]
        return p.op(queue, lambda e, o=out, i=in_: e.dma_start(
            out=o, in_=i, allow_slow_non_contiguous=slow),
            wait=wait, inc=inc, inc_n=16)

    def dmaT(queue, out, in_, wait=()):
        inc = {"sync": s_sy, "scalar": s_sc}[queue]
        return p.op(queue, lambda e, o=out, i=in_: e.dma_start_transpose(o, i),
                    wait=wait, inc=inc, inc_n=16)

    def vop(fn, wait=(), drain=True):
        return p.op("vector", fn, wait=wait, inc=s_ve, drain=drain)

    def sop(fn, wait=(), drain=True):
        return p.op("scalar", fn, wait=wait, inc=s_ac, drain=drain)

    def top(fn, wait=()):
        return p.op("tensor", fn, wait=wait, inc=s_te)

    def gind(out, in_, idx_ap, wait=()):
        return p.op("gpsimd", lambda e, o=out, i=in_, x=idx_ap: e.indirect_dma_start(
            out=o, out_offset=None, in_=i, in_offset=IndirectOffsetOnAxis(x, 0)),
            wait=wait, inc=s_po, inc_n=16)

    def gscat(out, idx_ap, in_, wait=()):
        return p.op("gpsimd", lambda e, o=out, i=in_, x=idx_ap: e.indirect_dma_start(
            out=o, out_offset=IndirectOffsetOnAxis(x, 0), in_=i, in_offset=None),
            wait=wait, inc=s_po, inc_n=16)

    def coll(ins, outs, groups, wait=()):
        return p.op("gpsimd", lambda e, i=ins, o=outs, g=groups: e.collective_compute(
            "AllGather", AL.bypass, ins=[i], outs=[o], replica_groups=g),
            wait=wait, inc=s_cc)

    G8 = [list(range(8))]
    GP = [[0, 1], [2, 3], [4, 5], [6, 7]]

    def fence():
        vals = [(n, p.sems[n][1]) for n in
                ("s_sy", "s_sc", "s_po", "s_te", "s_ve", "s_ac", "s_ge", "s_cc")]
        for eng in Prog.ENGINES:
            p.op(eng, None, wait=vals, inc=None, drain=False)

    def bc1(ap2d, nrep, m):
        return ap2d.unsqueeze(1).to_broadcast([128, nrep, m])

    def bc2(ap2d, m):
        return ap2d.unsqueeze(2).to_broadcast([128, ap2d.shape[1], m])

    def iota(dst, pattern, base, chm):
        return lambda e, d=dst, pt=pattern, b=base, c=chm: e.iota(
            d, pattern=pt, base=b, channel_multiplier=c,
            allow_small_or_imprecise_dtypes=True)

    for args in ((c_i384[:], [[1, CPAD]], 0, 0), (c_i32[:], [[1, E]], 0, 0),
                 (c_i32m[:], [[1, E]], -32, 0), (c_i8[:], [[1, NG]], 0, 0),
                 (c_i8m[:], [[1, NG]], -8, 0), (c_i64[:], [[1, 64]], 0, 0),
                 (c_ut[:], [[1, 128]], 0, -1), (c_id32[:], [[-1, 128]], 0, 1),
                 (c_rowb[:], [[CPAD, 8], [CPAD * 8, EPC]], 0, 0),
                 (a_lhsT[:, :, 0:1], [[0, NGT]], 0, 1),
                 (a_lhsT[:, :, 1:2], [[1, NGT]], 0, 0),
                 (c_gatok[:, :, 0:1], [[0, 16]], 0, 1),
                 (c_gatok[:, :, 1:2], [[1, 16]], 0, 0)):
        p.op("gpsimd", iota(*args), inc=None)
    g_iotas = p.op("gpsimd", lambda e: None, inc=s_ge, drain=True)

    vop(lambda e: e.memset(c_ones_col[:], 1.0))
    vop(lambda e: e.memset(a_stat[:, 15:16], float(EPS)))
    vop(lambda e: e.memset(c_ones_row[:], 1.0))
    vop(lambda e: e.memset(c_ones16[:], 1.0))
    vop(lambda e: e.memset(c_gatok[:, :, 2:3], 1.0), wait=[(s_ge, g_iotas)])
    vop(lambda e: e.tensor_scalar(out=c_ut[:], in0=c_ut[:], scalar1=0.0,
                                  scalar2=None, op0=AL.is_gt))
    vop(lambda e: e.tensor_scalar(out=c_id32[:], in0=c_id32[:], scalar1=0.0,
                                  scalar2=None, op0=AL.is_equal))
    v_const = vop(lambda e: e.tensor_copy(out=c_id16[:], in_=c_id32[:]))

    dma("sync", c_ebias[:], d_ebias[:])
    dma("sync", c_sel.ap().rearrange("p a b -> p (a b)"), d_sel[:])
    dma("sync", c_cc[:], d_cconst[:])
    dma("sync", c_rmask[:], d_rmask[:])
    dma("sync", c_bqkv[:, 0:HPC].rearrange("p (a o) -> p a o", o=1),
        d_bq.rearrange("(a p) o -> p a o", p=128), slow=True)
    dma("sync", c_bqkv[:, HPC:2 * HPC].rearrange("p (a o) -> p a o", o=1),
        d_bk.rearrange("(a p) o -> p a o", p=128), slow=True)
    ld_bv = dma("gpsimd", c_bv16[:], d_bv[:])
    v_z = vop(lambda e: e.memset(v_gb[:, 0, :], 0.0))
    dma("sync", d_res_ag[DUMMY_ROW:DUMMY_ROW + 128, :], v_gb[:, 0, :],
        wait=[(s_ve, v_z)])
    pad_st = dma("sync", d_ctx_my.rearrange("(h r) d -> h r d", r=2049)[:, 2048, :],
                 v_gb[0:8, 0, 0:DH])

    R = {"xln_my": [], "xln_ag": [], "scT": [], "res": [], "cnt": [],
         "xba": [], "ctx": []}

    def emit_ln_tile(x_src_rows, xb, xlnb, sq_buf, extra_f32_out=None,
                     ld_wait=(), out_wait=()):
        mu, ssq, var, rstd, negb = (a_stat[:, i:i + 1] for i in range(5))
        ld = dma("sync", xb, x_src_rows, wait=list(ld_wait))
        a1 = sop(lambda e: e.activation(out=sq_buf, in_=xb, func=AF.Square,
                                        accum_out=ssq), wait=[(s_sy, ld)])
        vop(lambda e: e.reduce_sum(out=mu, in_=xb, axis=AX.X), wait=[(s_sy, ld)])
        vop(lambda e: e.tensor_scalar_mul(mu, mu, 1.0 / D))
        vop(lambda e: e.tensor_scalar_mul(var, ssq, 1.0 / D), wait=[(s_ac, a1)])
        vop(lambda e: e.tensor_mul(ssq, mu, mu))
        v1 = vop(lambda e: e.tensor_sub(var, var, ssq))
        a2 = sop(lambda e: e.activation(out=rstd, in_=var, func=AF.Sqrt,
                                        bias=a_stat[:, 15:16]),
                 wait=[(s_ve, v1)])
        v2 = vop(lambda e: e.reciprocal(rstd, rstd), wait=[(s_ac, a2)])
        v3 = vop(lambda e: e.scalar_tensor_tensor(
            out=negb, in0=mu, scalar=-1.0, in1=rstd, op0=AL.mult, op1=AL.mult),
            wait=[(s_ve, v2)])
        a3 = sop(lambda e: e.activation(out=xlnb, in_=xb, func=AF.Identity,
                                        bias=negb, scale=rstd),
                 wait=[(s_ve, v3)] + list(out_wait))
        a4 = None
        if extra_f32_out is not None:
            a4 = sop(lambda e: e.activation(out=extra_f32_out, in_=xb,
                                            func=AF.Identity, bias=negb,
                                            scale=rstd), wait=[(s_ve, v3)])
        return a3, a4

    def emit_router(nt, scores_v, sc_v, oh_v, g2_v, gm_v, t2_v, M_v, ek_t,
                    wk_t, ws_t, want_w, wait0):
        vop(lambda e: e.tensor_tensor(out=sc_v[:], in0=scores_v[:],
                                      in1=bc1(c_ebias[:], nt, E), op=AL.add),
            wait=wait0)
        sc4 = sc_v.rearrange("p g (n f) -> p g n f", f=4)
        first = True
        for (i, j) in ((0, 1), (0, 2), (0, 3), (1, 2), (1, 3), (2, 3)):
            vop(lambda e, i=i, j=j: e.tensor_tensor(
                out=t2_v[:], in0=sc4[:, :, :, i], in1=sc4[:, :, :, j], op=AL.add))
            if first:
                vop(lambda e: e.tensor_copy(out=g2_v[:], in_=t2_v[:]))
                first = False
            else:
                vop(lambda e: e.tensor_tensor(out=g2_v[:], in0=g2_v[:], in1=t2_v[:],
                                              op=AL.max))
        vop(lambda e: e.memset(gm_v[:], 0.0))
        ws_g = ws_t[:, 0:nt]
        for r in range(TKG):
            vop(lambda e: e.reduce_max(out=ws_g, in_=g2_v[:], axis=AX.X))
            vop(lambda e: e.tensor_tensor(out=t2_v[:], in0=g2_v[:],
                                          in1=bc2(ws_g, NG), op=AL.is_equal))
            vop(lambda e: e.tensor_tensor(out=t2_v[:], in0=t2_v[:],
                                          in1=bc1(c_i8m[:], nt, NG), op=AL.mult))
            vop(lambda e: e.tensor_reduce(out=ws_g, in_=t2_v[:], axis=AX.X,
                                          op=AL.min))
            vop(lambda e: e.tensor_scalar_add(ws_g, ws_g, 8.0))
            vop(lambda e: e.tensor_tensor(out=t2_v[:], in0=bc1(c_i8[:], nt, NG),
                                          in1=bc2(ws_g, NG), op=AL.is_equal))
            vop(lambda e: e.scalar_tensor_tensor(out=g2_v[:], in0=t2_v[:],
                                                 scalar=NEGBIG, in1=g2_v[:],
                                                 op0=AL.mult, op1=AL.add))
            vop(lambda e: e.tensor_tensor(out=gm_v[:], in0=gm_v[:], in1=t2_v[:],
                                          op=AL.add))
        vop(lambda e: e.tensor_tensor(
            out=sc4[:], in0=sc4[:],
            in1=gm_v.unsqueeze(3).to_broadcast([128, nt, NG, 4]), op=AL.mult))
        vop(lambda e: e.memset(M_v[:], 0.0))
        for k in range(K):
            vop(lambda e: e.reduce_max(out=ws_g, in_=sc_v[:], axis=AX.X))
            vop(lambda e: e.tensor_tensor(out=oh_v[:], in0=sc_v[:],
                                          in1=bc2(ws_g, E), op=AL.is_equal))
            vop(lambda e: e.tensor_tensor(out=oh_v[:], in0=oh_v[:],
                                          in1=bc1(c_i32m[:], nt, E), op=AL.mult))
            vop(lambda e, k=k: e.tensor_reduce(out=ek_t[:, k, 0:nt], in_=oh_v[:],
                                               axis=AX.X, op=AL.min))
            vop(lambda e, k=k: e.tensor_scalar_add(ek_t[:, k, 0:nt],
                                                   ek_t[:, k, 0:nt], 32.0))
            vop(lambda e, k=k: e.tensor_tensor(
                out=oh_v[:], in0=bc1(c_i32[:], nt, E),
                in1=bc2(ek_t[:, k, 0:nt], E), op=AL.is_equal))
            vop(lambda e: e.scalar_tensor_tensor(out=sc_v[:], in0=oh_v[:],
                                                 scalar=NEGBIG, in1=sc_v[:],
                                                 op0=AL.mult, op1=AL.add))
            vop(lambda e: e.tensor_tensor(out=M_v[:], in0=M_v[:], in1=oh_v[:],
                                          op=AL.add))
            if want_w:
                vop(lambda e: e.tensor_tensor(out=oh_v[:], in0=oh_v[:],
                                              in1=scores_v[:], op=AL.mult))
                vop(lambda e, k=k: e.reduce_sum(out=wk_t[:, k, 0:nt], in_=oh_v[:],
                                                axis=AX.X))
        if want_w:
            vop(lambda e: e.tensor_add(ws_g, wk_t[:, 0, 0:nt], wk_t[:, 1, 0:nt]))
            vop(lambda e: e.tensor_add(ws_g, ws_g, wk_t[:, 2, 0:nt]))
            vop(lambda e: e.tensor_add(ws_g, ws_g, wk_t[:, 3, 0:nt]))
            vop(lambda e: e.tensor_scalar_add(ws_g, ws_g, 1.0e-20))
            vop(lambda e: e.reciprocal(ws_g, ws_g))
            vop(lambda e: e.tensor_scalar_mul(ws_g, ws_g, 2.5))
            for k in range(K):
                vop(lambda e, k=k: e.tensor_tensor(out=wk_t[:, k, 0:nt],
                                                   in0=wk_t[:, k, 0:nt],
                                                   in1=ws_g, op=AL.mult))
        return vop(lambda e: None)

    def emit_ffn(x_in, x_out, call):
        ld_rw = dma("sync", c_rwT[:], d_rwT.rearrange("(a p) b -> p a b", p=128))
        sc_ev = []
        ln_st = []
        a4p = {}
        for g in range(NTL):
            xb = a_x32[:, g % 2, :]
            xlnb = v_gb[:, g % 2, :]
            ldw = [(s_ac, a4p[g - 2])] if g >= 2 else []
            outw = [(s_sy, ln_st[g - 2])] if g >= 2 else []
            a3, a4 = emit_ln_tile(x_in[g * 128:(g + 1) * 128, :], xb, xlnb,
                                  f_sq[:], f_xstage[:], ld_wait=ldw,
                                  out_wait=outw)
            a4p[g] = a4
            wmy = R["xln_my"] if g == 0 else []
            ln_st.append(dma("sync", d_xln_my[g * 128:(g + 1) * 128, :], xlnb,
                             wait=[(s_ac, a3)] + wmy))
            evs = []
            for dmt in range(16):
                t1 = top(lambda e, dmt=dmt: e.transpose(
                    ps[1][:, 0:128], f_xstage[:, dmt * 128:(dmt + 1) * 128],
                    c_id32[:]),
                    wait=[(s_ac, a4), (s_ve, v_const)] +
                    ([(s_ve, evs[-1])] if evs else []))
                evs.append(vop(lambda e, dmt=dmt: e.tensor_copy(
                    out=f_sq[:, dmt * 128:(dmt + 1) * 128], in_=ps[1][:, 0:128]),
                    wait=[(s_te, t1)]))
            xtT = f_sq.rearrange("p (a b) -> p a b", b=128)

            def mmsc(e, xtT=xtT):
                for dmt in range(16):
                    e.matmul(ps[0][:, 0:E], xtT[:, dmt, :], c_rwT[:, dmt, :],
                             start=(dmt == 0), stop=(dmt == 15))
            t2 = top(mmsc, wait=[(s_ve, evs[-1]), (s_sy, ld_rw)] +
                     ([(s_ac, sc_ev[-1])] if sc_ev else []))
            sc_ev.append(sop(lambda e, g=g: e.activation(
                out=f_scmy[:, g, :], in_=ps[0][:, 0:E], func=AF.Sigmoid),
                wait=[(s_te, t2)]))
        ln_all = ln_st[-1]
        sc_all = sc_ev[-1]

        ag_x = coll(d_xln_my[:], d_xln_ag[:], G8,
                    wait=[(s_sy, ln_all)] + R["xln_ag"])

        trs = []
        for ts in range(NTL):
            t1 = top(lambda e, ts=ts: e.transpose(
                ps[1][0:E, 0:128], f_scmy[:, ts, :], c_id32[:]),
                wait=[(s_ac, sc_ev[ts])] + ([(s_ve, trs[-1])] if trs else []))
            trs.append(vop(lambda e, ts=ts: e.tensor_copy(
                out=a_scTb[0:E, ts * 128:(ts + 1) * 128], in_=ps[1][0:E, 0:128]),
                wait=[(s_te, t1)]))
        st_sc = dma("sync", d_scT_my[:], a_scTb[:],
                    wait=[(s_ve, trs[-1])] + R["scT"])
        ag_s = coll(d_scT_my[:], d_scT_ag[:], G8, wait=[(s_sy, st_sc)])

        xlT = [dmaT("scalar", v_xlnT[:, dmt, :],
                    d_xln_my[0:TL, dmt * 128:(dmt + 1) * 128],
                    wait=[(s_sy, ln_all)]) for dmt in range(16)]
        xlT_all = xlT[-1]

        v_loc = emit_router(NTL, f_scmy, l_sc, l_oh, l_g2, l_gm, l_t2, a_lM,
                            a_lek, a_lrk, a_lws, False, [(s_ac, sc_all)])
        def mmcnt(e):
            for g in range(NTL):
                e.matmul(ps[2][0:1, 0:E], c_ones_col[:], a_lM[:, g, :],
                         start=(g == 0), stop=(g == NTL - 1),
                         skip_group_check=True)
        t_cnt = top(mmcnt, wait=[(s_ve, v_loc)])
        v_cr = vop(lambda e: e.tensor_copy(out=a_carry[:], in_=ps[2][0:1, 0:E]),
                   wait=[(s_te, t_cnt)])
        st_cnt = dma("sync", d_cnt_my[:], a_carry[:],
                     wait=[(s_ve, v_cr)] + R["cnt"])
        ag_cnt = coll(d_cnt_my[:], d_cnt_ag[:], G8, wait=[(s_sy, st_cnt)])
        ld_cnt = dma("sync", c_cnt[:], d_cnt_ag[:], wait=[(s_cc, ag_cnt)])
        t_cb = top(lambda e: e.matmul(ps[2][0:1, 0:E], c_rmask[:], c_cnt[:],
                                      start=True, stop=True),
                   wait=[(s_sy, ld_cnt)])
        v_cba = vop(lambda e: e.tensor_copy(out=a_carry2[:], in_=ps[2][0:1, 0:E]),
                   wait=[(s_te, t_cb)])
        lp_ev = []
        ev_c = v_cba
        for g in range(NTL):
            pb = ps[3] if (g % 2 == 0) else ps[4]
            w1 = [(s_ve, ev_c)]
            if g >= 2:
                w1.append((s_ac, lp_ev[g - 2]))
            top(lambda e, g=g, pb=pb: e.matmul(
                pb[:, 0:E], c_ut[:], a_lM[:, g, :], start=True, stop=False,
                skip_group_check=True), wait=w1)
            t2 = top(lambda e, pb=pb: e.matmul(
                pb[:, 0:E], c_ones_row[:], a_carry2[:], start=False, stop=True,
                skip_group_check=True))
            t3 = top(lambda e, g=g: e.matmul(
                ps[2][0:1, 0:E], c_ones_col[:], a_lM[:, g, :],
                start=True, stop=True, skip_group_check=True))
            lp_ev.append(sop(lambda e, g=g, pb=pb: e.copy(
                out=a_lP[:, g, :], in_=pb[:, 0:E]), wait=[(s_te, t2)]))
            ev_c = vop(lambda e: e.tensor_add(a_carry2[:], a_carry2[:],
                                              ps[2][0:1, 0:E]),
                       wait=[(s_te, t3)])
        lp_all = lp_ev[-1]
        for k in range(K):
            vop(lambda e, k=k: e.tensor_tensor(
                out=l_oh[:], in0=bc1(c_i32[:], NTL, E),
                in1=bc2(a_lek[:, k, 0:NTL], E), op=AL.is_equal),
                wait=[(s_ac, lp_all)] if k == 0 else ())
            vop(lambda e: e.tensor_tensor(out=l_oh[:], in0=l_oh[:],
                                          in1=bc1(c_rowb[:], NTL, E),
                                          op=AL.mult))
            vop(lambda e, k=k: e.reduce_sum(out=a_lws[:, 0:NTL], in_=l_oh[:],
                                            axis=AX.X))
            vop(lambda e, k=k: e.tensor_tensor(
                out=l_oh[:], in0=bc1(c_i32[:], NTL, E),
                in1=bc2(a_lek[:, k, 0:NTL], E), op=AL.is_equal))
            vop(lambda e: e.tensor_tensor(out=l_oh[:], in0=l_oh[:], in1=a_lP[:],
                                          op=AL.mult))
            vop(lambda e, k=k: e.reduce_sum(out=a_lrk[:, k, :], in_=l_oh[:],
                                            axis=AX.X))
            vop(lambda e, k=k: e.tensor_tensor(
                out=a_lws[:, 0:NTL], in0=a_lws[:, 0:NTL],
                in1=a_lrk[:, k, :], op=AL.add))
            vop(lambda e, k=k: e.tensor_scalar(
                out=l_t2.rearrange("p a b -> p (a b)")[:, 0:NTL],
                in0=a_lrk[:, k, :], scalar1=float(CAP), scalar2=None,
                op0=AL.is_lt))
            vop(lambda e: e.tensor_tensor(
                out=a_lws[:, 0:NTL], in0=a_lws[:, 0:NTL],
                in1=l_t2.rearrange("p a b -> p (a b)")[:, 0:NTL], op=AL.mult))
            vop(lambda e: e.tensor_scalar(
                out=l_t2.rearrange("p a b -> p (a b)")[:, 0:NTL],
                in0=l_t2.rearrange("p a b -> p (a b)")[:, 0:NTL],
                scalar1=1.0, scalar2=float(-DUMMY_ROW),
                op0=AL.subtract, op1=AL.mult))
            vop(lambda e: e.tensor_tensor(
                out=a_lws[:, 0:NTL], in0=a_lws[:, 0:NTL],
                in1=l_t2.rearrange("p a b -> p (a b)")[:, 0:NTL], op=AL.add))
            vop(lambda e, k=k: e.tensor_copy(out=a_sgmy[:, :, k],
                                             in_=a_lws[:, 0:NTL]))
        v_sgmy = vop(lambda e: None)

        ld_sct = dma("sync", f_scT3[:, :, :],
                     d_scT_ag.rearrange("(r e) t -> e r t", e=E),
                     wait=[(s_cc, ag_s), (s_ve, v_sgmy)])
        scb = []
        f_scT_flat = f_scT3.rearrange("e a t -> e (a t)")
        for g in range(NGT):
            t1 = top(lambda e, g=g: e.transpose(
                ps[1][:, 0:E], f_scT_flat[:, g * 128:(g + 1) * 128],
                c_id32[0:E, 0:E]),
                wait=[(s_sy, ld_sct)] + ([(s_ve, scb[-1])] if scb else []))
            scb.append(vop(lambda e, g=g: e.tensor_copy(
                out=f_scores[:, g, :], in_=ps[1][:, 0:E]), wait=[(s_te, t1)]))
        v_rt = emit_router(NGT, f_scores, f_sc, f_oh, f_g2, f_gm, f_t2, f_M,
                           a_ek, a_wk, a_ws, True, [(s_ve, scb[-1])])
        vop(lambda e: e.memset(f_Ww[:], 0.0), wait=[(s_ve, v_rt)])
        for k in range(K):
            vop(lambda e, k=k: e.tensor_tensor(
                out=f_oh[:], in0=bc1(c_i32[:], NGT, E),
                in1=bc2(a_ek[:, k, 0:NGT], E), op=AL.is_equal))
            vop(lambda e, k=k: e.tensor_tensor(
                out=f_oh[:], in0=f_oh[:], in1=bc2(a_wk[:, k, 0:NGT], E),
                op=AL.mult))
            vop(lambda e: e.tensor_tensor(out=f_Ww[:], in0=f_Ww[:], in1=f_oh[:],
                                          op=AL.add))
        v_ww = vop(lambda e: None)

        ev_c = vop(lambda e: e.memset(a_carry[:], 0.0), wait=[(s_ve, v_ww)])
        gp_ev = []
        for g in range(NGT):
            pb = ps[3] if (g % 2 == 0) else ps[4]
            w1 = [(s_ve, ev_c)]
            if g >= 2:
                w1.append((s_ac, gp_ev[g - 2]))
            top(lambda e, g=g, pb=pb: e.matmul(
                pb[:, 0:E], c_ut[:], f_M[:, g, :], start=True, stop=False,
                skip_group_check=True), wait=w1)
            t2 = top(lambda e, pb=pb: e.matmul(
                pb[:, 0:E], c_ones_row[:], a_carry[:], start=False, stop=True,
                skip_group_check=True))
            t3 = top(lambda e, g=g: e.matmul(
                ps[2][0:1, 0:E], c_ones_col[:], f_M[:, g, :],
                start=True, stop=True, skip_group_check=True))
            gp_ev.append(sop(lambda e, g=g, pb=pb: e.copy(
                out=f_P[:, g, :], in_=pb[:, 0:E]), wait=[(s_te, t2)]))
            ev_c = vop(lambda e: e.tensor_add(a_carry[:], a_carry[:],
                                              ps[2][0:1, 0:E]),
                       wait=[(s_te, t3)])
        v_ranks = gp_ev[-1]

        vop(lambda e: e.tensor_tensor(out=f_oh[:], in0=f_P[:], in1=f_M[:],
                                      op=AL.mult), wait=[(s_ac, v_ranks)])
        vop(lambda e: e.tensor_scalar_add(f_M[:], f_M[:], -1.0))
        v_pm = vop(lambda e: e.tensor_tensor(out=f_P[:], in0=f_oh[:], in1=f_M[:],
                                             op=AL.add))
        for el in range(EPC):
            vop(lambda e, el=el: e.tensor_tensor(
                out=f_sc[:], in0=f_P[:],
                in1=c_sel[:, el, :].unsqueeze(1).to_broadcast([128, NGT, E]),
                op=AL.mult))
            vop(lambda e, el=el: e.reduce_sum(out=a_pmy[:, :, el], in_=f_sc[:],
                                              axis=AX.X))
            vop(lambda e, el=el: e.tensor_tensor(
                out=f_sc[:], in0=f_Ww[:],
                in1=c_sel[:, el, :].unsqueeze(1).to_broadcast([128, NGT, E]),
                op=AL.mult))
            vop(lambda e, el=el: e.reduce_sum(out=a_wmy[:, :, el], in_=f_sc[:],
                                              axis=AX.X))
        v_pm = vop(lambda e: None)

        idx_ready = []
        prev_mm = None
        prev_evl = None
        prev_sv = None
        for el in range(EPC):
            vl = vop(lambda e, el=el: e.tensor_copy(
                out=a_lhsT[:, :, 2], in_=a_wmy[:, :, el]),
                wait=[(s_ve, v_pm)] + ([(s_te, prev_mm)] if prev_mm else []))
            ohl = []
            mm_last = None
            for g in range(NGT):
                wv = [(s_ve, vl)] if g == 0 else []
                if g >= 2:
                    wv.append((s_te, ohl[g - 2]))
                ohv = vop(lambda e, g=g, el=el: e.tensor_scalar(
                    out=a_oh[:, g % 2, :], in0=c_i384[:],
                    scalar1=a_pmy[:, g, el:el + 1], scalar2=None,
                    op0=AL.is_equal), wait=wv)
                mmw = [(s_ve, ohv)]
                if g == 0 and prev_evl is not None:
                    mmw.append((s_ac, prev_evl))
                mm_last = top(lambda e, g=g: e.matmul(
                    ps[0][0:3, 0:CPAD], a_lhsT[:, g, :], a_oh[:, g % 2, :],
                    start=(g == 0), stop=(g == NGT - 1), skip_group_check=True),
                    wait=mmw)
                ohl.append(mm_last)
            prev_mm = mm_last
            evl = sop(lambda e, el=el: e.copy(
                out=a_sslot[:, el * CPAD:(el + 1) * CPAD], in_=ps[0][0:3, 0:CPAD]),
                wait=[(s_te, mm_last)])
            prev_evl = evl
            for st in range(3):
                t1 = top(lambda e, el=el, st=st: e.transpose(
                    ps[1][:, 0:3],
                    a_sslot[:, el * CPAD + st * 128:el * CPAD + (st + 1) * 128],
                    c_id32[0:3, 0:3]), wait=[(s_ac, evl)] +
                    ([(s_ve, prev_sv)] if prev_sv else []))
                prev_sv = vop(lambda e, el=el, st=st: e.tensor_copy(
                    out=a_slotT[:, el * 3 + st, :], in_=ps[1][:, 0:3]),
                    wait=[(s_te, t1)])
                vop(lambda e, el=el, st=st: None) if False else None
                vop(lambda e, el=el, st=st: e.scalar_tensor_tensor(
                    out=a_slotT[:, el * 3 + st, 1:2],
                    in0=a_slotT[:, el * 3 + st, 1:2], scalar=128.0,
                    in1=a_slotT[:, el * 3 + st, 0:1], op0=AL.mult, op1=AL.add))
                vop(lambda e, el=el, st=st: e.tensor_copy(
                    out=a_idx[:, el * 3 + st:el * 3 + st + 1],
                    in_=a_slotT[:, el * 3 + st, 1:2]))
                idx_ready.append(vop(lambda e, el=el, st=st: e.tensor_copy(
                    out=a_wcol[:, el * 3 + st:el * 3 + st + 1],
                    in_=a_slotT[:, el * 3 + st, 2:3])))
        v_idx = idx_ready[-1]

        xp_stores = {}
        prev_store = {0: None, 1: None}
        for el in range(EPC):
            gts = []
            for st in range(3):
                gw = [(s_ve, v_idx), (s_cc, ag_x)]
                if prev_store[st % 2] is not None:
                    gw.append((s_sy, prev_store[st % 2]))
                gi = gind(v_gb[:, st % 2, :], d_xln_ag[:],
                          a_idx[:, el * 3 + st:el * 3 + st + 1], wait=gw)
                stv = dma("sync", d_xp[el, st * 128:(st + 1) * 128, :],
                          v_gb[:, st % 2, :], wait=[(s_po, gi)])
                gts.append(stv)
                prev_store[st % 2] = stv
            xp_stores[el] = gts[-1]

        sh_st = []
        for tc in range(2):
            t0 = tc * 512
            pass_w = [(s_sy, sh_st[-1])] if sh_st else []
            prev_ev = {3: None, 4: None}
            prev_wl = {0: None, 1: None}
            for phase, (dsrc, dstv, act, pb, pbi) in enumerate(
                    ((d_sgT, v_shG, AF.Silu, ps[3], 3),
                     (d_suT, v_shU, AF.Identity, ps[4], 4))):
                for is_ in range(16):
                    buf = is_ % 2
                    wlw = []
                    if prev_wl[buf] is not None:
                        wlw.append((s_te, prev_wl[buf]))
                    wl = dma("gpsimd",
                             Cc[:, buf * 2:(buf + 1) * 2, :].rearrange(
                                 "p a b -> p (a b)").rearrange(
                                 "p (a b) -> p a b", b=128),
                             dsrc[:, is_ * 128:(is_ + 1) * 128].rearrange(
                                 "(a p) b -> p a b", p=128), wait=wlw)
                    wv = Cf[:, buf * 2048:(buf + 1) * 2048].rearrange(
                        "p (a b) -> p a b", b=128)

                    def mmsh(e, wv=wv, t0=t0, pb=pb):
                        for dmt in range(16):
                            e.matmul(pb[:, 0:512], wv[:, dmt, :],
                                     v_xlnT[:, dmt, t0:t0 + 512],
                                     start=(dmt == 0), stop=(dmt == 15))
                    t1 = top(mmsh, wait=[(s_po, wl), (s_sc, xlT_all)] +
                             pass_w +
                             ([(s_ac, prev_ev[pbi])] if prev_ev[pbi] else []))
                    prev_wl[buf] = t1
                    prev_ev[pbi] = sop(
                        lambda e, is_=is_, dstv=dstv, act=act, pb=pb: e.activation(
                            out=dstv[:, is_, :], in_=pb[:, 0:512], func=act),
                        wait=[(s_te, t1)])
            vhh = vop(lambda e: e.tensor_tensor(
                out=v_shG.rearrange("p a b -> p (a b)"),
                in0=v_shG.rearrange("p a b -> p (a b)"),
                in1=v_shU.rearrange("p a b -> p (a b)"), op=AL.mult),
                wait=[(s_ac, prev_ev[3]), (s_ac, prev_ev[4])])
            a1_prev = [None] * 4
            dn_last = None
            for dc in range(4):
                wl = dma("gpsimd",
                         Cc[:, 8:16, :].rearrange("p a b -> p (a b)").rearrange(
                             "p (a b) -> p a b", b=512),
                         d_sdT[:, dc * 512:(dc + 1) * 512].rearrange(
                             "(a p) b -> p a b", p=128),
                         wait=[(s_te, dn_last)] if dn_last else ())
                wv = Cf[:, 8 * 1024:16 * 1024].rearrange("p (a b) -> p a b", b=512)
                for tt in range(4):
                    pb = ps[3 + tt]

                    def mmdn(e, tt=tt, wv=wv, pb=pb):
                        for it in range(16):
                            e.matmul(pb[:, 0:512],
                                     v_shG[:, it, tt * 128:(tt + 1) * 128],
                                     wv[:, it, :], start=(it == 0), stop=(it == 15))
                    t1 = top(mmdn, wait=[(s_ve, vhh), (s_po, wl)] +
                             ([(s_ac, a1_prev[tt])] if a1_prev[tt] else []))
                    dn_last = t1
                    a1 = sop(lambda e, dc=dc, tt=tt, pb=pb: e.activation(
                        out=v_shY[:, tt, dc * 512:(dc + 1) * 512],
                        in_=pb[:, 0:512], func=AF.Identity), wait=[(s_te, t1)])
                    a1_prev[tt] = a1
                    if dc == 3:
                        sh_st.append(dma("sync",
                                         d_shared[t0 + tt * 128:
                                                  t0 + (tt + 1) * 128, :],
                                         v_shY[:, tt, :],
                                         wait=[(s_ac, a1)]))
        sh_all = sh_st[-1]

        y_st = []
        y_prev = {0: None, 1: None}
        ev_prev2 = None
        wd_prev = {0: None, 1: None}
        for el in range(EPC):
            xls = [dmaT("scalar", v_xpT[:, dmt, :],
                        d_xp[el, 0:CPAD, dmt * 128:(dmt + 1) * 128],
                        wait=[(s_sy, xp_stores[el])]) for dmt in range(16)]
            pe_prev = {0: None, 1: None}
            we_prev = {}
            for phase, (dsrc, dstv, act, pb, cbase) in enumerate(
                    ((d_wg, v_eg, AF.Silu, ps[0], 0),
                     (d_wu, v_eu, AF.Identity, ps[1], 4))):
                for is_ in range(8):
                    buf = cbase + (is_ % 2) * 2
                    wlw = [(s_te, we_prev[buf])] if buf in we_prev else []
                    wl = dma("gpsimd",
                             Cc[:, buf:buf + 2, :].rearrange(
                                 "p a b -> p (a b)").rearrange(
                                 "p (a b) -> p a b", b=128),
                             dsrc[el, :, is_ * 128:(is_ + 1) * 128].rearrange(
                                 "(a p) b -> p a b", p=128), wait=wlw)
                    wv = Cf[:, buf * 1024:(buf + 2) * 1024].rearrange(
                        "p (a b) -> p a b", b=128)

                    def mme(e, wv=wv, pb=pb):
                        for dmt in range(16):
                            e.matmul(pb[:, 0:CPAD], wv[:, dmt, :], v_xpT[:, dmt, :],
                                     start=(dmt == 0), stop=(dmt == 15))
                    t1 = top(mme, wait=[(s_po, wl), (s_sc, xls[-1])] +
                             ([(s_ac, pe_prev[phase])] if pe_prev[phase] else []))
                    we_prev[buf] = t1
                    pe_prev[phase] = sop(
                        lambda e, is_=is_, dstv=dstv, act=act, pb=pb: e.activation(
                            out=dstv[:, is_, :], in_=pb[:, 0:CPAD], func=act),
                        wait=[(s_te, t1)])
            vhh = vop(lambda e: e.tensor_tensor(
                out=v_eg.rearrange("p a b -> p (a b)"),
                in0=v_eg.rearrange("p a b -> p (a b)"),
                in1=v_eu.rearrange("p a b -> p (a b)"), op=AL.mult),
                wait=[(s_ac, pe_prev[0]), (s_ac, pe_prev[1])])
            for dc in range(4):
                buf = 8 + (dc % 2) * 4
                wlw = [(s_te, wd_prev[dc % 2])] if wd_prev[dc % 2] else []
                wl = dma("gpsimd",
                         Cc[:, buf:buf + 4, :].rearrange(
                             "p a b -> p (a b)").rearrange(
                             "p (a b) -> p a b", b=512),
                         d_wd[el, :, dc * 512:(dc + 1) * 512].rearrange(
                             "(a p) b -> p a b", p=128), wait=wlw)
                wv = Cf[:, buf * 1024:(buf + 4) * 1024].rearrange(
                    "p (a b) -> p a b", b=512)
                for st in range(3):
                    def mmd(e, st=st, wv=wv):
                        for it in range(8):
                            e.matmul(ps[2][:, 0:512],
                                     v_eg[:, it, st * 128:(st + 1) * 128],
                                     wv[:, it, :], start=(it == 0), stop=(it == 7))
                    t1 = top(mmd, wait=[(s_ve, vhh), (s_po, wl)] +
                             ([(s_ve, ev_prev2)] if ev_prev2 else []))
                    wd_prev[dc % 2] = t1
                    evw = [(s_te, t1)]
                    if y_prev[st % 2] is not None:
                        evw.append((s_sy, y_prev[st % 2]))
                    ev = vop(lambda e, st=st, dc=dc, el=el: e.tensor_scalar(
                        out=a_y[:, st % 2, :], in0=ps[2][:, 0:512],
                        scalar1=a_wcol[:, el * 3 + st:el * 3 + st + 1],
                        scalar2=None, op0=AL.mult), wait=evw)
                    ev_prev2 = ev
                    yv = dma("sync",
                             d_res_my[el * CPAD + st * 128:
                                      el * CPAD + (st + 1) * 128,
                                      dc * 512:(dc + 1) * 512],
                             a_y[:, st % 2, :], wait=[(s_ve, ev)])
                    y_st.append(yv)
                    y_prev[st % 2] = yv
        res_my_v = d_res_my.rearrange("(e r) d -> e r d", r=CPAD)
        for el in range(EPC):
            ag_r = coll(res_my_v[el],
                        d_res_ag[el * 8 * CPAD:(el + 1) * 8 * CPAD, :],
                        G8, wait=[(s_sy, y_st[12 * (el + 1) - 1])] + R["res"])

        out_st = []
        last_g = None
        v1p = None
        v4p = None
        for g in range(NTL):
            gsems = []
            for k in range(K):
                gw = [(s_cc, ag_r)]
                if v1p is not None:
                    gw.append((s_ve, v1p))
                gi = gind(v_comb[:, k, :], d_res_ag[:],
                          a_sgmy[:, g, k:k + 1], wait=gw)
                gsems.append(gi)
            last_g = gsems[-1]
            xw = [(s_sy, out_st[g - 2])] if g >= 2 else []
            ldx = dma("sync", a_x32[:, g % 2, :], x_in[g * 128:(g + 1) * 128, :],
                      wait=xw)
            lds = dma("scalar", v_gb[:, 1, :], d_shared[g * 128:(g + 1) * 128, :],
                      wait=[(s_sy, sh_all)] + ([(s_ve, v4p)] if v4p else []))
            v1 = vop(lambda e: e.tensor_tensor(out=f_sq[:], in0=v_comb[:, 0, :],
                                               in1=v_comb[:, 1, :], op=AL.add),
                     wait=[(s_po, gsems[-1])])
            vop(lambda e: e.tensor_tensor(out=f_xstage[:], in0=v_comb[:, 2, :],
                                          in1=v_comb[:, 3, :], op=AL.add))
            vop(lambda e: e.tensor_tensor(out=f_sq[:], in0=f_sq[:],
                                          in1=f_xstage[:], op=AL.add))
            v4p = vop(lambda e: e.tensor_tensor(out=f_sq[:], in0=f_sq[:],
                                                in1=v_gb[:, 1, :], op=AL.add),
                      wait=[(s_sc, lds)])
            v5 = vop(lambda e, g=g: e.scalar_tensor_tensor(
                out=a_x32[:, g % 2, :], in0=f_sq[:], scalar=0.5,
                in1=a_x32[:, g % 2, :], op0=AL.mult, op1=AL.add),
                wait=[(s_sy, ldx)])
            v1p = v5
            out_st.append(dma("sync", x_out[g * 128:(g + 1) * 128, :],
                              a_x32[:, g % 2, :], wait=[(s_ve, v5)]))
        R["xln_my"] = [(s_sc, xlT_all)]
        R["xln_ag"] = [(s_sy, xp_stores[EPC - 1])]
        R["scT"] = [(s_sy, ld_sct)]
        R["res"] = [(s_po, last_g)]
        R["cnt"] = [(s_sy, ld_cnt)]
        return out_st[-1]

    def emit_attn(x_in, x_out):
        def sub_exit(w0):
            prev = None
            for q in range(4):
                w = list(w0) if q == 0 else []
                prev = dma("sync", x_out[q * 256:(q + 1) * 256, :],
                           x_in[q * 256:(q + 1) * 256, :], wait=w)
            return prev
        ln_st = []
        a3p = {}
        for g in range(NTL):
            xb = a_x32[:, g % 2, :]
            xlnb = v_gb[:, g % 2, :]
            ldw = [(s_ac, a3p[g - 2])] if g >= 2 else []
            outw = [(s_sy, ln_st[g - 2])] if g >= 2 else []
            a3, _ = emit_ln_tile(x_in[g * 128:(g + 1) * 128, :], xb, xlnb,
                                 n_sq[:], None, ld_wait=ldw, out_wait=outw)
            a3p[g] = a3
            wmy = R["xln_my"] if g == 0 else []
            ln_st.append(dma("sync", d_xln_my[g * 128:(g + 1) * 128, :], xlnb,
                             wait=[(s_ac, a3)] + wmy))
        ln_all = ln_st[-1]
        ag_b = coll(d_xln_my[:], d_xba[:], GP, wait=[(s_sy, ln_all)] + R["xba"])

        xlT = [dmaT("scalar", A[:, dmt, :],
                    d_xba[0:S, dmt * 128:(dmt + 1) * 128],
                    wait=[(s_cc, ag_b)]) for dmt in range(16)]
        xlT_all = xlT[-1]
        axlnT = A

        wvl = []
        for ch in range(2):
            wvl.append(dma("gpsimd",
                           Dd[:, ch * 8:(ch + 1) * 8, :].rearrange(
                               "p a b -> p (a b)").rearrange(
                               "p (a b) -> p a b", b=512),
                           d_wv[:, ch * 512:(ch + 1) * 512].rearrange(
                               "(a p) b -> p a b", p=128)))
        v_ev = []
        for ch in range(2):
            wv = Df[:, ch * 8192:(ch + 1) * 8192].rearrange(
                "p (a b) -> p a b", b=512)
            for ts in range(16):
                def mmv(e, ch=ch, ts=ts, wv=wv):
                    for dmt in range(16):
                        e.matmul(ps[6][:, 0:512],
                                 axlnT[:, dmt, ts * 128:(ts + 1) * 128],
                                 wv[:, dmt, :], start=(dmt == 0), stop=False,
                                 skip_group_check=True)
                    e.matmul(ps[6][:, 0:512], c_ones16[:],
                             c_bv16[:, ch * 512:(ch + 1) * 512],
                             start=False, stop=True, skip_group_check=True)
                t1 = top(mmv, wait=[(s_po, wvl[ch]), (s_sc, xlT_all),
                                    (s_po, ld_bv)] +
                         ([(s_ac, v_ev[-1])] if v_ev else []))
                v_ev.append(sop(lambda e, ch=ch, ts=ts: e.activation(
                    out=Cc[:, ts, ch * 512:(ch + 1) * 512], in_=ps[6][:, 0:512],
                    func=AF.Identity), wait=[(s_te, t1)]))
        v_all = v_ev[-1]
        if ASUB == 1:
            return sub_exit([(s_ac, v_all)])

        qk_sp = []
        nrm_dma = []
        prev_k_te = None
        prev_spill = {0: None, 1: None}
        q_ev_pp = None
        k_ev_pp = None
        nev_pp = None
        for h in range(HPC):
            hb = h % 2
            qdst = t_q0 if hb == 0 else t_q1
            kdst = t_k0 if hb == 0 else t_k1
            wqw = [(s_ac, v_all)] if h == 0 else [(s_te, prev_k_te)]
            if prev_spill[hb] is not None:
                wqw.append((s_sy, prev_spill[hb]))
            wql = dma("gpsimd",
                      Dd[:, 8:10, :].rearrange("p a b -> p (a b)").rearrange(
                          "p (a b) -> p a b", b=128),
                      d_wq[:, h * 128:(h + 1) * 128].rearrange(
                          "(a p) b -> p a b", p=128),
                      wait=wqw)
            qev = []
            for tc in range(4):
                def mmq(e, tc=tc):
                    for dmt in range(16):
                        e.matmul(ps[0][:, 0:512], t_whd[:, dmt, :],
                                 axlnT[:, dmt, tc * 512:(tc + 1) * 512],
                                 start=(dmt == 0), stop=(dmt == 15))
                t1 = top(mmq, wait=[(s_po, wql)] +
                         ([(s_ac, qev[-1])] if qev else
                          ([(s_ac, q_ev_pp)] if q_ev_pp else [])))
                qev.append(sop(lambda e, tc=tc, h=h, qdst=qdst: e.activation(
                    out=qdst[:, tc * 512:(tc + 1) * 512], in_=ps[0][:, 0:512],
                    func=AF.Identity, bias=c_bqkv[:, h:h + 1]),
                    wait=[(s_te, t1)]))
            wkl = dma("gpsimd",
                      Dd[:, 8:10, :].rearrange("p a b -> p (a b)").rearrange(
                          "p (a b) -> p a b", b=128),
                      d_wk[:, h * 128:(h + 1) * 128].rearrange(
                          "(a p) b -> p a b", p=128),
                      wait=[(s_te, p.val("s_te"))])
            kev = []
            for tc in range(4):
                def mmk(e, tc=tc):
                    for dmt in range(16):
                        e.matmul(ps[1][:, 0:512], t_whd[:, dmt, :],
                                 axlnT[:, dmt, tc * 512:(tc + 1) * 512],
                                 start=(dmt == 0), stop=(dmt == 15))
                t1 = top(mmk, wait=[(s_po, wkl)] +
                         ([(s_ac, kev[-1])] if kev else
                          ([(s_ac, k_ev_pp)] if k_ev_pp else [])))
                kev.append(sop(lambda e, tc=tc, h=h, kdst=kdst: e.activation(
                    out=kdst[:, tc * 512:(tc + 1) * 512], in_=ps[1][:, 0:512],
                    func=AF.Identity, bias=c_bqkv[:, HPC + h:HPC + h + 1]),
                    wait=[(s_te, t1)]))
            aq = sop(lambda e, qdst=qdst: e.activation(
                out=n_sq[:], in_=qdst, func=AF.Square), wait=[(s_ac, qev[-1])])
            nevs = []
            for chk in range(4):
                nw = [(s_ac, aq)]
                if nevs:
                    nw.append((s_ve, nevs[-1]))
                elif nev_pp is not None:
                    nw.append((s_ve, nev_pp))
                t1 = top(lambda e, chk=chk: e.matmul(
                    ps[2][0:1, 0:512], c_ones_col[:],
                    n_sq[:, chk * 512:(chk + 1) * 512], start=True, stop=True,
                    skip_group_check=True), wait=nw)
                cw = [(s_te, t1)]
                if not nevs and nrm_dma:
                    cw.append((s_sy, nrm_dma[-1]))
                nevs.append(vop(lambda e, chk=chk: e.tensor_copy(
                    out=n_p0[:, chk * 512:(chk + 1) * 512], in_=ps[2][0:1, 0:512]),
                    wait=cw))
            nrm_dma.append(dma("sync", n_norm[h:h + 1, :], n_p0[:],
                               wait=[(s_ve, nevs[-1])]))
            qk_sp.append(dma("sync", d_qk[h * 128:(h + 1) * 128, :], qdst,
                             wait=[(s_ac, aq)]))
            ksp = dma("sync", d_qk[1024 + h * 128:1024 + (h + 1) * 128, :],
                      kdst, wait=[(s_ac, kev[-1])])
            qk_sp.append(ksp)
            prev_spill[hb] = ksp
            prev_k_te = p.val("s_te")
            q_ev_pp = qev[-1]
            k_ev_pp = kev[-1]
            nev_pp = nevs[-1]
        qk_all = qk_sp[-1]
        if ASUB == 2:
            return sub_exit([(s_sy, qk_all)])

        v0 = vop(lambda e: e.tensor_copy(out=n_work[:], in_=n_norm[:]),
                 wait=[(s_sy, nrm_dma[-1])])
        for rnd in range(8):
            vop(lambda e: e.max(out=a_gsum[0:8, :], in_=n_work[:]))
            if rnd == 7:
                vop(lambda e: e.memset(a_gsum[0:8, 6:8], -1.0))
            vop(lambda e: e.match_replace(out=n_work[:], in_to_replace=a_gsum[0:8, :],
                                          in_values=n_work[:], imm_value=-1.0))
        vop(lambda e: e.tensor_tensor(out=n_m[:], in0=n_norm[:], in1=n_work[:],
                                      op=AL.not_equal))
        vop(lambda e: e.memset(n_m[0:8, 0:1], 1.0))
        vop(lambda e: e.memset(n_m[0:8, 2047:2048], 1.0))
        vop(lambda e: e.tensor_tensor_scan(out=n_r[:], data0=n_m[:], data1=n_m[:],
                                           initial=0.0, op0=AL.add, op1=AL.bypass))
        v_scan = vop(lambda e: e.tensor_sub(n_r[:], n_r[:], n_m[:]))
        rc_ev = []
        for g in range(16):
            t1 = top(lambda e, g=g: e.transpose(
                ps[2][:, 0:8], n_r[:, g * 128:(g + 1) * 128], c_id32[0:8, 0:8]),
                wait=[(s_ve, v_scan)] + ([(s_ve, rc_ev[-1])] if rc_ev else []))
            rc_ev.append(vop(lambda e, g=g: e.tensor_copy(
                out=a_rcm[:, g, 0, :], in_=ps[2][:, 0:8]), wait=[(s_te, t1)]))
            t2 = top(lambda e, g=g: e.transpose(
                ps[2][:, 0:8], n_m[:, g * 128:(g + 1) * 128], c_id32[0:8, 0:8]),
                wait=[(s_ve, rc_ev[-1])])
            rc_ev.append(vop(lambda e, g=g: e.tensor_copy(
                out=a_rcm[:, g, 1, :], in_=ps[2][:, 0:8]), wait=[(s_te, t2)]))
        v_rcm = rc_ev[-1]
        if ASUB == 3:
            return sub_exit([(s_ve, v_rcm)])

        xg_st = []
        wx_prev = None
        gi_prev = None
        for h in range(HPC):
            mml = None
            for g in range(16):
                w0l = []
                if g == 0 and h == 0:
                    w0l = [(s_ve, v_rcm)]
                elif g == 0:
                    w0l = [(s_ve, wx_prev)]
                else:
                    w0l = [(s_te, mml)]
                v1 = vop(lambda e, g=g, h=h: e.tensor_scalar(
                    out=a_wtf[:], in0=c_i64[:],
                    scalar1=a_rcm[:, g, 0, h:h + 1], scalar2=None,
                    op0=AL.is_equal), wait=w0l)
                v2 = vop(lambda e, g=g, h=h: e.tensor_scalar(
                    out=a_wtf[:], in0=a_wtf[:],
                    scalar1=a_rcm[:, g, 1, h:h + 1], scalar2=None, op0=AL.mult))
                mml = top(lambda e, g=g: e.matmul(
                    ps[3][0:64, 0:3], a_wtf[:, 0:64], c_gatok[:, g, :],
                    start=(g == 0), stop=(g == 15), skip_group_check=True),
                    wait=[(s_ve, v2)])
            ev = vop(lambda e, h=h: e.tensor_copy(out=a_gidx[:, 0:3],
                                                  in_=ps[3][0:64, 0:3]),
                     wait=[(s_te, mml)])
            vop(lambda e: e.scalar_tensor_tensor(
                out=a_gidx[:, 1:2], in0=a_gidx[:, 1:2], scalar=128.0,
                in1=a_gidx[:, 0:1], op0=AL.mult, op1=AL.add))
            vop(lambda e: e.tensor_scalar(out=a_gidx[:, 3:4], in0=a_gidx[:, 2:3],
                                          scalar1=0.0, scalar2=None, op0=AL.is_gt))
            vop(lambda e: e.tensor_tensor(out=a_gidx[:, 4:5], in0=a_gidx[:, 1:2],
                                          in1=a_gidx[:, 3:4], op=AL.mult))
            vop(lambda e: e.tensor_tensor(out=a_gidx[:, 5:6], in0=a_gidx[:, 4:5],
                                          in1=a_gidx[:, 3:4], op=AL.mult)
                if False else e.tensor_copy(out=a_gidx[:, 5:6], in_=a_gidx[:, 4:5]))
            vop(lambda e: e.tensor_scalar(out=a_gidx[:, 6:7], in0=a_gidx[:, 3:4],
                                          scalar1=1.0, scalar2=-2048.0,
                                          op0=AL.subtract, op1=AL.mult))
            vop(lambda e: e.tensor_tensor(out=a_gidx[:, 5:6], in0=a_gidx[:, 5:6],
                                          in1=a_gidx[:, 6:7], op=AL.add))
            vz = vop(lambda e: e.memset(a_gi32[:], 0),
                     wait=[(s_po, gi_prev)] if gi_prev else ())
            v3 = vop(lambda e: e.tensor_copy(out=a_gi32[0:64, 0:1],
                                             in_=a_gidx[:, 4:5]))
            v4 = vop(lambda e, h=h: e.tensor_scalar(
                out=a_sci[0:64, h:h + 1], in0=a_gidx[:, 5:6],
                scalar1=float(h * 2049), scalar2=None, op0=AL.add))
            wx_prev = v4
            giw = [(s_ve, v4), (s_cc, ag_b)]
            if xg_st:
                giw.append((s_sy, xg_st[-1]))
            gi = gind(Dd[:, 12:14, :].rearrange("p a b -> p (a b)"),
                      d_xba[:], a_gi32[:, 0:1], wait=giw)
            gi_prev = gi
            xg_st.append(dma("sync", d_xg[h * 128:(h + 1) * 128, :],
                             Dd[:, 12:14, :].rearrange("p a b -> p (a b)"),
                             wait=[(s_po, gi)]))
        xgT = [dmaT("scalar", t_xgT[:, dmt, :],
                    d_xg[0:1024, dmt * 128:(dmt + 1) * 128],
                    wait=[(s_sy, xg_st[-1]), (s_sc, xlT_all)])
               for dmt in range(16)]
        xgT_all = xgT[-1]
        if ASUB == 4:
            return sub_exit([(s_sc, xgT_all)])

        vz_pad = vop(lambda e: e.memset(t_probs[:, 0:64], 0.0),
                     wait=[(s_ac, v_all)])
        scat_po = []
        prev_core_te = {0: 0, 1: 0}
        gev_pp = None
        evg_pp = None
        for h in range(HPC):
            hb = h % 2
            qb = t_q0 if hb == 0 else t_q1
            kb = t_k0 if hb == 0 else t_k1
            lqw = [(s_sy, qk_all), (s_sc, xgT_all)]
            if prev_core_te[hb]:
                lqw.append((s_te, prev_core_te[hb]))
            lq = dma("sync", qb, d_qk[h * 128:(h + 1) * 128, :], wait=lqw)
            lk = dma("sync", kb, d_qk[1024 + h * 128:1024 + (h + 1) * 128, :])
            st_blocks = []
            for b in range(NB):
                w0 = max(b * 128 - 64, 0)
                w1 = min(b * 128 + 192, S)
                W = w1 - w0
                smw = [(s_sy, lk)]
                if b == 0 and h == 0:
                    smw.append((s_ve, vz_pad))
                if b > 0:
                    smw += [(s_ac, a1), (s_ve, mxv)]
                t1 = top(lambda e, b=b, w0=w0, W=W, qb=qb, kb=kb: e.matmul(
                    ps[0][:, 0:W], qb[:, b * 128:(b + 1) * 128], kb[:, w0:w0 + W],
                    start=True, stop=True), wait=smw)
                mx = a_stat[:, 5:6]
                sm = a_stat[:, 6:7]
                rs = a_stat[:, 7:8]
                mxv = vop(lambda e, W=W, mx=mx: e.reduce_max(
                    out=mx, in_=ps[0][:, 0:W], axis=AX.X), wait=[(s_te, t1)])
                v2 = vop(lambda e, mx=mx: e.tensor_scalar_mul(mx, mx, -ISQ))
                a1w = [(s_ve, v2)]
                if b > 0:
                    a1w.append((s_te, trs[-1][0]))
                a1 = sop(lambda e, W=W, mx=mx, sm=sm: e.activation(
                    out=t_probs[:, 64:64 + W], in_=ps[0][:, 0:W], func=AF.Exp,
                    bias=mx, scale=ISQ, accum_out=sm), wait=a1w)
                v3 = vop(lambda e, sm=sm, rs=rs: e.reciprocal(rs, sm),
                         wait=[(s_ac, a1)])
                if b == 0:
                    segs = [(64, 128, 0, 0), (192, 64, 1, 1)]
                elif b == NB - 1:
                    segs = [(0, 128, 0, b - 1), (128, 128, 1, b)]
                else:
                    segs = [(0, 128, 0, b - 1), (128, 128, 1, b),
                            (256, 64, 2, b + 1)]
                trs = []
                for (c0, n, reg, ttile) in segs:
                    t2 = top(lambda e, c0=c0, n=n: e.transpose(
                        ps[1].ap().bitcast(BF16)[0:n, 0:128],
                        t_probs[:, c0:c0 + n], c_id16[:]),
                        wait=[(s_ac, a1)] + ([(s_ve, trs[-1][1])] if trs else []))
                    trs.append((t2, vop(
                        lambda e, n=n, reg=reg: e.tensor_copy(
                            out=t_probsT[0:n, reg * 128:reg * 128 + 128],
                            in_=ps[1].ap().bitcast(BF16)[0:n, 0:128]),
                        wait=[(s_te, t2)])))

                def ctx_mms(e, h=h, segs=segs):
                    for i, (c0, n, reg, ttile) in enumerate(segs):
                        e.matmul(ps[2][:, 0:DH],
                                 t_probsT[0:n, reg * 128:reg * 128 + 128],
                                 Cc[0:n, ttile, h * 128:(h + 1) * 128],
                                 start=(i == 0), stop=(i == len(segs) - 1),
                                 skip_group_check=True)
                t3w = [(s_ve, trs[-1][1])]
                if b > 0:
                    t3w.append((s_ve, evp))
                t3 = top(ctx_mms, wait=t3w)
                evw2 = [(s_te, t3), (s_ve, v3)]
                if b >= 4:
                    evw2.append((s_sy, st_blocks[b - 4]))
                ev = vop(lambda e, b=b, rs=rs: e.tensor_scalar(
                    out=t_ctxs[:, b % 4, :], in0=ps[2][:, 0:DH],
                    scalar1=rs, scalar2=None, op0=AL.mult), wait=evw2)
                evp = ev
                st_blocks.append(dma("sync",
                                     d_ctx_my[h * 2049 + b * 128:
                                              h * 2049 + (b + 1) * 128, :],
                                     t_ctxs[:, b % 4, :], wait=[(s_ve, ev)]))
            if ASUB == 44:
                prev_core_te[hb] = p.val("s_te")
                continue
            wq2 = dma("gpsimd",
                      Dd[:, 8:10, :].rearrange("p a b -> p (a b)").rearrange(
                          "p (a b) -> p a b", b=128),
                      d_wq[:, h * 128:(h + 1) * 128].rearrange(
                          "(a p) b -> p a b", p=128),
                      wait=[(s_te, p.val("s_te"))])

            def mmqg(e, h=h):
                for dmt in range(16):
                    e.matmul(ps[3][:, 0:64], t_whd[:, dmt, :],
                             t_xgT[:, dmt, h * 128:h * 128 + 64],
                             start=(dmt == 0), stop=(dmt == 15))
            tqg = top(mmqg, wait=[(s_po, wq2)] +
                      ([(s_ac, gev_pp)] if gev_pp else []))
            aqg = sop(lambda e, h=h: e.activation(
                out=a_qgT[:], in_=ps[3][:, 0:64], func=AF.Identity,
                bias=c_bqkv[:, h:h + 1]), wait=[(s_te, tqg)])
            if ASUB == 4411:
                prev_core_te[hb] = p.val("s_te")
                gev_pp = aqg
                continue
            gevs = []
            gmx_prev = 0
            for chk in range(4):
                t1 = top(lambda e, chk=chk, kb=kb: e.matmul(
                    ps[3][0:64, 0:512], a_qgT[:], kb[:, chk * 512:(chk + 1) * 512],
                    start=True, stop=True),
                    wait=[(s_ac, aqg)] + ([(s_ac, gevs[-1]),
                                          (s_ve, gmx_prev)] if gevs else []))
                gevs.append(sop(lambda e, chk=chk: e.activation(
                    out=t_gq[:, chk * 512:(chk + 1) * 512], in_=ps[3][0:64, 0:512],
                    func=AF.Identity), wait=[(s_te, t1)]))
                gmx_prev = vop(lambda e, chk=chk: e.reduce_max(
                    out=a_gsum[:, chk:chk + 1],
                    in_=t_gq[:, chk * 512:(chk + 1) * 512],
                    axis=AX.X), wait=[(s_ac, gevs[-1])])
            if ASUB == 4412:
                prev_core_te[hb] = p.val("s_te")
                gev_pp = gevs[-1]
                continue
            vop(lambda e: e.tensor_tensor(out=a_gsum[:, 4:5], in0=a_gsum[:, 0:1],
                                          in1=a_gsum[:, 1:2], op=AL.max),
                wait=[(s_ac, gevs[-1])])
            vop(lambda e: e.tensor_tensor(out=a_gsum[:, 4:5], in0=a_gsum[:, 4:5],
                                          in1=a_gsum[:, 2:3], op=AL.max))
            vop(lambda e: e.tensor_tensor(out=a_gsum[:, 4:5], in0=a_gsum[:, 4:5],
                                          in1=a_gsum[:, 3:4], op=AL.max))
            vmx = vop(lambda e: e.tensor_scalar_mul(a_gsum[:, 4:5],
                                                    a_gsum[:, 4:5], -ISQ))
            asum = sop(lambda e: e.activation(
                out=t_gq[:], in_=t_gq[:],
                func=AF.Exp, bias=a_gsum[0:64, 4:5], scale=ISQ,
                accum_out=a_gsum[0:64, 5:6]), wait=[(s_ve, vmx)])
            vrs = vop(lambda e: e.reciprocal(a_gsum[:, 5:6], a_gsum[:, 5:6]),
                      wait=[(s_ac, asum)])
            if ASUB == 441:
                prev_core_te[hb] = p.val("s_te")
                gev_pp = gevs[-1]
                continue
            gtr = []
            for j in range(16):
                t1 = top(lambda e, j=j: e.transpose(
                    ps[4].ap().bitcast(BF16)[:, 0:64],
                    t_gq[:, j * 128:(j + 1) * 128], c_id16[0:64, 0:64]),
                    wait=[(s_ac, asum)] + ([(s_ve, gtr[-1])] if gtr else []))
                gtr.append(vop(lambda e, j=j: e.tensor_copy(
                    out=t_gpT[:, j, :], in_=ps[4].ap().bitcast(BF16)[:, 0:64]),
                    wait=[(s_te, t1)]))

            if ASUB == 442:
                prev_core_te[hb] = p.val("s_te")
                gev_pp = gevs[-1]
                continue

            def mmgc(e, h=h):
                for j in range(16):
                    e.matmul(ps[5][0:64, 0:DH], t_gpT[:, j, :],
                             Cc[:, j, h * 128:(h + 1) * 128],
                             start=(j == 0), stop=(j == 15))
            t2 = top(mmgc, wait=[(s_ve, gtr[-1])] +
                     ([(s_ve, evg_pp)] if evg_pp else []))
            evgw = [(s_te, t2), (s_ve, vrs)]
            if scat_po:
                evgw.append((s_po, scat_po[-1]))
            evg = vop(lambda e: e.tensor_scalar(
                out=a_gctx[:], in0=ps[5][0:64, 0:DH],
                scalar1=a_gsum[0:64, 5:6], scalar2=None, op0=AL.mult),
                wait=evgw)
            if ASUB == 443:
                prev_core_te[hb] = p.val("s_te")
                gev_pp = gevs[-1]
                evg_pp = evg
                continue
            v5 = vop(lambda e, h=h: e.tensor_copy(out=a_gi32[0:64, 1:2],
                                                  in_=a_sci[0:64, h:h + 1]))
            scat_po.append(gscat(d_ctx_my[:], a_gi32[0:64, 1:2], a_gctx[:],
                                 wait=[(s_ve, evg), (s_ve, v5),
                                       (s_sy, st_blocks[-1])]))
            prev_core_te[hb] = p.val("s_te")
            gev_pp = gevs[-1]
            evg_pp = evg
        if ASUB == 44:
            return sub_exit([(s_sy, st_blocks[-1])])
        if ASUB in (4411, 4412, 441, 442, 443):
            return sub_exit([(s_te, p.val("s_te")), (s_ve, p.val("s_ve"))])
        if ASUB == 45:
            return sub_exit([(s_po, scat_po[-1])])
        ag_c = coll(d_ctx_my[:], d_ctx_pag[:], GP,
                    wait=[(s_po, scat_po[-1]), (s_sy, pad_st)] + R["ctx"])
        if ASUB == 5:
            return sub_exit([(s_cc, ag_c)])

        out_st = []
        asm_lds = []
        v2p = None
        for g in range(NTL):
            lds = []
            for hg in range(16):
                seg = (hg // 8) * (HPC * 2049) + (hg % 8) * 2049
                for hf in range(2):
                    ldw = [(s_cc, ag_c)]
                    if v2p is not None:
                        ldw.append((s_ve, v2p))
                    lds.append(dma("scalar",
                                   Dd[:, hg, hf * 512:hf * 512 + 128],
                                   d_ctx_pag[seg + hf * 1024 + g * 128:
                                             seg + hf * 1024 + (g + 1) * 128, :],
                                   wait=ldw))
            asm_lds = lds
            xw = [(s_sy, out_st[g - 2])] if g >= 2 else []
            ldx = dma("sync", a_x32[:, g % 2, :], x_in[g * 128:(g + 1) * 128, :],
                      wait=xw)
            scr = Cc[:, 0:2, :].rearrange("p a b -> p (a b)").rearrange(
                "p (a b) -> p a b", b=128)
            v1 = vop(lambda e: e.tensor_tensor(
                out=scr[:], in0=Dd[:, 0:16, 512:640],
                in1=Dd[:, 0:16, 0:128], op=AL.subtract),
                wait=[(s_sc, lds[-1])])
            v2 = vop(lambda e: e.scalar_tensor_tensor(
                out=scr[:], in0=scr[:],
                scalar=c_cc[:, 1:2], in1=Dd[:, 0:16, 0:128],
                op0=AL.mult, op1=AL.add))
            v3 = vop(lambda e, g=g: e.tensor_tensor(
                out=a_x32[:, g % 2, :], in0=a_x32[:, g % 2, :],
                in1=scr.rearrange("p a b -> p (a b)"), op=AL.add),
                wait=[(s_sy, ldx)])
            v2p = v2
            out_st.append(dma("sync", x_out[g * 128:(g + 1) * 128, :],
                              a_x32[:, g % 2, :], wait=[(s_ve, v3)]))
        R["xln_my"] = [(s_sc, xlT_all)]
        R["xba"] = [(s_sc, xgT_all), (s_po, p.val("s_po"))]
        R["ctx"] = [(s_sc, asm_lds[-1])]
        return out_st[-1]

    if STAGE == 1:
        emit_ffn(d_x0, d_out, 1)
    elif STAGE == 2:
        emit_ffn(d_x0, d_x1, 1)
        fence()
        emit_attn(d_x1, d_out)
    else:
        emit_ffn(d_x0, d_x1, 1)
        fence()
        emit_attn(d_x1, d_x2)
        fence()
        emit_ffn(d_x2, d_out, 2)

    p.op("gpsimd", None, wait=[(s_sy, p.val("s_sy")), (s_sc, p.val("s_sc")),
                               (s_po, p.val("s_po"))], inc=None, drain=False)
    p.streams["gpsimd"].append(
        (None, ((s_sy, p.val("s_sy")),), None, 0, False))

    p.build_block()
    p.ctx.close()
    return nc


def kernel(**inputs):
    x = np.ascontiguousarray(np.asarray(inputs["L0_attn_hidden_states"],
                                        dtype=np.float32).reshape(T, D))
    wq = np.asarray(inputs["L0_attn_Wq"], dtype=np.float32)
    wk = np.asarray(inputs["L0_attn_Wk"], dtype=np.float32)
    wv = np.asarray(inputs["L0_attn_Wv"], dtype=np.float32)
    bq = np.asarray(inputs["L0_attn_bq"], dtype=np.float32).reshape(D, 1)
    bk = np.asarray(inputs["L0_attn_bk"], dtype=np.float32).reshape(D, 1)
    bv = np.asarray(inputs["L0_attn_bv"], dtype=np.float32).reshape(1, D)
    rw = np.asarray(inputs["L0_ffn_router_w"], dtype=np.float32)
    ebias = np.asarray(inputs["L0_ffn_e_bias"], dtype=np.float32)
    wg = np.asarray(inputs["L0_ffn_W_gate"], dtype=np.float32)
    wu = np.asarray(inputs["L0_ffn_W_up"], dtype=np.float32)
    wd = np.asarray(inputs["L0_ffn_W_down"], dtype=np.float32)
    sg = np.asarray(inputs["L0_ffn_shared_gate"], dtype=np.float32)
    su = np.asarray(inputs["L0_ffn_shared_up"], dtype=np.float32)
    sd = np.asarray(inputs["L0_ffn_shared_down"], dtype=np.float32)

    sgT = np.ascontiguousarray(sg.T)
    suT = np.ascontiguousarray(su.T)
    sdT = np.ascontiguousarray(sd.T)
    rwT = np.ascontiguousarray(rw.T)

    nc = build_program()
    in_maps = []
    for c in range(8):
        hs = slice((c % 2) * HPC * DH, ((c % 2) + 1) * HPC * DH)
        sel = np.zeros((EPC, E), np.float32)
        for el in range(EPC):
            sel[el, EPC * c + el] = 1.0
        sel = np.ascontiguousarray(
            np.broadcast_to(sel[None], (128, EPC, E)).reshape(128, EPC * E))
        cconst = np.zeros((128, 4), np.float32)
        cconst[:, 0] = EPC * c
        cconst[:, 1] = c % 2
        rmask = np.zeros((8, 1), np.float32)
        rmask[:c, 0] = 1.0
        in_maps.append({
            "x0": x[c * TL:(c + 1) * TL],
            "wq": np.ascontiguousarray(wq[:, hs]),
            "wk": np.ascontiguousarray(wk[:, hs]),
            "wv": np.ascontiguousarray(wv[:, hs]),
            "bq": np.ascontiguousarray(bq[hs]),
            "bk": np.ascontiguousarray(bk[hs]),
            "bv": np.ascontiguousarray(bv[:, hs]),
            "rwT": rwT,
            "ebias": np.ascontiguousarray(
                np.broadcast_to(ebias[None, :], (128, E))),
            "wg": np.ascontiguousarray(wg[EPC * c:EPC * (c + 1)]),
            "wu": np.ascontiguousarray(wu[EPC * c:EPC * (c + 1)]),
            "wd": np.ascontiguousarray(wd[EPC * c:EPC * (c + 1)]),
            "sgT": sgT, "suT": suT, "sdT": sdT,
            "sel": sel, "cconst": cconst, "rmask": rmask,
        })
    res = run_bass_kernel_spmd(nc, in_maps, core_ids=list(range(8)))
    out = np.concatenate([res.results[c]["xout"] for c in range(8)], axis=0)
    return out.reshape(B, S, D)


# version 05
# speedup vs baseline: 1.0387x; 1.0387x over previous
import numpy as np
from contextlib import ExitStack

import concourse.bass as bass
import concourse.mybir as mybir
from concourse.bass_utils import run_bass_kernel_spmd
from concourse.bass_core_rust import IndirectOffsetOnAxis

F32 = mybir.dt.float32
BF16 = mybir.dt.bfloat16
F16 = mybir.dt.float16
I32 = mybir.dt.int32
AL = mybir.AluOpType
AF = mybir.ActivationFunctionType
AX = mybir.AxisListType

B, S, D = 4, 2048, 2048
T = B * S
TL = 1024
NTL = 8
NGT = 64
E, EPC, K = 32, 4, 4
I_FF, ISH = 1024, 2048
CAP, CPAD = 320, 384
SEG = EPC * CPAD
DUMMY_ROW = 8 * SEG
RES_ROWS = 8 * SEG + 128
NG, TKG = 8, 3
HPC, DH = 8, 128
NB = 16
EPS = 1e-5
ISQ = 0.08838834764831845
NEGBIG = -1.0e9

STAGE = 3
ASUB = 99


class Prog:
    ENGINES = ("sync", "scalar", "vector", "tensor", "gpsimd")

    def __init__(self, nc):
        self.nc = nc
        self.streams = {e: [] for e in self.ENGINES}
        self.ctx = ExitStack()
        self.sems = {}
        self.bnd = {}

    def sem(self, name):
        h = self.ctx.enter_context(self.nc.semaphore(name))
        self.sems[name] = [h, 0]
        return name

    def val(self, name):
        return self.sems[name][1]

    DMA_SEMS = {"s_sy": "sync", "s_sc": "scalar", "s_po": "gpsimd"}

    def op(self, engine, fn, wait=(), inc=None, inc_n=1, drain=False):
        nw = []
        for (n, v) in wait:
            if n in self.DMA_SEMS and v > 0:
                tot = self.sems[n][1]
                if tot not in self.bnd.setdefault(n, {0}):
                    qeng = self.DMA_SEMS[n]
                    h = self.sems[n][0]
                    self.streams[qeng].append(
                        (None, ((n, tot),), None, 0, False))
                    self.bnd[n].add(tot)
                nw.append((n, tot))
            else:
                nw.append((n, v))
        wait = nw
        if engine in ("sync", "scalar", "gpsimd") and wait:
            for sname, qeng in self.DMA_SEMS.items():
                if qeng == engine:
                    self.bnd.setdefault(sname, {0}).add(self.sems[sname][1])
        after = None
        if inc is not None:
            self.sems[inc][1] += inc_n
            after = self.sems[inc][1]
        self.streams[engine].append((fn, tuple(wait), inc, inc_n, drain))
        return after

    def build_block(self):
        blk = self.ctx.enter_context(self.nc.Block())
        for eng_name in self.ENGINES:
            stream = self.streams[eng_name]

            def make(stream):
                def emit(eng):
                    for fn, waits, inc, inc_n, drain in stream:
                        for sname, v in waits:
                            if v > 0:
                                eng.wait_ge(self.sems[sname][0], v)
                        inst = fn(eng) if fn is not None else None
                        if drain or (inc is not None and inst is None):
                            inst = eng.drain()
                        if inc is not None:
                            inst.then_inc(self.sems[inc][0], inc_n)
                return emit

            getattr(blk, eng_name)(make(stream))


def build_program():
    nc = bass.Bass("TRN2", target_bir_lowering=False)
    p = Prog(nc)
    EN = p.ctx.enter_context

    d_x0 = nc.dram_tensor("x0", [TL, D], F32, kind="ExternalInput").ap()
    d_wq = nc.dram_tensor("wq", [D, HPC * DH], F32, kind="ExternalInput").ap()
    d_wk = nc.dram_tensor("wk", [D, HPC * DH], F32, kind="ExternalInput").ap()
    d_wv = nc.dram_tensor("wv", [D, HPC * DH], F32, kind="ExternalInput").ap()
    d_bq = nc.dram_tensor("bq", [HPC * DH, 1], F32, kind="ExternalInput").ap()
    d_bk = nc.dram_tensor("bk", [HPC * DH, 1], F32, kind="ExternalInput").ap()
    d_bv = nc.dram_tensor("bv", [1, HPC * DH], F32, kind="ExternalInput").ap()
    d_rwT = nc.dram_tensor("rwT", [D, E], F32, kind="ExternalInput").ap()
    d_ebias = nc.dram_tensor("ebias", [128, E], F32, kind="ExternalInput").ap()
    d_wg = nc.dram_tensor("wg", [EPC, D, I_FF], F32, kind="ExternalInput").ap()
    d_wu = nc.dram_tensor("wu", [EPC, D, I_FF], F32, kind="ExternalInput").ap()
    d_wd = nc.dram_tensor("wd", [EPC, I_FF, D], F32, kind="ExternalInput").ap()
    d_sgT = nc.dram_tensor("sgT", [D, ISH], F32, kind="ExternalInput").ap()
    d_suT = nc.dram_tensor("suT", [D, ISH], F32, kind="ExternalInput").ap()
    d_sdT = nc.dram_tensor("sdT", [ISH, D], F32, kind="ExternalInput").ap()
    d_sel = nc.dram_tensor("sel", [128, EPC * E], F32, kind="ExternalInput").ap()
    d_cconst = nc.dram_tensor("cconst", [128, 4], F32, kind="ExternalInput").ap()
    d_rmask = nc.dram_tensor("rmask", [8, 1], F32, kind="ExternalInput").ap()
    d_out = nc.dram_tensor("xout", [TL, D], F32, kind="ExternalOutput").ap()

    d_xln_my = nc.dram_tensor("xln_my", [TL, D], BF16, kind="Internal").ap()
    d_xln_ag = nc.dram_tensor("xln_ag", [T, D], BF16, kind="Internal").ap()
    d_xba = nc.dram_tensor("xba", [S, D], BF16, kind="Internal").ap()
    d_scT_my = nc.dram_tensor("scT_my", [E, TL], F32, kind="Internal").ap()
    d_scT_ag = nc.dram_tensor("scT_ag", [8 * E, TL], F32, kind="Internal").ap()
    d_cnt_my = nc.dram_tensor("cnt_my", [1, E], F32, kind="Internal").ap()
    d_cnt_ag = nc.dram_tensor("cnt_ag", [8, E], F32, kind="Internal").ap()
    d_res_my = nc.dram_tensor("res_my", [SEG, D], BF16, kind="Internal").ap()
    d_res_ag = nc.dram_tensor("res_ag", [RES_ROWS, D], BF16, kind="Internal").ap()
    d_xp = nc.dram_tensor("xp_rows", [EPC, CPAD, D], BF16, kind="Internal").ap()
    d_shared = nc.dram_tensor("shared_out", [TL, D], BF16, kind="Internal").ap()
    d_x1 = nc.dram_tensor("x1", [TL, D], F32, kind="Internal").ap()
    d_x2 = nc.dram_tensor("x2", [TL, D], F32, kind="Internal").ap()
    d_qk = nc.dram_tensor("qk_spill", [2 * HPC * DH, S], BF16, kind="Internal").ap()
    d_ctx_my = nc.dram_tensor("ctx_my", [HPC * 2049, DH], BF16, kind="Internal").ap()
    d_ctx_pag = nc.dram_tensor("ctx_pag", [2 * HPC * 2049, DH], BF16,
                               kind="Internal").ap()
    d_xg = nc.dram_tensor("xg_rows", [HPC * 128, D], BF16, kind="Internal").ap()

    c_i384 = EN(nc.sbuf_tensor("c_i384", [128, CPAD], F16))
    c_i32 = EN(nc.sbuf_tensor("c_i32", [128, E], F32))
    c_i32m = EN(nc.sbuf_tensor("c_i32m", [128, E], F32))
    c_i8 = EN(nc.sbuf_tensor("c_i8", [128, NG], F32))
    c_i8m = EN(nc.sbuf_tensor("c_i8m", [128, NG], F32))
    c_i64 = EN(nc.sbuf_tensor("c_i64", [128, 64], F32))
    c_ut = EN(nc.sbuf_tensor("c_ut", [128, 128], F32))
    c_id16 = EN(nc.sbuf_tensor("c_id16", [128, 128], BF16))
    c_id32 = EN(nc.sbuf_tensor("c_id32", [128, 128], F32))
    c_ones_col = EN(nc.sbuf_tensor("c_ones_col", [128, 1], F32))
    c_ones_row = EN(nc.sbuf_tensor("c_ones_row", [1, 128], F32))
    c_ones16 = EN(nc.sbuf_tensor("c_ones16", [1, 128], BF16))
    c_ebias = EN(nc.sbuf_tensor("c_ebias", [128, E], F32))
    c_gatok = EN(nc.sbuf_tensor("c_gatok", [128, 16, 3], F32))
    c_bqkv = EN(nc.sbuf_tensor("c_bqkv", [128, 2 * HPC], F32))
    c_bv16 = EN(nc.sbuf_tensor("c_bv16", [1, HPC * DH], BF16))
    c_sel = EN(nc.sbuf_tensor("c_sel", [128, EPC, E], F32))
    c_cc = EN(nc.sbuf_tensor("c_cc", [128, 4], F32))
    c_rmask = EN(nc.sbuf_tensor("c_rmask", [8, 1], F32))
    c_cnt = EN(nc.sbuf_tensor("c_cnt", [8, E], F32))

    A = EN(nc.sbuf_tensor("A", [128, 16, 2048], BF16))
    Bf = EN(nc.sbuf_tensor("Bf", [128, 12, 1024], F32))
    Cc = EN(nc.sbuf_tensor("Cc", [128, 16, 1024], BF16))
    Dd = EN(nc.sbuf_tensor("Dd", [128, 16, 1024], BF16))
    a_oh = EN(nc.sbuf_tensor("a_oh", [128, 2, CPAD], F16))
    a_lhsT = EN(nc.sbuf_tensor("a_lhsT", [128, NGT, 3], F16))
    a_stat = EN(nc.sbuf_tensor("a_stat", [128, 16], F32))
    a_slotT = EN(nc.sbuf_tensor("a_slotT", [128, 12, 3], F32))
    a_wcol = EN(nc.sbuf_tensor("a_wcol", [128, 12], F32))
    a_idx = EN(nc.sbuf_tensor("a_idx", [128, 16], I32))
    a_carry = EN(nc.sbuf_tensor("a_carry", [1, E], F32))
    a_carry2 = EN(nc.sbuf_tensor("a_carry2", [1, E], F32))
    a_crow = EN(nc.sbuf_tensor("a_crow", [E, 1], F32))
    a_ek = EN(nc.sbuf_tensor("a_ek", [128, K, NGT], F32))
    a_wk = EN(nc.sbuf_tensor("a_wk", [128, K, NGT], F32))
    a_ws = EN(nc.sbuf_tensor("a_ws", [128, NGT], F32))
    a_sgmy = EN(nc.sbuf_tensor("a_sgmy", [128, NTL, K], I32))
    a_gidx = EN(nc.sbuf_tensor("a_gidx", [64, 8], F32))
    a_gi32 = EN(nc.sbuf_tensor("a_gi32", [128, 2], I32))
    a_sci = EN(nc.sbuf_tensor("a_sci", [128, HPC], I32))
    a_x32 = EN(nc.sbuf_tensor("a_x32", [128, 2, 2048], F32))
    a_qgT = EN(nc.sbuf_tensor("a_qgT", [128, 64], BF16))
    a_gsum = EN(nc.sbuf_tensor("a_gsum", [64, 8], F32))
    a_gctx = EN(nc.sbuf_tensor("a_gctx", [64, DH], BF16))
    a_y = EN(nc.sbuf_tensor("a_y", [128, 2, 512], BF16))

    ps = [EN(nc.psum_tensor(f"ps{i}", [128, 512], F32)) for i in range(8)]

    Bff = Bf.ap().rearrange("p a b -> p (a b)")

    def bfv(lo, hi):
        return Bff[:, lo * 1024:hi * 1024]

    Af = A.ap().rearrange("p a b -> p (a b)")
    Df = Dd.ap().rearrange("p a b -> p (a b)")
    Cf = Cc.ap().rearrange("p a b -> p (a b)")

    v_xlnT = Af[:, 0:16 * 1024].rearrange("p (a b) -> p a b", b=1024)
    v_shG = Af[:, 16 * 1024:24 * 1024].rearrange("p (a b) -> p a b", b=512)
    v_shU = Af[:, 24 * 1024:32 * 1024].rearrange("p (a b) -> p a b", b=512)
    v_shY = Af[:, 24 * 1024:32 * 1024].rearrange("p (a b) -> p a b", b=2048)
    f_scores = bfv(0, 2).rearrange("p (g e) -> p g e", e=E)
    a_sslot = bfv(0, 2)[0:3, 0:EPC * CPAD]
    a_lM = bfv(0, 1)[:, 0:256].rearrange("p (g e) -> p g e", e=E)
    a_lP = bfv(0, 1)[:, 256:512].rearrange("p (g e) -> p g e", e=E)
    a_lek = bfv(0, 1)[:, 512:544].rearrange("p (k g) -> p k g", g=NTL)
    a_lrk = bfv(0, 1)[:, 544:576].rearrange("p (k g) -> p k g", g=NTL)
    a_lws = bfv(0, 1)[:, 576:584]
    f_sc = bfv(2, 4).rearrange("p (g e) -> p g e", e=E)
    f_M = bfv(4, 6).rearrange("p (g e) -> p g e", e=E)
    a_scTb = bfv(4, 6)[0:E, 0:TL]
    f_Ww = bfv(6, 8).rearrange("p (g e) -> p g e", e=E)
    f_P = bfv(8, 10).rearrange("p (g e) -> p g e", e=E)
    f_oh = bfv(10, 12).rearrange("p (g e) -> p g e", e=E)
    f_scT3 = Bf[0:E, 4:12, :]
    f_sq = bfv(2, 4)
    f_xstage = bfv(6, 8)
    f_scmy = bfv(10, 11)[:, 0:256].rearrange("p (a b) -> p a b", b=E)
    f_g2 = bfv(10, 11)[:, 512:1024].rearrange("p (g n) -> p g n", n=NG)
    f_gm = bfv(11, 12)[:, 0:512].rearrange("p (g n) -> p g n", n=NG)
    f_t2 = bfv(11, 12)[:, 512:1024].rearrange("p (g n) -> p g n", n=NG)
    l_oh = bfv(10, 11)[:, 256:512].rearrange("p (g e) -> p g e", e=E)
    l_g2 = bfv(11, 12)[:, 0:64].rearrange("p (g n) -> p g n", n=NG)
    l_t2 = bfv(11, 12)[:, 64:128].rearrange("p (g n) -> p g n", n=NG)
    l_gm = bfv(11, 12)[:, 128:192].rearrange("p (g n) -> p g n", n=NG)
    l_sc = bfv(11, 12)[:, 768:1024].rearrange("p (g n) -> p g n", n=E) \
        if False else bfv(2, 3)[:, 0:256].rearrange("p (g e) -> p g e", e=E)

    v_xpT = Df[:, 0:6 * 1024].rearrange("p (a b) -> p a b", b=CPAD)
    v_eg = Df[:, 6 * 1024:9 * 1024].rearrange("p (a b) -> p a b", b=CPAD)[:, 0:8, :]
    v_eu = Df[:, 9 * 1024:12 * 1024].rearrange("p (a b) -> p a b", b=CPAD)[:, 0:8, :]
    v_gb = Df[:, 12 * 1024:16 * 1024].rearrange("p (a b) -> p a b", b=2048)
    v_comb = Df[:, 0:8 * 1024].rearrange("p (a b) -> p a b", b=2048)

    t_xgT = Af[:, 0:16 * 1024].rearrange("p (a b) -> p a b", b=1024)
    t_q0 = Df[:, 0:2048]
    t_k0 = Df[:, 2048:4096]
    t_q1 = Df[:, 4096:6144]
    t_k1 = Df[:, 6144:8192]
    t_whd = Df[:, 8192:10240].rearrange("p (a b) -> p a b", b=128)
    t_pr = (Df[:, 14336:14720], Df[:, 15104:15488])
    t_prT = (Df[:, 14720:15104], Df[:, 15488:15872])
    t_gpT = Df[:, 10752:11776].rearrange("p (a b) -> p a b", b=64)
    t_ctxs = Df[:, 11776:12288].rearrange("p (a b) -> p a b", b=128)
    t_gq = Df[0:64, 12288:14336]
    t_asm = Df[:, 0:16 * 1024].rearrange("p (a b) -> p a b", b=1024)
    n_norm = bfv(0, 2)[0:8, :]
    n_work = bfv(2, 4)[0:8, :]
    n_r = bfv(4, 6)[0:8, :]
    n_m = bfv(6, 8)[0:8, :]
    n_p0 = bfv(8, 10)[0:1, 0:2048]
    a_rcm = bfv(8, 10)[:, 512:768].rearrange("p (g t n) -> p g t n", t=2, n=8)
    a_wtf = bfv(8, 10)[:, 768:832]
    a_pmy = bfv(10, 11)[:, 0:256].rearrange("p (g l) -> p g l", l=EPC)
    a_wmy = bfv(10, 11)[:, 256:512].rearrange("p (g l) -> p g l", l=EPC)
    c_rwT = bfv(8, 10)[:, 1024:1536].rearrange("p (a b) -> p a b", b=E)
    n_sq = bfv(10, 12)

    s_sy = p.sem("s_sy")
    s_sc = p.sem("s_sc")
    s_po = p.sem("s_po")
    s_te = p.sem("s_te")
    s_ve = p.sem("s_ve")
    s_ac = p.sem("s_ac")
    s_ge = p.sem("s_ge")
    s_cc = p.sem("s_cc")

    def dma(queue, out, in_, wait=(), slow=False):
        inc = {"sync": s_sy, "scalar": s_sc, "gpsimd": s_po}[queue]
        return p.op(queue, lambda e, o=out, i=in_: e.dma_start(
            out=o, in_=i, allow_slow_non_contiguous=slow),
            wait=wait, inc=inc, inc_n=16)

    def dmaT(queue, out, in_, wait=()):
        inc = {"sync": s_sy, "scalar": s_sc}[queue]
        return p.op(queue, lambda e, o=out, i=in_: e.dma_start_transpose(o, i),
                    wait=wait, inc=inc, inc_n=16)

    def vop(fn, wait=(), drain=True):
        return p.op("vector", fn, wait=wait, inc=s_ve, drain=drain)

    def sop(fn, wait=(), drain=True):
        return p.op("scalar", fn, wait=wait, inc=s_ac, drain=drain)

    def top(fn, wait=()):
        return p.op("tensor", fn, wait=wait, inc=s_te)

    def gind(out, in_, idx_ap, wait=()):
        return p.op("gpsimd", lambda e, o=out, i=in_, x=idx_ap: e.indirect_dma_start(
            out=o, out_offset=None, in_=i, in_offset=IndirectOffsetOnAxis(x, 0)),
            wait=wait, inc=s_po, inc_n=16)

    def gscat(out, idx_ap, in_, wait=()):
        return p.op("gpsimd", lambda e, o=out, i=in_, x=idx_ap: e.indirect_dma_start(
            out=o, out_offset=IndirectOffsetOnAxis(x, 0), in_=i, in_offset=None),
            wait=wait, inc=s_po, inc_n=16)

    def coll(ins, outs, groups, wait=()):
        return p.op("gpsimd", lambda e, i=ins, o=outs, g=groups: e.collective_compute(
            "AllGather", AL.bypass, ins=[i], outs=[o], replica_groups=g),
            wait=wait, inc=s_cc)

    G8 = [list(range(8))]
    GP = [[0, 1], [2, 3], [4, 5], [6, 7]]

    def fence():
        vals = [(n, p.sems[n][1]) for n in
                ("s_sy", "s_sc", "s_po", "s_te", "s_ve", "s_ac", "s_ge", "s_cc")]
        for eng in Prog.ENGINES:
            p.op(eng, None, wait=vals, inc=None, drain=False)

    def bc1(ap2d, nrep, m):
        return ap2d.unsqueeze(1).to_broadcast([128, nrep, m])

    def bc2(ap2d, m):
        return ap2d.unsqueeze(2).to_broadcast([128, ap2d.shape[1], m])

    def iota(dst, pattern, base, chm):
        return lambda e, d=dst, pt=pattern, b=base, c=chm: e.iota(
            d, pattern=pt, base=b, channel_multiplier=c,
            allow_small_or_imprecise_dtypes=True)

    for args in ((c_i384[:], [[1, CPAD]], 0, 0), (c_i32[:], [[1, E]], 0, 0),
                 (c_i32m[:], [[1, E]], -32, 0), (c_i8[:], [[1, NG]], 0, 0),
                 (c_i8m[:], [[1, NG]], -8, 0), (c_i64[:], [[1, 64]], 0, 0),
                 (c_ut[:], [[1, 128]], 0, -1), (c_id32[:], [[-1, 128]], 0, 1),
                 (a_lhsT[:, :, 0:1], [[0, NGT]], 0, 1),
                 (a_lhsT[:, :, 1:2], [[1, NGT]], 0, 0),
                 (c_gatok[:, :, 0:1], [[0, 16]], 0, 1),
                 (c_gatok[:, :, 1:2], [[1, 16]], 0, 0)):
        p.op("gpsimd", iota(*args), inc=None)
    g_iotas = p.op("gpsimd", lambda e: None, inc=s_ge, drain=True)

    vop(lambda e: e.memset(c_ones_col[:], 1.0))
    vop(lambda e: e.memset(a_stat[:, 15:16], float(EPS)))
    vop(lambda e: e.memset(c_ones_row[:], 1.0))
    vop(lambda e: e.memset(c_ones16[:], 1.0))
    vop(lambda e: e.memset(c_gatok[:, :, 2:3], 1.0), wait=[(s_ge, g_iotas)])
    vop(lambda e: e.tensor_scalar(out=c_ut[:], in0=c_ut[:], scalar1=0.0,
                                  scalar2=None, op0=AL.is_gt))
    vop(lambda e: e.tensor_scalar(out=c_id32[:], in0=c_id32[:], scalar1=0.0,
                                  scalar2=None, op0=AL.is_equal))
    v_const = vop(lambda e: e.tensor_copy(out=c_id16[:], in_=c_id32[:]))

    dma("sync", c_ebias[:], d_ebias[:])
    dma("sync", c_sel.ap().rearrange("p a b -> p (a b)"), d_sel[:])
    dma("sync", c_cc[:], d_cconst[:])
    dma("sync", c_rmask[:], d_rmask[:])
    dma("sync", c_bqkv[:, 0:HPC].rearrange("p (a o) -> p a o", o=1),
        d_bq.rearrange("(a p) o -> p a o", p=128), slow=True)
    dma("sync", c_bqkv[:, HPC:2 * HPC].rearrange("p (a o) -> p a o", o=1),
        d_bk.rearrange("(a p) o -> p a o", p=128), slow=True)
    ld_bv = dma("gpsimd", c_bv16[:], d_bv[:])
    v_z = vop(lambda e: e.memset(v_gb[:, 0, :], 0.0))
    dma("sync", d_res_ag[DUMMY_ROW:DUMMY_ROW + 128, :], v_gb[:, 0, :],
        wait=[(s_ve, v_z)])
    pad_st = dma("sync", d_ctx_my.rearrange("(h r) d -> h r d", r=2049)[:, 2048, :],
                 v_gb[0:8, 0, 0:DH])

    R = {"xln_my": [], "xln_ag": [], "scT": [], "res": [], "cnt": [],
         "xba": [], "ctx": []}

    def emit_ln_tile(x_src_rows, xb, xlnb, sq_buf, extra_f32_out=None,
                     ld_wait=(), out_wait=()):
        mu, ssq, var, rstd, negb = (a_stat[:, i:i + 1] for i in range(5))
        ld = dma("sync", xb, x_src_rows, wait=list(ld_wait))
        a1 = sop(lambda e: e.activation(out=sq_buf, in_=xb, func=AF.Square,
                                        accum_out=ssq), wait=[(s_sy, ld)])
        vop(lambda e: e.reduce_sum(out=mu, in_=xb, axis=AX.X), wait=[(s_sy, ld)])
        vop(lambda e: e.tensor_scalar_mul(mu, mu, 1.0 / D))
        vop(lambda e: e.tensor_scalar_mul(var, ssq, 1.0 / D), wait=[(s_ac, a1)])
        vop(lambda e: e.tensor_mul(ssq, mu, mu))
        v1 = vop(lambda e: e.tensor_sub(var, var, ssq))
        a2 = sop(lambda e: e.activation(out=rstd, in_=var, func=AF.Sqrt,
                                        bias=a_stat[:, 15:16]),
                 wait=[(s_ve, v1)])
        v2 = vop(lambda e: e.reciprocal(rstd, rstd), wait=[(s_ac, a2)])
        v3 = vop(lambda e: e.scalar_tensor_tensor(
            out=negb, in0=mu, scalar=-1.0, in1=rstd, op0=AL.mult, op1=AL.mult),
            wait=[(s_ve, v2)])
        a3 = sop(lambda e: e.activation(out=xlnb, in_=xb, func=AF.Identity,
                                        bias=negb, scale=rstd),
                 wait=[(s_ve, v3)] + list(out_wait))
        a4 = None
        if extra_f32_out is not None:
            a4 = sop(lambda e: e.activation(out=extra_f32_out, in_=xb,
                                            func=AF.Identity, bias=negb,
                                            scale=rstd), wait=[(s_ve, v3)])
        return a3, a4

    def emit_router(nt, scores_v, sc_v, oh_v, g2_v, gm_v, t2_v, M_v, ek_t,
                    wk_t, ws_t, want_w, wait0):
        vop(lambda e: e.tensor_tensor(out=sc_v[:], in0=scores_v[:],
                                      in1=bc1(c_ebias[:], nt, E), op=AL.add),
            wait=wait0)
        sc4 = sc_v.rearrange("p g (n f) -> p g n f", f=4)
        first = True
        for (i, j) in ((0, 1), (0, 2), (0, 3), (1, 2), (1, 3), (2, 3)):
            vop(lambda e, i=i, j=j: e.tensor_tensor(
                out=t2_v[:], in0=sc4[:, :, :, i], in1=sc4[:, :, :, j], op=AL.add))
            if first:
                vop(lambda e: e.tensor_copy(out=g2_v[:], in_=t2_v[:]))
                first = False
            else:
                vop(lambda e: e.tensor_tensor(out=g2_v[:], in0=g2_v[:], in1=t2_v[:],
                                              op=AL.max))
        vop(lambda e: e.memset(gm_v[:], 0.0))
        ws_g = ws_t[:, 0:nt]
        for r in range(TKG):
            vop(lambda e: e.reduce_max(out=ws_g, in_=g2_v[:], axis=AX.X))
            vop(lambda e: e.tensor_tensor(out=t2_v[:], in0=g2_v[:],
                                          in1=bc2(ws_g, NG), op=AL.is_equal))
            vop(lambda e: e.tensor_tensor(out=t2_v[:], in0=t2_v[:],
                                          in1=bc1(c_i8m[:], nt, NG), op=AL.mult))
            vop(lambda e: e.tensor_reduce(out=ws_g, in_=t2_v[:], axis=AX.X,
                                          op=AL.min))
            vop(lambda e: e.tensor_scalar_add(ws_g, ws_g, 8.0))
            vop(lambda e: e.tensor_tensor(out=t2_v[:], in0=bc1(c_i8[:], nt, NG),
                                          in1=bc2(ws_g, NG), op=AL.is_equal))
            vop(lambda e: e.scalar_tensor_tensor(out=g2_v[:], in0=t2_v[:],
                                                 scalar=NEGBIG, in1=g2_v[:],
                                                 op0=AL.mult, op1=AL.add))
            vop(lambda e: e.tensor_tensor(out=gm_v[:], in0=gm_v[:], in1=t2_v[:],
                                          op=AL.add))
        vop(lambda e: e.tensor_tensor(
            out=sc4[:], in0=sc4[:],
            in1=gm_v.unsqueeze(3).to_broadcast([128, nt, NG, 4]), op=AL.mult))
        vop(lambda e: e.memset(M_v[:], 0.0))
        for k in range(K):
            vop(lambda e: e.reduce_max(out=ws_g, in_=sc_v[:], axis=AX.X))
            vop(lambda e: e.tensor_tensor(out=oh_v[:], in0=sc_v[:],
                                          in1=bc2(ws_g, E), op=AL.is_equal))
            vop(lambda e: e.tensor_tensor(out=oh_v[:], in0=oh_v[:],
                                          in1=bc1(c_i32m[:], nt, E), op=AL.mult))
            vop(lambda e, k=k: e.tensor_reduce(out=ek_t[:, k, 0:nt], in_=oh_v[:],
                                               axis=AX.X, op=AL.min))
            vop(lambda e, k=k: e.tensor_scalar_add(ek_t[:, k, 0:nt],
                                                   ek_t[:, k, 0:nt], 32.0))
            vop(lambda e, k=k: e.tensor_tensor(
                out=oh_v[:], in0=bc1(c_i32[:], nt, E),
                in1=bc2(ek_t[:, k, 0:nt], E), op=AL.is_equal))
            vop(lambda e: e.scalar_tensor_tensor(out=sc_v[:], in0=oh_v[:],
                                                 scalar=NEGBIG, in1=sc_v[:],
                                                 op0=AL.mult, op1=AL.add))
            vop(lambda e: e.tensor_tensor(out=M_v[:], in0=M_v[:], in1=oh_v[:],
                                          op=AL.add))
            if want_w:
                vop(lambda e: e.tensor_tensor(out=oh_v[:], in0=oh_v[:],
                                              in1=scores_v[:], op=AL.mult))
                vop(lambda e, k=k: e.reduce_sum(out=wk_t[:, k, 0:nt], in_=oh_v[:],
                                                axis=AX.X))
        if want_w:
            vop(lambda e: e.tensor_add(ws_g, wk_t[:, 0, 0:nt], wk_t[:, 1, 0:nt]))
            vop(lambda e: e.tensor_add(ws_g, ws_g, wk_t[:, 2, 0:nt]))
            vop(lambda e: e.tensor_add(ws_g, ws_g, wk_t[:, 3, 0:nt]))
            vop(lambda e: e.tensor_scalar_add(ws_g, ws_g, 1.0e-20))
            vop(lambda e: e.reciprocal(ws_g, ws_g))
            vop(lambda e: e.tensor_scalar_mul(ws_g, ws_g, 2.5))
            for k in range(K):
                vop(lambda e, k=k: e.tensor_tensor(out=wk_t[:, k, 0:nt],
                                                   in0=wk_t[:, k, 0:nt],
                                                   in1=ws_g, op=AL.mult))
        return vop(lambda e: None)

    def emit_ffn(x_in, x_out, call):
        ld_rw = dma("sync", c_rwT[:], d_rwT.rearrange("(a p) b -> p a b", p=128))
        sc_ev = []
        ln_st = []
        a4p = {}
        for g in range(NTL):
            xb = a_x32[:, g % 2, :]
            xlnb = v_gb[:, g % 2, :]
            ldw = [(s_ac, a4p[g - 2])] if g >= 2 else []
            outw = [(s_sy, ln_st[g - 2])] if g >= 2 else []
            a3, a4 = emit_ln_tile(x_in[g * 128:(g + 1) * 128, :], xb, xlnb,
                                  f_sq[:], f_xstage[:], ld_wait=ldw,
                                  out_wait=outw)
            a4p[g] = a4
            wmy = R["xln_my"] if g == 0 else []
            ln_st.append(dma("sync", d_xln_my[g * 128:(g + 1) * 128, :], xlnb,
                             wait=[(s_ac, a3)] + wmy))
            evs = []
            for dmt in range(16):
                t1 = top(lambda e, dmt=dmt: e.transpose(
                    ps[1][:, 0:128], f_xstage[:, dmt * 128:(dmt + 1) * 128],
                    c_id32[:]),
                    wait=[(s_ac, a4), (s_ve, v_const)] +
                    ([(s_ve, evs[-1])] if evs else []))
                evs.append(vop(lambda e, dmt=dmt: e.tensor_copy(
                    out=f_sq[:, dmt * 128:(dmt + 1) * 128], in_=ps[1][:, 0:128]),
                    wait=[(s_te, t1)]))
            xtT = f_sq.rearrange("p (a b) -> p a b", b=128)

            def mmsc(e, xtT=xtT):
                for dmt in range(16):
                    e.matmul(ps[0][:, 0:E], xtT[:, dmt, :], c_rwT[:, dmt, :],
                             start=(dmt == 0), stop=(dmt == 15))
            t2 = top(mmsc, wait=[(s_ve, evs[-1]), (s_sy, ld_rw)] +
                     ([(s_ac, sc_ev[-1])] if sc_ev else []))
            sc_ev.append(sop(lambda e, g=g: e.activation(
                out=f_scmy[:, g, :], in_=ps[0][:, 0:E], func=AF.Sigmoid),
                wait=[(s_te, t2)]))
        ln_all = ln_st[-1]
        sc_all = sc_ev[-1]

        ag_x = coll(d_xln_my[:], d_xln_ag[:], G8,
                    wait=[(s_sy, ln_all)] + R["xln_ag"])

        trs = []
        for ts in range(NTL):
            t1 = top(lambda e, ts=ts: e.transpose(
                ps[1][0:E, 0:128], f_scmy[:, ts, :], c_id32[:]),
                wait=[(s_ac, sc_ev[ts])] + ([(s_ve, trs[-1])] if trs else []))
            trs.append(vop(lambda e, ts=ts: e.tensor_copy(
                out=a_scTb[0:E, ts * 128:(ts + 1) * 128], in_=ps[1][0:E, 0:128]),
                wait=[(s_te, t1)]))
        st_sc = dma("sync", d_scT_my[:], a_scTb[:],
                    wait=[(s_ve, trs[-1])] + R["scT"])
        ag_s = coll(d_scT_my[:], d_scT_ag[:], G8, wait=[(s_sy, st_sc)])

        xlT = [dmaT("scalar", v_xlnT[:, dmt, :],
                    d_xln_my[0:TL, dmt * 128:(dmt + 1) * 128],
                    wait=[(s_sy, ln_all)]) for dmt in range(16)]
        xlT_all = xlT[-1]

        v_loc = emit_router(NTL, f_scmy, l_sc, l_oh, l_g2, l_gm, l_t2, a_lM,
                            a_lek, a_lrk, a_lws, False, [(s_ac, sc_all)])
        def mmcnt(e):
            for g in range(NTL):
                e.matmul(ps[2][0:1, 0:E], c_ones_col[:], a_lM[:, g, :],
                         start=(g == 0), stop=(g == NTL - 1),
                         skip_group_check=True)
        t_cnt = top(mmcnt, wait=[(s_ve, v_loc)])
        v_cr = vop(lambda e: e.tensor_copy(out=a_carry[:], in_=ps[2][0:1, 0:E]),
                   wait=[(s_te, t_cnt)])
        st_cnt = dma("sync", d_cnt_my[:], a_carry[:],
                     wait=[(s_ve, v_cr)] + R["cnt"])
        ag_cnt = coll(d_cnt_my[:], d_cnt_ag[:], G8, wait=[(s_sy, st_cnt)])
        ld_cnt = dma("sync", c_cnt[:], d_cnt_ag[:], wait=[(s_cc, ag_cnt)])
        t_cb = top(lambda e: e.matmul(ps[2][0:1, 0:E], c_rmask[:], c_cnt[:],
                                      start=True, stop=True),
                   wait=[(s_sy, ld_cnt)])
        v_cba = vop(lambda e: e.tensor_copy(out=a_carry2[:], in_=ps[2][0:1, 0:E]),
                   wait=[(s_te, t_cb)])
        lp_ev = []
        ev_c = v_cba
        for g in range(NTL):
            pb = ps[3] if (g % 2 == 0) else ps[4]
            w1 = [(s_ve, ev_c)]
            if g >= 2:
                w1.append((s_ac, lp_ev[g - 2]))
            top(lambda e, g=g, pb=pb: e.matmul(
                pb[:, 0:E], c_ut[:], a_lM[:, g, :], start=True, stop=False,
                skip_group_check=True), wait=w1)
            t2 = top(lambda e, pb=pb: e.matmul(
                pb[:, 0:E], c_ones_row[:], a_carry2[:], start=False, stop=True,
                skip_group_check=True))
            t3 = top(lambda e, g=g: e.matmul(
                ps[2][0:1, 0:E], c_ones_col[:], a_lM[:, g, :],
                start=True, stop=True, skip_group_check=True))
            lp_ev.append(sop(lambda e, g=g, pb=pb: e.copy(
                out=a_lP[:, g, :], in_=pb[:, 0:E]), wait=[(s_te, t2)]))
            ev_c = vop(lambda e: e.tensor_add(a_carry2[:], a_carry2[:],
                                              ps[2][0:1, 0:E]),
                       wait=[(s_te, t3)])
        lp_all = lp_ev[-1]
        for k in range(K):
            vop(lambda e, k=k: e.tensor_tensor(
                out=l_oh[:], in0=bc1(c_i32[:], NTL, E),
                in1=bc2(a_lek[:, k, 0:NTL], E), op=AL.is_equal),
                wait=[(s_ac, lp_all)] if k == 0 else ())
            vop(lambda e: e.tensor_tensor(out=l_oh[:], in0=l_oh[:], in1=a_lP[:],
                                          op=AL.mult))
            vop(lambda e, k=k: e.reduce_sum(out=a_lrk[:, k, :], in_=l_oh[:],
                                            axis=AX.X))
            vop(lambda e, k=k: e.scalar_tensor_tensor(
                out=a_lws[:, 0:NTL], in0=a_lek[:, k, 0:NTL], scalar=float(CPAD),
                in1=a_lrk[:, k, :], op0=AL.mult, op1=AL.add))
            vop(lambda e, k=k: e.tensor_scalar(
                out=l_t2.rearrange("p a b -> p (a b)")[:, 0:NTL],
                in0=a_lrk[:, k, :], scalar1=float(CAP), scalar2=None,
                op0=AL.is_lt))
            vop(lambda e: e.tensor_tensor(
                out=a_lws[:, 0:NTL], in0=a_lws[:, 0:NTL],
                in1=l_t2.rearrange("p a b -> p (a b)")[:, 0:NTL], op=AL.mult))
            vop(lambda e: e.tensor_scalar(
                out=l_t2.rearrange("p a b -> p (a b)")[:, 0:NTL],
                in0=l_t2.rearrange("p a b -> p (a b)")[:, 0:NTL],
                scalar1=1.0, scalar2=float(-DUMMY_ROW),
                op0=AL.subtract, op1=AL.mult))
            vop(lambda e: e.tensor_tensor(
                out=a_lws[:, 0:NTL], in0=a_lws[:, 0:NTL],
                in1=l_t2.rearrange("p a b -> p (a b)")[:, 0:NTL], op=AL.add))
            vop(lambda e, k=k: e.tensor_copy(out=a_sgmy[:, :, k],
                                             in_=a_lws[:, 0:NTL]))
        v_sgmy = vop(lambda e: None)

        ld_sct = dma("sync", f_scT3[:, :, :],
                     d_scT_ag.rearrange("(r e) t -> e r t", e=E),
                     wait=[(s_cc, ag_s), (s_ve, v_sgmy)])
        scb = []
        f_scT_flat = f_scT3.rearrange("e a t -> e (a t)")
        for g in range(NGT):
            t1 = top(lambda e, g=g: e.transpose(
                ps[1][:, 0:E], f_scT_flat[:, g * 128:(g + 1) * 128],
                c_id32[0:E, 0:E]),
                wait=[(s_sy, ld_sct)] + ([(s_ve, scb[-1])] if scb else []))
            scb.append(vop(lambda e, g=g: e.tensor_copy(
                out=f_scores[:, g, :], in_=ps[1][:, 0:E]), wait=[(s_te, t1)]))
        v_rt = emit_router(NGT, f_scores, f_sc, f_oh, f_g2, f_gm, f_t2, f_M,
                           a_ek, a_wk, a_ws, True, [(s_ve, scb[-1])])
        vop(lambda e: e.memset(f_Ww[:], 0.0), wait=[(s_ve, v_rt)])
        for k in range(K):
            vop(lambda e, k=k: e.tensor_tensor(
                out=f_oh[:], in0=bc1(c_i32[:], NGT, E),
                in1=bc2(a_ek[:, k, 0:NGT], E), op=AL.is_equal))
            vop(lambda e, k=k: e.tensor_tensor(
                out=f_oh[:], in0=f_oh[:], in1=bc2(a_wk[:, k, 0:NGT], E),
                op=AL.mult))
            vop(lambda e: e.tensor_tensor(out=f_Ww[:], in0=f_Ww[:], in1=f_oh[:],
                                          op=AL.add))
        v_ww = vop(lambda e: None)

        ev_c = vop(lambda e: e.memset(a_carry[:], 0.0), wait=[(s_ve, v_ww)])
        gp_ev = []
        for g in range(NGT):
            pb = ps[3] if (g % 2 == 0) else ps[4]
            w1 = [(s_ve, ev_c)]
            if g >= 2:
                w1.append((s_ac, gp_ev[g - 2]))
            top(lambda e, g=g, pb=pb: e.matmul(
                pb[:, 0:E], c_ut[:], f_M[:, g, :], start=True, stop=False,
                skip_group_check=True), wait=w1)
            t2 = top(lambda e, pb=pb: e.matmul(
                pb[:, 0:E], c_ones_row[:], a_carry[:], start=False, stop=True,
                skip_group_check=True))
            t3 = top(lambda e, g=g: e.matmul(
                ps[2][0:1, 0:E], c_ones_col[:], f_M[:, g, :],
                start=True, stop=True, skip_group_check=True))
            gp_ev.append(sop(lambda e, g=g, pb=pb: e.copy(
                out=f_P[:, g, :], in_=pb[:, 0:E]), wait=[(s_te, t2)]))
            ev_c = vop(lambda e: e.tensor_add(a_carry[:], a_carry[:],
                                              ps[2][0:1, 0:E]),
                       wait=[(s_te, t3)])
        v_ranks = gp_ev[-1]

        vop(lambda e: e.tensor_tensor(out=f_oh[:], in0=f_P[:], in1=f_M[:],
                                      op=AL.mult), wait=[(s_ac, v_ranks)])
        vop(lambda e: e.tensor_scalar_add(f_M[:], f_M[:], -1.0))
        v_pm = vop(lambda e: e.tensor_tensor(out=f_P[:], in0=f_oh[:], in1=f_M[:],
                                             op=AL.add))
        for el in range(EPC):
            vop(lambda e, el=el: e.tensor_tensor(
                out=f_sc[:], in0=f_P[:],
                in1=c_sel[:, el, :].unsqueeze(1).to_broadcast([128, NGT, E]),
                op=AL.mult))
            vop(lambda e, el=el: e.reduce_sum(out=a_pmy[:, :, el], in_=f_sc[:],
                                              axis=AX.X))
            vop(lambda e, el=el: e.tensor_tensor(
                out=f_sc[:], in0=f_Ww[:],
                in1=c_sel[:, el, :].unsqueeze(1).to_broadcast([128, NGT, E]),
                op=AL.mult))
            vop(lambda e, el=el: e.reduce_sum(out=a_wmy[:, :, el], in_=f_sc[:],
                                              axis=AX.X))
        v_pm = vop(lambda e: None)

        idx_ready = []
        prev_mm = None
        prev_evl = None
        prev_sv = None
        for el in range(EPC):
            vl = vop(lambda e, el=el: e.tensor_copy(
                out=a_lhsT[:, :, 2], in_=a_wmy[:, :, el]),
                wait=[(s_ve, v_pm)] + ([(s_te, prev_mm)] if prev_mm else []))
            ohl = []
            mm_last = None
            for g in range(NGT):
                wv = [(s_ve, vl)] if g == 0 else []
                if g >= 2:
                    wv.append((s_te, ohl[g - 2]))
                ohv = vop(lambda e, g=g, el=el: e.tensor_scalar(
                    out=a_oh[:, g % 2, :], in0=c_i384[:],
                    scalar1=a_pmy[:, g, el:el + 1], scalar2=None,
                    op0=AL.is_equal), wait=wv)
                mmw = [(s_ve, ohv)]
                if g == 0 and prev_evl is not None:
                    mmw.append((s_ac, prev_evl))
                mm_last = top(lambda e, g=g: e.matmul(
                    ps[0][0:3, 0:CPAD], a_lhsT[:, g, :], a_oh[:, g % 2, :],
                    start=(g == 0), stop=(g == NGT - 1), skip_group_check=True),
                    wait=mmw)
                ohl.append(mm_last)
            prev_mm = mm_last
            evl = sop(lambda e, el=el: e.copy(
                out=a_sslot[:, el * CPAD:(el + 1) * CPAD], in_=ps[0][0:3, 0:CPAD]),
                wait=[(s_te, mm_last)])
            prev_evl = evl
            for st in range(3):
                t1 = top(lambda e, el=el, st=st: e.transpose(
                    ps[1][:, 0:3],
                    a_sslot[:, el * CPAD + st * 128:el * CPAD + (st + 1) * 128],
                    c_id32[0:3, 0:3]), wait=[(s_ac, evl)] +
                    ([(s_ve, prev_sv)] if prev_sv else []))
                prev_sv = vop(lambda e, el=el, st=st: e.tensor_copy(
                    out=a_slotT[:, el * 3 + st, :], in_=ps[1][:, 0:3]),
                    wait=[(s_te, t1)])
                vop(lambda e, el=el, st=st: None) if False else None
                vop(lambda e, el=el, st=st: e.scalar_tensor_tensor(
                    out=a_slotT[:, el * 3 + st, 1:2],
                    in0=a_slotT[:, el * 3 + st, 1:2], scalar=128.0,
                    in1=a_slotT[:, el * 3 + st, 0:1], op0=AL.mult, op1=AL.add))
                vop(lambda e, el=el, st=st: e.tensor_copy(
                    out=a_idx[:, el * 3 + st:el * 3 + st + 1],
                    in_=a_slotT[:, el * 3 + st, 1:2]))
                idx_ready.append(vop(lambda e, el=el, st=st: e.tensor_copy(
                    out=a_wcol[:, el * 3 + st:el * 3 + st + 1],
                    in_=a_slotT[:, el * 3 + st, 2:3])))
        v_idx = idx_ready[-1]

        xp_stores = {}
        prev_store = {0: None, 1: None}
        for el in range(EPC):
            gts = []
            for st in range(3):
                gw = [(s_ve, v_idx), (s_cc, ag_x)]
                if prev_store[st % 2] is not None:
                    gw.append((s_sy, prev_store[st % 2]))
                gi = gind(v_gb[:, st % 2, :], d_xln_ag[:],
                          a_idx[:, el * 3 + st:el * 3 + st + 1], wait=gw)
                stv = dma("sync", d_xp[el, st * 128:(st + 1) * 128, :],
                          v_gb[:, st % 2, :], wait=[(s_po, gi)])
                gts.append(stv)
                prev_store[st % 2] = stv
            xp_stores[el] = gts[-1]

        sh_st = []
        for tc in range(2):
            t0 = tc * 512
            pass_w = [(s_sy, sh_st[-1])] if sh_st else []
            prev_ev = {3: None, 4: None}
            prev_wl = {0: None, 1: None}
            for phase, (dsrc, dstv, act, pb, pbi) in enumerate(
                    ((d_sgT, v_shG, AF.Silu, ps[3], 3),
                     (d_suT, v_shU, AF.Identity, ps[4], 4))):
                for is_ in range(16):
                    buf = is_ % 2
                    wlw = []
                    if prev_wl[buf] is not None:
                        wlw.append((s_te, prev_wl[buf]))
                    wl = dma("gpsimd",
                             Cc[:, buf * 2:(buf + 1) * 2, :].rearrange(
                                 "p a b -> p (a b)").rearrange(
                                 "p (a b) -> p a b", b=128),
                             dsrc[:, is_ * 128:(is_ + 1) * 128].rearrange(
                                 "(a p) b -> p a b", p=128), wait=wlw)
                    wv = Cf[:, buf * 2048:(buf + 1) * 2048].rearrange(
                        "p (a b) -> p a b", b=128)

                    def mmsh(e, wv=wv, t0=t0, pb=pb):
                        for dmt in range(16):
                            e.matmul(pb[:, 0:512], wv[:, dmt, :],
                                     v_xlnT[:, dmt, t0:t0 + 512],
                                     start=(dmt == 0), stop=(dmt == 15))
                    t1 = top(mmsh, wait=[(s_po, wl), (s_sc, xlT_all)] +
                             pass_w +
                             ([(s_ac, prev_ev[pbi])] if prev_ev[pbi] else []))
                    prev_wl[buf] = t1
                    prev_ev[pbi] = sop(
                        lambda e, is_=is_, dstv=dstv, act=act, pb=pb: e.activation(
                            out=dstv[:, is_, :], in_=pb[:, 0:512], func=act),
                        wait=[(s_te, t1)])
            vhh = vop(lambda e: e.tensor_tensor(
                out=v_shG.rearrange("p a b -> p (a b)"),
                in0=v_shG.rearrange("p a b -> p (a b)"),
                in1=v_shU.rearrange("p a b -> p (a b)"), op=AL.mult),
                wait=[(s_ac, prev_ev[3]), (s_ac, prev_ev[4])])
            a1_prev = [None] * 4
            dn_last = None
            for dc in range(4):
                wl = dma("gpsimd",
                         Cc[:, 8:16, :].rearrange("p a b -> p (a b)").rearrange(
                             "p (a b) -> p a b", b=512),
                         d_sdT[:, dc * 512:(dc + 1) * 512].rearrange(
                             "(a p) b -> p a b", p=128),
                         wait=[(s_te, dn_last)] if dn_last else ())
                wv = Cf[:, 8 * 1024:16 * 1024].rearrange("p (a b) -> p a b", b=512)
                for tt in range(4):
                    pb = ps[3 + tt]

                    def mmdn(e, tt=tt, wv=wv, pb=pb):
                        for it in range(16):
                            e.matmul(pb[:, 0:512],
                                     v_shG[:, it, tt * 128:(tt + 1) * 128],
                                     wv[:, it, :], start=(it == 0), stop=(it == 15))
                    t1 = top(mmdn, wait=[(s_ve, vhh), (s_po, wl)] +
                             ([(s_ac, a1_prev[tt])] if a1_prev[tt] else []))
                    dn_last = t1
                    a1 = sop(lambda e, dc=dc, tt=tt, pb=pb: e.activation(
                        out=v_shY[:, tt, dc * 512:(dc + 1) * 512],
                        in_=pb[:, 0:512], func=AF.Identity), wait=[(s_te, t1)])
                    a1_prev[tt] = a1
                    if dc == 3:
                        sh_st.append(dma("sync",
                                         d_shared[t0 + tt * 128:
                                                  t0 + (tt + 1) * 128, :],
                                         v_shY[:, tt, :],
                                         wait=[(s_ac, a1)]))
        sh_all = sh_st[-1]

        y_st = []
        y_prev = {0: None, 1: None}
        ev_prev2 = None
        wd_prev = {0: None, 1: None}
        for el in range(EPC):
            xls = [dmaT("scalar", v_xpT[:, dmt, :],
                        d_xp[el, 0:CPAD, dmt * 128:(dmt + 1) * 128],
                        wait=[(s_sy, xp_stores[el])]) for dmt in range(16)]
            pe_prev = {0: None, 1: None}
            we_prev = {}
            for phase, (dsrc, dstv, act, pb, cbase) in enumerate(
                    ((d_wg, v_eg, AF.Silu, ps[0], 0),
                     (d_wu, v_eu, AF.Identity, ps[1], 4))):
                for is_ in range(8):
                    buf = cbase + (is_ % 2) * 2
                    wlw = [(s_te, we_prev[buf])] if buf in we_prev else []
                    wl = dma("gpsimd",
                             Cc[:, buf:buf + 2, :].rearrange(
                                 "p a b -> p (a b)").rearrange(
                                 "p (a b) -> p a b", b=128),
                             dsrc[el, :, is_ * 128:(is_ + 1) * 128].rearrange(
                                 "(a p) b -> p a b", p=128), wait=wlw)
                    wv = Cf[:, buf * 1024:(buf + 2) * 1024].rearrange(
                        "p (a b) -> p a b", b=128)

                    def mme(e, wv=wv, pb=pb):
                        for dmt in range(16):
                            e.matmul(pb[:, 0:CPAD], wv[:, dmt, :], v_xpT[:, dmt, :],
                                     start=(dmt == 0), stop=(dmt == 15))
                    t1 = top(mme, wait=[(s_po, wl), (s_sc, xls[-1])] +
                             ([(s_ac, pe_prev[phase])] if pe_prev[phase] else []))
                    we_prev[buf] = t1
                    pe_prev[phase] = sop(
                        lambda e, is_=is_, dstv=dstv, act=act, pb=pb: e.activation(
                            out=dstv[:, is_, :], in_=pb[:, 0:CPAD], func=act),
                        wait=[(s_te, t1)])
            vhh = vop(lambda e: e.tensor_tensor(
                out=v_eg.rearrange("p a b -> p (a b)"),
                in0=v_eg.rearrange("p a b -> p (a b)"),
                in1=v_eu.rearrange("p a b -> p (a b)"), op=AL.mult),
                wait=[(s_ac, pe_prev[0]), (s_ac, pe_prev[1])])
            for dc in range(4):
                buf = 8 + (dc % 2) * 4
                wlw = [(s_te, wd_prev[dc % 2])] if wd_prev[dc % 2] else []
                wl = dma("gpsimd",
                         Cc[:, buf:buf + 4, :].rearrange(
                             "p a b -> p (a b)").rearrange(
                             "p (a b) -> p a b", b=512),
                         d_wd[el, :, dc * 512:(dc + 1) * 512].rearrange(
                             "(a p) b -> p a b", p=128), wait=wlw)
                wv = Cf[:, buf * 1024:(buf + 4) * 1024].rearrange(
                    "p (a b) -> p a b", b=512)
                for st in range(3):
                    def mmd(e, st=st, wv=wv):
                        for it in range(8):
                            e.matmul(ps[2][:, 0:512],
                                     v_eg[:, it, st * 128:(st + 1) * 128],
                                     wv[:, it, :], start=(it == 0), stop=(it == 7))
                    t1 = top(mmd, wait=[(s_ve, vhh), (s_po, wl)] +
                             ([(s_ve, ev_prev2)] if ev_prev2 else []))
                    wd_prev[dc % 2] = t1
                    evw = [(s_te, t1)]
                    if y_prev[st % 2] is not None:
                        evw.append((s_sy, y_prev[st % 2]))
                    ev = vop(lambda e, st=st, dc=dc, el=el: e.tensor_scalar(
                        out=a_y[:, st % 2, :], in0=ps[2][:, 0:512],
                        scalar1=a_wcol[:, el * 3 + st:el * 3 + st + 1],
                        scalar2=None, op0=AL.mult), wait=evw)
                    ev_prev2 = ev
                    yv = dma("sync",
                             d_res_my[el * CPAD + st * 128:
                                      el * CPAD + (st + 1) * 128,
                                      dc * 512:(dc + 1) * 512],
                             a_y[:, st % 2, :], wait=[(s_ve, ev)])
                    y_st.append(yv)
                    y_prev[st % 2] = yv
        ag_r = coll(d_res_my[:], d_res_ag[0:8 * SEG, :], G8,
                    wait=[(s_sy, y_st[-1])] + R["res"])

        out_st = []
        last_g = None
        v1p = None
        v4p = None
        for g in range(NTL):
            gsems = []
            for k in range(K):
                gw = [(s_cc, ag_r)]
                if v1p is not None:
                    gw.append((s_ve, v1p))
                gi = gind(v_comb[:, k, :], d_res_ag[:],
                          a_sgmy[:, g, k:k + 1], wait=gw)
                gsems.append(gi)
            last_g = gsems[-1]
            xw = [(s_sy, out_st[g - 2])] if g >= 2 else []
            ldx = dma("sync", a_x32[:, g % 2, :], x_in[g * 128:(g + 1) * 128, :],
                      wait=xw)
            lds = dma("scalar", v_gb[:, 1, :], d_shared[g * 128:(g + 1) * 128, :],
                      wait=[(s_sy, sh_all)] + ([(s_ve, v4p)] if v4p else []))
            v1 = vop(lambda e: e.tensor_tensor(out=f_sq[:], in0=v_comb[:, 0, :],
                                               in1=v_comb[:, 1, :], op=AL.add),
                     wait=[(s_po, gsems[-1])])
            vop(lambda e: e.tensor_tensor(out=f_xstage[:], in0=v_comb[:, 2, :],
                                          in1=v_comb[:, 3, :], op=AL.add))
            vop(lambda e: e.tensor_tensor(out=f_sq[:], in0=f_sq[:],
                                          in1=f_xstage[:], op=AL.add))
            v4p = vop(lambda e: e.tensor_tensor(out=f_sq[:], in0=f_sq[:],
                                                in1=v_gb[:, 1, :], op=AL.add),
                      wait=[(s_sc, lds)])
            v5 = vop(lambda e, g=g: e.scalar_tensor_tensor(
                out=a_x32[:, g % 2, :], in0=f_sq[:], scalar=0.5,
                in1=a_x32[:, g % 2, :], op0=AL.mult, op1=AL.add),
                wait=[(s_sy, ldx)])
            v1p = v5
            out_st.append(dma("sync", x_out[g * 128:(g + 1) * 128, :],
                              a_x32[:, g % 2, :], wait=[(s_ve, v5)]))
        R["xln_my"] = [(s_sc, xlT_all)]
        R["xln_ag"] = [(s_sy, xp_stores[EPC - 1])]
        R["scT"] = [(s_sy, ld_sct)]
        R["res"] = [(s_po, last_g)]
        R["cnt"] = [(s_sy, ld_cnt)]
        return out_st[-1]

    def emit_attn(x_in, x_out):
        def sub_exit(w0):
            prev = None
            for q in range(4):
                w = list(w0) if q == 0 else []
                prev = dma("sync", x_out[q * 256:(q + 1) * 256, :],
                           x_in[q * 256:(q + 1) * 256, :], wait=w)
            return prev
        ln_st = []
        a3p = {}
        for g in range(NTL):
            xb = a_x32[:, g % 2, :]
            xlnb = v_gb[:, g % 2, :]
            ldw = [(s_ac, a3p[g - 2])] if g >= 2 else []
            outw = [(s_sy, ln_st[g - 2])] if g >= 2 else []
            a3, _ = emit_ln_tile(x_in[g * 128:(g + 1) * 128, :], xb, xlnb,
                                 n_sq[:], None, ld_wait=ldw, out_wait=outw)
            a3p[g] = a3
            wmy = R["xln_my"] if g == 0 else []
            ln_st.append(dma("sync", d_xln_my[g * 128:(g + 1) * 128, :], xlnb,
                             wait=[(s_ac, a3)] + wmy))
        ln_all = ln_st[-1]
        ag_b = coll(d_xln_my[:], d_xba[:], GP, wait=[(s_sy, ln_all)] + R["xba"])

        xlT = [dmaT("scalar", A[:, dmt, :],
                    d_xba[0:S, dmt * 128:(dmt + 1) * 128],
                    wait=[(s_cc, ag_b)]) for dmt in range(16)]
        xlT_all = xlT[-1]
        axlnT = A

        wvl = []
        for ch in range(2):
            wvl.append(dma("gpsimd",
                           Dd[:, ch * 8:(ch + 1) * 8, :].rearrange(
                               "p a b -> p (a b)").rearrange(
                               "p (a b) -> p a b", b=512),
                           d_wv[:, ch * 512:(ch + 1) * 512].rearrange(
                               "(a p) b -> p a b", p=128)))
        v_ev = []
        for ch in range(2):
            wv = Df[:, ch * 8192:(ch + 1) * 8192].rearrange(
                "p (a b) -> p a b", b=512)
            for ts in range(16):
                def mmv(e, ch=ch, ts=ts, wv=wv):
                    for dmt in range(16):
                        e.matmul(ps[6][:, 0:512],
                                 axlnT[:, dmt, ts * 128:(ts + 1) * 128],
                                 wv[:, dmt, :], start=(dmt == 0), stop=False,
                                 skip_group_check=True)
                    e.matmul(ps[6][:, 0:512], c_ones16[:],
                             c_bv16[:, ch * 512:(ch + 1) * 512],
                             start=False, stop=True, skip_group_check=True)
                t1 = top(mmv, wait=[(s_po, wvl[ch]), (s_sc, xlT_all),
                                    (s_po, ld_bv)] +
                         ([(s_ac, v_ev[-1])] if v_ev else []))
                v_ev.append(sop(lambda e, ch=ch, ts=ts: e.activation(
                    out=Cc[:, ts, ch * 512:(ch + 1) * 512], in_=ps[6][:, 0:512],
                    func=AF.Identity), wait=[(s_te, t1)]))
        v_all = v_ev[-1]
        if ASUB == 1:
            return sub_exit([(s_ac, v_all)])

        qk_sp = []
        nrm_dma = []
        prev_k_te = None
        prev_spill = {0: None, 1: None}
        q_ev_pp = None
        k_ev_pp = None
        nev_pp = None
        for h in range(HPC):
            hb = h % 2
            qdst = t_q0 if hb == 0 else t_q1
            kdst = t_k0 if hb == 0 else t_k1
            wqw = [(s_ac, v_all)] if h == 0 else [(s_te, prev_k_te)]
            if prev_spill[hb] is not None:
                wqw.append((s_sy, prev_spill[hb]))
            wql = dma("gpsimd",
                      Dd[:, 8:10, :].rearrange("p a b -> p (a b)").rearrange(
                          "p (a b) -> p a b", b=128),
                      d_wq[:, h * 128:(h + 1) * 128].rearrange(
                          "(a p) b -> p a b", p=128),
                      wait=wqw)
            qev = []
            for tc in range(4):
                def mmq(e, tc=tc):
                    for dmt in range(16):
                        e.matmul(ps[0][:, 0:512], t_whd[:, dmt, :],
                                 axlnT[:, dmt, tc * 512:(tc + 1) * 512],
                                 start=(dmt == 0), stop=(dmt == 15))
                t1 = top(mmq, wait=[(s_po, wql)] +
                         ([(s_ac, qev[-1])] if qev else
                          ([(s_ac, q_ev_pp)] if q_ev_pp else [])))
                qev.append(sop(lambda e, tc=tc, h=h, qdst=qdst: e.activation(
                    out=qdst[:, tc * 512:(tc + 1) * 512], in_=ps[0][:, 0:512],
                    func=AF.Identity, bias=c_bqkv[:, h:h + 1]),
                    wait=[(s_te, t1)]))
            wkl = dma("gpsimd",
                      Dd[:, 8:10, :].rearrange("p a b -> p (a b)").rearrange(
                          "p (a b) -> p a b", b=128),
                      d_wk[:, h * 128:(h + 1) * 128].rearrange(
                          "(a p) b -> p a b", p=128),
                      wait=[(s_te, p.val("s_te"))])
            kev = []
            for tc in range(4):
                def mmk(e, tc=tc):
                    for dmt in range(16):
                        e.matmul(ps[1][:, 0:512], t_whd[:, dmt, :],
                                 axlnT[:, dmt, tc * 512:(tc + 1) * 512],
                                 start=(dmt == 0), stop=(dmt == 15))
                t1 = top(mmk, wait=[(s_po, wkl)] +
                         ([(s_ac, kev[-1])] if kev else
                          ([(s_ac, k_ev_pp)] if k_ev_pp else [])))
                kev.append(sop(lambda e, tc=tc, h=h, kdst=kdst: e.activation(
                    out=kdst[:, tc * 512:(tc + 1) * 512], in_=ps[1][:, 0:512],
                    func=AF.Identity, bias=c_bqkv[:, HPC + h:HPC + h + 1]),
                    wait=[(s_te, t1)]))
            aq = sop(lambda e, qdst=qdst: e.activation(
                out=n_sq[:], in_=qdst, func=AF.Square), wait=[(s_ac, qev[-1])])
            nevs = []
            for chk in range(4):
                nw = [(s_ac, aq)]
                if nevs:
                    nw.append((s_ve, nevs[-1]))
                elif nev_pp is not None:
                    nw.append((s_ve, nev_pp))
                t1 = top(lambda e, chk=chk: e.matmul(
                    ps[2][0:1, 0:512], c_ones_col[:],
                    n_sq[:, chk * 512:(chk + 1) * 512], start=True, stop=True,
                    skip_group_check=True), wait=nw)
                cw = [(s_te, t1)]
                if not nevs and nrm_dma:
                    cw.append((s_sy, nrm_dma[-1]))
                nevs.append(vop(lambda e, chk=chk: e.tensor_copy(
                    out=n_p0[:, chk * 512:(chk + 1) * 512], in_=ps[2][0:1, 0:512]),
                    wait=cw))
            nrm_dma.append(dma("sync", n_norm[h:h + 1, :], n_p0[:],
                               wait=[(s_ve, nevs[-1])]))
            qk_sp.append(dma("sync", d_qk[h * 128:(h + 1) * 128, :], qdst,
                             wait=[(s_ac, aq)]))
            ksp = dma("sync", d_qk[1024 + h * 128:1024 + (h + 1) * 128, :],
                      kdst, wait=[(s_ac, kev[-1])])
            qk_sp.append(ksp)
            prev_spill[hb] = ksp
            prev_k_te = p.val("s_te")
            q_ev_pp = qev[-1]
            k_ev_pp = kev[-1]
            nev_pp = nevs[-1]
        qk_all = qk_sp[-1]
        if ASUB == 2:
            return sub_exit([(s_sy, qk_all)])

        v0 = vop(lambda e: e.tensor_copy(out=n_work[:], in_=n_norm[:]),
                 wait=[(s_sy, nrm_dma[-1])])
        for rnd in range(8):
            vop(lambda e: e.max(out=a_gsum[0:8, :], in_=n_work[:]))
            if rnd == 7:
                vop(lambda e: e.memset(a_gsum[0:8, 6:8], -1.0))
            vop(lambda e: e.match_replace(out=n_work[:], in_to_replace=a_gsum[0:8, :],
                                          in_values=n_work[:], imm_value=-1.0))
        vop(lambda e: e.tensor_tensor(out=n_m[:], in0=n_norm[:], in1=n_work[:],
                                      op=AL.not_equal))
        vop(lambda e: e.memset(n_m[0:8, 0:1], 1.0))
        vop(lambda e: e.memset(n_m[0:8, 2047:2048], 1.0))
        vop(lambda e: e.tensor_tensor_scan(out=n_r[:], data0=n_m[:], data1=n_m[:],
                                           initial=0.0, op0=AL.add, op1=AL.bypass))
        v_scan = vop(lambda e: e.tensor_sub(n_r[:], n_r[:], n_m[:]))
        rc_ev = []
        for g in range(16):
            t1 = top(lambda e, g=g: e.transpose(
                ps[2][:, 0:8], n_r[:, g * 128:(g + 1) * 128], c_id32[0:8, 0:8]),
                wait=[(s_ve, v_scan)] + ([(s_ve, rc_ev[-1])] if rc_ev else []))
            rc_ev.append(vop(lambda e, g=g: e.tensor_copy(
                out=a_rcm[:, g, 0, :], in_=ps[2][:, 0:8]), wait=[(s_te, t1)]))
            t2 = top(lambda e, g=g: e.transpose(
                ps[2][:, 0:8], n_m[:, g * 128:(g + 1) * 128], c_id32[0:8, 0:8]),
                wait=[(s_ve, rc_ev[-1])])
            rc_ev.append(vop(lambda e, g=g: e.tensor_copy(
                out=a_rcm[:, g, 1, :], in_=ps[2][:, 0:8]), wait=[(s_te, t2)]))
        v_rcm = rc_ev[-1]
        if ASUB == 3:
            return sub_exit([(s_ve, v_rcm)])

        xg_st = []
        wx_prev = None
        gi_prev = None
        for h in range(HPC):
            mml = None
            for g in range(16):
                w0l = []
                if g == 0 and h == 0:
                    w0l = [(s_ve, v_rcm)]
                elif g == 0:
                    w0l = [(s_ve, wx_prev)]
                else:
                    w0l = [(s_te, mml)]
                v1 = vop(lambda e, g=g, h=h: e.tensor_scalar(
                    out=a_wtf[:], in0=c_i64[:],
                    scalar1=a_rcm[:, g, 0, h:h + 1], scalar2=None,
                    op0=AL.is_equal), wait=w0l)
                v2 = vop(lambda e, g=g, h=h: e.tensor_scalar(
                    out=a_wtf[:], in0=a_wtf[:],
                    scalar1=a_rcm[:, g, 1, h:h + 1], scalar2=None, op0=AL.mult))
                mml = top(lambda e, g=g: e.matmul(
                    ps[3][0:64, 0:3], a_wtf[:, 0:64], c_gatok[:, g, :],
                    start=(g == 0), stop=(g == 15), skip_group_check=True),
                    wait=[(s_ve, v2)])
            ev = vop(lambda e, h=h: e.tensor_copy(out=a_gidx[:, 0:3],
                                                  in_=ps[3][0:64, 0:3]),
                     wait=[(s_te, mml)])
            vop(lambda e: e.scalar_tensor_tensor(
                out=a_gidx[:, 1:2], in0=a_gidx[:, 1:2], scalar=128.0,
                in1=a_gidx[:, 0:1], op0=AL.mult, op1=AL.add))
            vop(lambda e: e.tensor_scalar(out=a_gidx[:, 3:4], in0=a_gidx[:, 2:3],
                                          scalar1=0.0, scalar2=None, op0=AL.is_gt))
            vop(lambda e: e.tensor_tensor(out=a_gidx[:, 4:5], in0=a_gidx[:, 1:2],
                                          in1=a_gidx[:, 3:4], op=AL.mult))
            vop(lambda e: e.tensor_tensor(out=a_gidx[:, 5:6], in0=a_gidx[:, 4:5],
                                          in1=a_gidx[:, 3:4], op=AL.mult)
                if False else e.tensor_copy(out=a_gidx[:, 5:6], in_=a_gidx[:, 4:5]))
            vop(lambda e: e.tensor_scalar(out=a_gidx[:, 6:7], in0=a_gidx[:, 3:4],
                                          scalar1=1.0, scalar2=-2048.0,
                                          op0=AL.subtract, op1=AL.mult))
            vop(lambda e: e.tensor_tensor(out=a_gidx[:, 5:6], in0=a_gidx[:, 5:6],
                                          in1=a_gidx[:, 6:7], op=AL.add))
            vz = vop(lambda e: e.memset(a_gi32[:], 0),
                     wait=[(s_po, gi_prev)] if gi_prev else ())
            v3 = vop(lambda e: e.tensor_copy(out=a_gi32[0:64, 0:1],
                                             in_=a_gidx[:, 4:5]))
            v4 = vop(lambda e, h=h: e.tensor_scalar(
                out=a_sci[0:64, h:h + 1], in0=a_gidx[:, 5:6],
                scalar1=float(h * 2049), scalar2=None, op0=AL.add))
            wx_prev = v4
            giw = [(s_ve, v4), (s_cc, ag_b)]
            if xg_st:
                giw.append((s_sy, xg_st[-1]))
            gi = gind(Dd[:, 12:14, :].rearrange("p a b -> p (a b)"),
                      d_xba[:], a_gi32[:, 0:1], wait=giw)
            gi_prev = gi
            xg_st.append(dma("sync", d_xg[h * 128:(h + 1) * 128, :],
                             Dd[:, 12:14, :].rearrange("p a b -> p (a b)"),
                             wait=[(s_po, gi)]))
        xgT = [dmaT("scalar", t_xgT[:, dmt, :],
                    d_xg[0:1024, dmt * 128:(dmt + 1) * 128],
                    wait=[(s_sy, xg_st[-1]), (s_sc, xlT_all)])
               for dmt in range(16)]
        xgT_all = xgT[-1]
        if ASUB == 4:
            return sub_exit([(s_sc, xgT_all)])

        vop(lambda e: e.memset(t_pr[0][:, 0:64], 0.0), wait=[(s_ac, v_all)])
        vz_pad = vop(lambda e: e.memset(t_pr[1][:, 0:64], 0.0))
        scat_po = []
        prev_core_te = {0: 0, 1: 0}
        gev_pp = None
        evg_pp = None
        pp_a1 = {0: None, 1: None}
        pp_mxv = {0: None, 1: None}
        pp_trs_t = {0: None, 1: None}
        pp_trs_c = {0: None, 1: None}
        pp_ev = {0: None, 1: None}
        for h in range(HPC):
            hb = h % 2
            qb = t_q0 if hb == 0 else t_q1
            kb = t_k0 if hb == 0 else t_k1
            lqw = [(s_sy, qk_all), (s_sc, xgT_all)]
            if prev_core_te[hb]:
                lqw.append((s_te, prev_core_te[hb]))
            lq = dma("sync", qb, d_qk[h * 128:(h + 1) * 128, :], wait=lqw)
            lk = dma("sync", kb, d_qk[1024 + h * 128:1024 + (h + 1) * 128, :])
            st_blocks = []

            def emit_scores(b, h=h, qb=qb, kb=kb, lk=lk):
                w0 = max(b * 128 - 64, 0)
                W = min(b * 128 + 192, S) - w0
                par = b % 2
                sb = ps[0] if par == 0 else ps[7]
                smw = [(s_sy, lk)]
                if h == 0 and b <= 1:
                    smw.append((s_ve, vz_pad))
                if pp_a1[par] is not None:
                    smw += [(s_ac, pp_a1[par]), (s_ve, pp_mxv[par])]
                return top(lambda e, b=b, w0=w0, W=W, sb=sb: e.matmul(
                    sb[:, 0:W], qb[:, b * 128:(b + 1) * 128], kb[:, w0:w0 + W],
                    start=True, stop=True), wait=smw)

            t1s = {0: emit_scores(0)}
            for b in range(NB):
                if b + 1 < NB:
                    t1s[b + 1] = emit_scores(b + 1)
                par = b % 2
                sb = ps[0] if par == 0 else ps[7]
                tb = ps[1] if par == 0 else ps[6]
                cb = ps[2] if par == 0 else ps[5]
                w0 = max(b * 128 - 64, 0)
                W = min(b * 128 + 192, S) - w0
                mx = a_stat[:, 5 + 3 * par:6 + 3 * par]
                sm = a_stat[:, 6 + 3 * par:7 + 3 * par]
                rs = a_stat[:, 7 + 3 * par:8 + 3 * par]
                mxw = [(s_te, t1s[b])]
                if pp_a1[par] is not None:
                    mxw.append((s_ac, pp_a1[par]))
                mxv = vop(lambda e, W=W, mx=mx, sb=sb: e.reduce_max(
                    out=mx, in_=sb[:, 0:W], axis=AX.X), wait=mxw)
                pp_mxv[par] = mxv
                v2 = vop(lambda e, mx=mx: e.tensor_scalar_mul(mx, mx, -ISQ))
                a1w = [(s_ve, v2)]
                if pp_trs_t[par] is not None:
                    a1w.append((s_te, pp_trs_t[par]))
                a1 = sop(lambda e, W=W, mx=mx, sm=sm, sb=sb, par=par:
                         e.activation(
                             out=t_pr[par][:, 64:64 + W], in_=sb[:, 0:W],
                             func=AF.Exp, bias=mx, scale=ISQ, accum_out=sm),
                         wait=a1w)
                pp_a1[par] = a1
                v3 = vop(lambda e, sm=sm, rs=rs: e.reciprocal(rs, sm),
                         wait=[(s_ac, a1)])
                if b == 0:
                    segs = [(64, 128, 0, 0), (192, 64, 1, 1)]
                elif b == NB - 1:
                    segs = [(0, 128, 0, b - 1), (128, 128, 1, b)]
                else:
                    segs = [(0, 128, 0, b - 1), (128, 128, 1, b),
                            (256, 64, 2, b + 1)]
                trs = []
                for (c0, n, reg, ttile) in segs:
                    t2w = [(s_ac, a1)]
                    if trs:
                        t2w.append((s_ve, trs[-1][1]))
                    elif pp_trs_c[par] is not None:
                        t2w.append((s_ve, pp_trs_c[par]))
                    t2 = top(lambda e, c0=c0, n=n, tb=tb, par=par: e.transpose(
                        tb.ap().bitcast(BF16)[0:n, 0:128],
                        t_pr[par][:, c0:c0 + n], c_id16[:]), wait=t2w)
                    trs.append((t2, vop(
                        lambda e, n=n, reg=reg, tb=tb, par=par: e.tensor_copy(
                            out=t_prT[par][0:n, reg * 128:reg * 128 + 128],
                            in_=tb.ap().bitcast(BF16)[0:n, 0:128]),
                        wait=[(s_te, t2)])))
                pp_trs_t[par] = trs[-1][0]
                pp_trs_c[par] = trs[-1][1]

                def ctx_mms(e, h=h, segs=segs, cb=cb, par=par):
                    for i, (c0, n, reg, ttile) in enumerate(segs):
                        e.matmul(cb[:, 0:DH],
                                 t_prT[par][0:n, reg * 128:reg * 128 + 128],
                                 Cc[0:n, ttile, h * 128:(h + 1) * 128],
                                 start=(i == 0), stop=(i == len(segs) - 1),
                                 skip_group_check=True)
                t3w = [(s_ve, trs[-1][1])]
                if pp_ev[par] is not None:
                    t3w.append((s_ve, pp_ev[par]))
                if b == 1 and evg_pp is not None:
                    t3w.append((s_ve, evg_pp))
                t3 = top(ctx_mms, wait=t3w)
                evw2 = [(s_te, t3), (s_ve, v3)]
                if b >= 4:
                    evw2.append((s_sy, st_blocks[b - 4]))
                ev = vop(lambda e, b=b, rs=rs, cb=cb: e.tensor_scalar(
                    out=t_ctxs[:, b % 4, :], in0=cb[:, 0:DH],
                    scalar1=rs, scalar2=None, op0=AL.mult), wait=evw2)
                pp_ev[par] = ev
                st_blocks.append(dma("sync",
                                     d_ctx_my[h * 2049 + b * 128:
                                              h * 2049 + (b + 1) * 128, :],
                                     t_ctxs[:, b % 4, :], wait=[(s_ve, ev)]))
            if ASUB == 44:
                prev_core_te[hb] = p.val("s_te")
                continue
            wq2 = dma("gpsimd",
                      Dd[:, 8:10, :].rearrange("p a b -> p (a b)").rearrange(
                          "p (a b) -> p a b", b=128),
                      d_wq[:, h * 128:(h + 1) * 128].rearrange(
                          "(a p) b -> p a b", p=128),
                      wait=[(s_te, p.val("s_te"))])

            def mmqg(e, h=h):
                for dmt in range(16):
                    e.matmul(ps[3][:, 0:64], t_whd[:, dmt, :],
                             t_xgT[:, dmt, h * 128:h * 128 + 64],
                             start=(dmt == 0), stop=(dmt == 15))
            tqg = top(mmqg, wait=[(s_po, wq2)] +
                      ([(s_ac, gev_pp)] if gev_pp else []))
            aqg = sop(lambda e, h=h: e.activation(
                out=a_qgT[:], in_=ps[3][:, 0:64], func=AF.Identity,
                bias=c_bqkv[:, h:h + 1]), wait=[(s_te, tqg)])
            if ASUB == 4411:
                prev_core_te[hb] = p.val("s_te")
                gev_pp = aqg
                continue
            gevs = []
            gmx_prev = 0
            for chk in range(4):
                t1 = top(lambda e, chk=chk, kb=kb: e.matmul(
                    ps[3][0:64, 0:512], a_qgT[:], kb[:, chk * 512:(chk + 1) * 512],
                    start=True, stop=True),
                    wait=[(s_ac, aqg)] + ([(s_ac, gevs[-1]),
                                          (s_ve, gmx_prev)] if gevs else []))
                gevs.append(sop(lambda e, chk=chk: e.activation(
                    out=t_gq[:, chk * 512:(chk + 1) * 512], in_=ps[3][0:64, 0:512],
                    func=AF.Identity), wait=[(s_te, t1)]))
                gmx_prev = vop(lambda e, chk=chk: e.reduce_max(
                    out=a_gsum[:, chk:chk + 1],
                    in_=t_gq[:, chk * 512:(chk + 1) * 512],
                    axis=AX.X), wait=[(s_ac, gevs[-1])])
            if ASUB == 4412:
                prev_core_te[hb] = p.val("s_te")
                gev_pp = gevs[-1]
                continue
            vop(lambda e: e.tensor_tensor(out=a_gsum[:, 4:5], in0=a_gsum[:, 0:1],
                                          in1=a_gsum[:, 1:2], op=AL.max),
                wait=[(s_ac, gevs[-1])])
            vop(lambda e: e.tensor_tensor(out=a_gsum[:, 4:5], in0=a_gsum[:, 4:5],
                                          in1=a_gsum[:, 2:3], op=AL.max))
            vop(lambda e: e.tensor_tensor(out=a_gsum[:, 4:5], in0=a_gsum[:, 4:5],
                                          in1=a_gsum[:, 3:4], op=AL.max))
            vmx = vop(lambda e: e.tensor_scalar_mul(a_gsum[:, 4:5],
                                                    a_gsum[:, 4:5], -ISQ))
            asum = sop(lambda e: e.activation(
                out=t_gq[:], in_=t_gq[:],
                func=AF.Exp, bias=a_gsum[0:64, 4:5], scale=ISQ,
                accum_out=a_gsum[0:64, 5:6]), wait=[(s_ve, vmx)])
            vrs = vop(lambda e: e.reciprocal(a_gsum[:, 5:6], a_gsum[:, 5:6]),
                      wait=[(s_ac, asum)])
            if ASUB == 441:
                prev_core_te[hb] = p.val("s_te")
                gev_pp = gevs[-1]
                continue
            gtr = []
            for j in range(16):
                t1 = top(lambda e, j=j: e.transpose(
                    ps[4].ap().bitcast(BF16)[:, 0:64],
                    t_gq[:, j * 128:(j + 1) * 128], c_id16[0:64, 0:64]),
                    wait=[(s_ac, asum)] + ([(s_ve, gtr[-1])] if gtr else []))
                gtr.append(vop(lambda e, j=j: e.tensor_copy(
                    out=t_gpT[:, j, :], in_=ps[4].ap().bitcast(BF16)[:, 0:64]),
                    wait=[(s_te, t1)]))

            if ASUB == 442:
                prev_core_te[hb] = p.val("s_te")
                gev_pp = gevs[-1]
                continue

            def mmgc(e, h=h):
                for j in range(16):
                    e.matmul(ps[5][0:64, 0:DH], t_gpT[:, j, :],
                             Cc[:, j, h * 128:(h + 1) * 128],
                             start=(j == 0), stop=(j == 15))
            t2 = top(mmgc, wait=[(s_ve, gtr[-1])] +
                     ([(s_ve, evg_pp)] if evg_pp else []))
            evgw = [(s_te, t2), (s_ve, vrs)]
            if scat_po:
                evgw.append((s_po, scat_po[-1]))
            evg = vop(lambda e: e.tensor_scalar(
                out=a_gctx[:], in0=ps[5][0:64, 0:DH],
                scalar1=a_gsum[0:64, 5:6], scalar2=None, op0=AL.mult),
                wait=evgw)
            if ASUB == 443:
                prev_core_te[hb] = p.val("s_te")
                gev_pp = gevs[-1]
                evg_pp = evg
                continue
            v5 = vop(lambda e, h=h: e.tensor_copy(out=a_gi32[0:64, 1:2],
                                                  in_=a_sci[0:64, h:h + 1]))
            scat_po.append(gscat(d_ctx_my[:], a_gi32[0:64, 1:2], a_gctx[:],
                                 wait=[(s_ve, evg), (s_ve, v5),
                                       (s_sy, st_blocks[-1])]))
            prev_core_te[hb] = p.val("s_te")
            gev_pp = gevs[-1]
            evg_pp = evg
        if ASUB == 44:
            return sub_exit([(s_sy, st_blocks[-1])])
        if ASUB in (4411, 4412, 441, 442, 443):
            return sub_exit([(s_te, p.val("s_te")), (s_ve, p.val("s_ve"))])
        if ASUB == 45:
            return sub_exit([(s_po, scat_po[-1])])
        ag_c = coll(d_ctx_my[:], d_ctx_pag[:], GP,
                    wait=[(s_po, scat_po[-1]), (s_sy, pad_st)] + R["ctx"])
        if ASUB == 5:
            return sub_exit([(s_cc, ag_c)])

        out_st = []
        asm_lds = []
        v2p = None
        for g in range(NTL):
            lds = []
            for hg in range(16):
                seg = (hg // 8) * (HPC * 2049) + (hg % 8) * 2049
                for hf in range(2):
                    ldw = [(s_cc, ag_c)]
                    if v2p is not None:
                        ldw.append((s_ve, v2p))
                    lds.append(dma("scalar",
                                   Dd[:, hg, hf * 512:hf * 512 + 128],
                                   d_ctx_pag[seg + hf * 1024 + g * 128:
                                             seg + hf * 1024 + (g + 1) * 128, :],
                                   wait=ldw))
            asm_lds = lds
            xw = [(s_sy, out_st[g - 2])] if g >= 2 else []
            ldx = dma("sync", a_x32[:, g % 2, :], x_in[g * 128:(g + 1) * 128, :],
                      wait=xw)
            scr = Cc[:, 0:2, :].rearrange("p a b -> p (a b)").rearrange(
                "p (a b) -> p a b", b=128)
            v1 = vop(lambda e: e.tensor_tensor(
                out=scr[:], in0=Dd[:, 0:16, 512:640],
                in1=Dd[:, 0:16, 0:128], op=AL.subtract),
                wait=[(s_sc, lds[-1])])
            v2 = vop(lambda e: e.scalar_tensor_tensor(
                out=scr[:], in0=scr[:],
                scalar=c_cc[:, 1:2], in1=Dd[:, 0:16, 0:128],
                op0=AL.mult, op1=AL.add))
            v3 = vop(lambda e, g=g: e.tensor_tensor(
                out=a_x32[:, g % 2, :], in0=a_x32[:, g % 2, :],
                in1=scr.rearrange("p a b -> p (a b)"), op=AL.add),
                wait=[(s_sy, ldx)])
            v2p = v2
            out_st.append(dma("sync", x_out[g * 128:(g + 1) * 128, :],
                              a_x32[:, g % 2, :], wait=[(s_ve, v3)]))
        R["xln_my"] = [(s_sc, xlT_all)]
        R["xba"] = [(s_sc, xgT_all), (s_po, p.val("s_po"))]
        R["ctx"] = [(s_sc, asm_lds[-1])]
        return out_st[-1]

    if STAGE == 1:
        emit_ffn(d_x0, d_out, 1)
    elif STAGE == 2:
        emit_ffn(d_x0, d_x1, 1)
        fence()
        emit_attn(d_x1, d_out)
    else:
        emit_ffn(d_x0, d_x1, 1)
        fence()
        emit_attn(d_x1, d_x2)
        fence()
        emit_ffn(d_x2, d_out, 2)

    p.op("gpsimd", None, wait=[(s_sy, p.val("s_sy")), (s_sc, p.val("s_sc")),
                               (s_po, p.val("s_po"))], inc=None, drain=False)
    p.streams["gpsimd"].append(
        (None, ((s_sy, p.val("s_sy")),), None, 0, False))

    p.build_block()
    p.ctx.close()
    return nc


def kernel(**inputs):
    x = np.ascontiguousarray(np.asarray(inputs["L0_attn_hidden_states"],
                                        dtype=np.float32).reshape(T, D))
    wq = np.asarray(inputs["L0_attn_Wq"], dtype=np.float32)
    wk = np.asarray(inputs["L0_attn_Wk"], dtype=np.float32)
    wv = np.asarray(inputs["L0_attn_Wv"], dtype=np.float32)
    bq = np.asarray(inputs["L0_attn_bq"], dtype=np.float32).reshape(D, 1)
    bk = np.asarray(inputs["L0_attn_bk"], dtype=np.float32).reshape(D, 1)
    bv = np.asarray(inputs["L0_attn_bv"], dtype=np.float32).reshape(1, D)
    rw = np.asarray(inputs["L0_ffn_router_w"], dtype=np.float32)
    ebias = np.asarray(inputs["L0_ffn_e_bias"], dtype=np.float32)
    wg = np.asarray(inputs["L0_ffn_W_gate"], dtype=np.float32)
    wu = np.asarray(inputs["L0_ffn_W_up"], dtype=np.float32)
    wd = np.asarray(inputs["L0_ffn_W_down"], dtype=np.float32)
    sg = np.asarray(inputs["L0_ffn_shared_gate"], dtype=np.float32)
    su = np.asarray(inputs["L0_ffn_shared_up"], dtype=np.float32)
    sd = np.asarray(inputs["L0_ffn_shared_down"], dtype=np.float32)

    sgT = np.ascontiguousarray(sg.T)
    suT = np.ascontiguousarray(su.T)
    sdT = np.ascontiguousarray(sd.T)
    rwT = np.ascontiguousarray(rw.T)

    nc = build_program()
    in_maps = []
    for c in range(8):
        hs = slice((c % 2) * HPC * DH, ((c % 2) + 1) * HPC * DH)
        sel = np.zeros((EPC, E), np.float32)
        for el in range(EPC):
            sel[el, EPC * c + el] = 1.0
        sel = np.ascontiguousarray(
            np.broadcast_to(sel[None], (128, EPC, E)).reshape(128, EPC * E))
        cconst = np.zeros((128, 4), np.float32)
        cconst[:, 0] = EPC * c
        cconst[:, 1] = c % 2
        rmask = np.zeros((8, 1), np.float32)
        rmask[:c, 0] = 1.0
        in_maps.append({
            "x0": x[c * TL:(c + 1) * TL],
            "wq": np.ascontiguousarray(wq[:, hs]),
            "wk": np.ascontiguousarray(wk[:, hs]),
            "wv": np.ascontiguousarray(wv[:, hs]),
            "bq": np.ascontiguousarray(bq[hs]),
            "bk": np.ascontiguousarray(bk[hs]),
            "bv": np.ascontiguousarray(bv[:, hs]),
            "rwT": rwT,
            "ebias": np.ascontiguousarray(
                np.broadcast_to(ebias[None, :], (128, E))),
            "wg": np.ascontiguousarray(wg[EPC * c:EPC * (c + 1)]),
            "wu": np.ascontiguousarray(wu[EPC * c:EPC * (c + 1)]),
            "wd": np.ascontiguousarray(wd[EPC * c:EPC * (c + 1)]),
            "sgT": sgT, "suT": suT, "sdT": sdT,
            "sel": sel, "cconst": cconst, "rmask": rmask,
        })
    res = run_bass_kernel_spmd(nc, in_maps, core_ids=list(range(8)))
    out = np.concatenate([res.results[c]["xout"] for c in range(8)], axis=0)
    return out.reshape(B, S, D)


# version 06 (final)
# speedup vs baseline: 1.0521x; 1.0129x over previous
import numpy as np
from contextlib import ExitStack

import concourse.bass as bass
import concourse.mybir as mybir
from concourse.bass_utils import run_bass_kernel_spmd
from concourse.bass_core_rust import IndirectOffsetOnAxis

F32 = mybir.dt.float32
BF16 = mybir.dt.bfloat16
F16 = mybir.dt.float16
I32 = mybir.dt.int32
AL = mybir.AluOpType
AF = mybir.ActivationFunctionType
AX = mybir.AxisListType

B, S, D = 4, 2048, 2048
T = B * S
TL = 1024
NTL = 8
NGT = 64
E, EPC, K = 32, 4, 4
I_FF, ISH = 1024, 2048
CAP, CPAD = 320, 384
SEG = EPC * CPAD
DUMMY_ROW = 8 * SEG
RES_ROWS = 8 * SEG + 128
NG, TKG = 8, 3
HPC, DH = 8, 128
NB = 16
EPS = 1e-5
ISQ = 0.08838834764831845
NEGBIG = -1.0e9

STAGE = 3
ASUB = 99


class Prog:
    ENGINES = ("sync", "scalar", "vector", "tensor", "gpsimd")

    def __init__(self, nc):
        self.nc = nc
        self.streams = {e: [] for e in self.ENGINES}
        self.ctx = ExitStack()
        self.sems = {}
        self.bnd = {}

    def sem(self, name):
        h = self.ctx.enter_context(self.nc.semaphore(name))
        self.sems[name] = [h, 0]
        return name

    def val(self, name):
        return self.sems[name][1]

    DMA_SEMS = {"s_sy": "sync", "s_sc": "scalar", "s_po": "gpsimd"}

    def op(self, engine, fn, wait=(), inc=None, inc_n=1, drain=False):
        nw = []
        for (n, v) in wait:
            if n in self.DMA_SEMS and v > 0:
                tot = self.sems[n][1]
                if tot not in self.bnd.setdefault(n, {0}):
                    qeng = self.DMA_SEMS[n]
                    h = self.sems[n][0]
                    self.streams[qeng].append(
                        (None, ((n, tot),), None, 0, False))
                    self.bnd[n].add(tot)
                nw.append((n, tot))
            else:
                nw.append((n, v))
        wait = nw
        if engine in ("sync", "scalar", "gpsimd") and wait:
            for sname, qeng in self.DMA_SEMS.items():
                if qeng == engine:
                    self.bnd.setdefault(sname, {0}).add(self.sems[sname][1])
        after = None
        if inc is not None:
            self.sems[inc][1] += inc_n
            after = self.sems[inc][1]
        self.streams[engine].append((fn, tuple(wait), inc, inc_n, drain))
        return after

    def build_block(self):
        blk = self.ctx.enter_context(self.nc.Block())
        for eng_name in self.ENGINES:
            stream = self.streams[eng_name]

            def make(stream):
                def emit(eng):
                    for fn, waits, inc, inc_n, drain in stream:
                        for sname, v in waits:
                            if v > 0:
                                eng.wait_ge(self.sems[sname][0], v)
                        inst = fn(eng) if fn is not None else None
                        if drain or (inc is not None and inst is None):
                            inst = eng.drain()
                        if inc is not None:
                            inst.then_inc(self.sems[inc][0], inc_n)
                return emit

            getattr(blk, eng_name)(make(stream))


def build_program():
    nc = bass.Bass("TRN2", target_bir_lowering=False)
    p = Prog(nc)
    EN = p.ctx.enter_context

    d_x0 = nc.dram_tensor("x0", [TL, D], F32, kind="ExternalInput").ap()
    d_wq = nc.dram_tensor("wq", [D, HPC * DH], F32, kind="ExternalInput").ap()
    d_wk = nc.dram_tensor("wk", [D, HPC * DH], F32, kind="ExternalInput").ap()
    d_wv = nc.dram_tensor("wv", [D, HPC * DH], F32, kind="ExternalInput").ap()
    d_bq = nc.dram_tensor("bq", [HPC * DH, 1], F32, kind="ExternalInput").ap()
    d_bk = nc.dram_tensor("bk", [HPC * DH, 1], F32, kind="ExternalInput").ap()
    d_bv = nc.dram_tensor("bv", [1, HPC * DH], F32, kind="ExternalInput").ap()
    d_rwT = nc.dram_tensor("rwT", [D, E], F32, kind="ExternalInput").ap()
    d_ebias = nc.dram_tensor("ebias", [128, E], F32, kind="ExternalInput").ap()
    d_wg = nc.dram_tensor("wg", [EPC, D, I_FF], F32, kind="ExternalInput").ap()
    d_wu = nc.dram_tensor("wu", [EPC, D, I_FF], F32, kind="ExternalInput").ap()
    d_wd = nc.dram_tensor("wd", [EPC, I_FF, D], F32, kind="ExternalInput").ap()
    d_sgT = nc.dram_tensor("sgT", [D, ISH], F32, kind="ExternalInput").ap()
    d_suT = nc.dram_tensor("suT", [D, ISH], F32, kind="ExternalInput").ap()
    d_sdT = nc.dram_tensor("sdT", [ISH, D], F32, kind="ExternalInput").ap()
    d_sel = nc.dram_tensor("sel", [128, EPC * E], F32, kind="ExternalInput").ap()
    d_cconst = nc.dram_tensor("cconst", [128, 4], F32, kind="ExternalInput").ap()
    d_rmask = nc.dram_tensor("rmask", [8, 1], F32, kind="ExternalInput").ap()
    d_out = nc.dram_tensor("xout", [TL, D], F32, kind="ExternalOutput").ap()

    d_xln_my = nc.dram_tensor("xln_my", [TL, D], BF16, kind="Internal").ap()
    d_xln_ag = nc.dram_tensor("xln_ag", [T, D], BF16, kind="Internal").ap()
    d_xba = nc.dram_tensor("xba", [S, D], BF16, kind="Internal").ap()
    d_scT_my = nc.dram_tensor("scT_my", [E, TL], F32, kind="Internal").ap()
    d_scT_ag = nc.dram_tensor("scT_ag", [8 * E, TL], F32, kind="Internal").ap()
    d_cnt_my = nc.dram_tensor("cnt_my", [1, E], F32, kind="Internal").ap()
    d_cnt_ag = nc.dram_tensor("cnt_ag", [8, E], F32, kind="Internal").ap()
    d_res_my = nc.dram_tensor("res_my", [SEG, D], BF16, kind="Internal").ap()
    d_res_ag = nc.dram_tensor("res_ag", [RES_ROWS, D], BF16, kind="Internal").ap()
    d_xp = nc.dram_tensor("xp_rows", [EPC, CPAD, D], BF16, kind="Internal").ap()
    d_shared = nc.dram_tensor("shared_out", [TL, D], BF16, kind="Internal").ap()
    d_x1 = nc.dram_tensor("x1", [TL, D], F32, kind="Internal").ap()
    d_x2 = nc.dram_tensor("x2", [TL, D], F32, kind="Internal").ap()
    d_qk = nc.dram_tensor("qk_spill", [2 * HPC * DH, S], BF16, kind="Internal").ap()
    d_ctx_my = nc.dram_tensor("ctx_my", [HPC * 2049, DH], BF16, kind="Internal").ap()
    d_ctx_pag = nc.dram_tensor("ctx_pag", [2 * HPC * 2049, DH], BF16,
                               kind="Internal").ap()
    d_xg = nc.dram_tensor("xg_rows", [HPC * 128, D], BF16, kind="Internal").ap()

    c_i384 = EN(nc.sbuf_tensor("c_i384", [128, CPAD], F16))
    c_i32 = EN(nc.sbuf_tensor("c_i32", [128, E], F32))
    c_i32m = EN(nc.sbuf_tensor("c_i32m", [128, E], F32))
    c_i8 = EN(nc.sbuf_tensor("c_i8", [128, NG], F32))
    c_i8m = EN(nc.sbuf_tensor("c_i8m", [128, NG], F32))
    c_i64 = EN(nc.sbuf_tensor("c_i64", [128, 64], F32))
    c_ut = EN(nc.sbuf_tensor("c_ut", [128, 128], F32))
    c_id16 = EN(nc.sbuf_tensor("c_id16", [128, 128], BF16))
    c_id32 = EN(nc.sbuf_tensor("c_id32", [128, 128], F32))
    c_ones_col = EN(nc.sbuf_tensor("c_ones_col", [128, 1], F32))
    c_ones_row = EN(nc.sbuf_tensor("c_ones_row", [1, 128], F32))
    c_ones16 = EN(nc.sbuf_tensor("c_ones16", [1, 128], BF16))
    c_ebias = EN(nc.sbuf_tensor("c_ebias", [128, E], F32))
    c_gatok = EN(nc.sbuf_tensor("c_gatok", [128, 16, 3], F32))
    c_bqkv = EN(nc.sbuf_tensor("c_bqkv", [128, 2 * HPC], F32))
    c_bv16 = EN(nc.sbuf_tensor("c_bv16", [1, HPC * DH], BF16))
    c_sel = EN(nc.sbuf_tensor("c_sel", [128, EPC, E], F32))
    c_cc = EN(nc.sbuf_tensor("c_cc", [128, 4], F32))
    c_rmask = EN(nc.sbuf_tensor("c_rmask", [8, 1], F32))
    c_cnt = EN(nc.sbuf_tensor("c_cnt", [8, E], F32))

    A = EN(nc.sbuf_tensor("A", [128, 16, 2048], BF16))
    Bf = EN(nc.sbuf_tensor("Bf", [128, 12, 1024], F32))
    Cc = EN(nc.sbuf_tensor("Cc", [128, 16, 1024], BF16))
    Dd = EN(nc.sbuf_tensor("Dd", [128, 16, 1024], BF16))
    a_oh = EN(nc.sbuf_tensor("a_oh", [128, 2, CPAD], F16))
    a_lhsT = EN(nc.sbuf_tensor("a_lhsT", [128, NGT, 3], F16))
    a_stat = EN(nc.sbuf_tensor("a_stat", [128, 16], F32))
    a_slotT = EN(nc.sbuf_tensor("a_slotT", [128, 12, 3], F32))
    a_wcol = EN(nc.sbuf_tensor("a_wcol", [128, 12], F32))
    a_idx = EN(nc.sbuf_tensor("a_idx", [128, 16], I32))
    a_carry = EN(nc.sbuf_tensor("a_carry", [1, E], F32))
    a_carry2 = EN(nc.sbuf_tensor("a_carry2", [1, E], F32))
    a_crow = EN(nc.sbuf_tensor("a_crow", [E, 1], F32))
    a_ek = EN(nc.sbuf_tensor("a_ek", [128, K, NGT], F32))
    a_wk = EN(nc.sbuf_tensor("a_wk", [128, K, NGT], F32))
    a_ws = EN(nc.sbuf_tensor("a_ws", [128, NGT], F32))
    a_sgmy = EN(nc.sbuf_tensor("a_sgmy", [128, NTL, K], I32))
    a_gidx = EN(nc.sbuf_tensor("a_gidx", [64, 8], F32))
    a_gi32 = EN(nc.sbuf_tensor("a_gi32", [128, 2], I32))
    a_sci = EN(nc.sbuf_tensor("a_sci", [128, HPC], I32))
    a_x32 = EN(nc.sbuf_tensor("a_x32", [128, 2, 2048], F32))
    a_qgT = EN(nc.sbuf_tensor("a_qgT", [128, 64], BF16))
    a_gsum = EN(nc.sbuf_tensor("a_gsum", [64, 8], F32))
    a_gctx = EN(nc.sbuf_tensor("a_gctx", [64, DH], BF16))
    a_y = EN(nc.sbuf_tensor("a_y", [128, 2, 512], BF16))

    ps = [EN(nc.psum_tensor(f"ps{i}", [128, 512], F32)) for i in range(8)]

    Bff = Bf.ap().rearrange("p a b -> p (a b)")

    def bfv(lo, hi):
        return Bff[:, lo * 1024:hi * 1024]

    Af = A.ap().rearrange("p a b -> p (a b)")
    Df = Dd.ap().rearrange("p a b -> p (a b)")
    Cf = Cc.ap().rearrange("p a b -> p (a b)")

    v_xlnT = Af[:, 0:16 * 1024].rearrange("p (a b) -> p a b", b=1024)
    v_shG = Af[:, 16 * 1024:24 * 1024].rearrange("p (a b) -> p a b", b=512)
    v_shU = Af[:, 24 * 1024:32 * 1024].rearrange("p (a b) -> p a b", b=512)
    v_shY = Af[:, 24 * 1024:32 * 1024].rearrange("p (a b) -> p a b", b=2048)
    f_scores = bfv(0, 2).rearrange("p (g e) -> p g e", e=E)
    a_sslot = bfv(0, 2)[0:3, 0:EPC * CPAD]
    a_lM = bfv(0, 1)[:, 0:256].rearrange("p (g e) -> p g e", e=E)
    a_lP = bfv(0, 1)[:, 256:512].rearrange("p (g e) -> p g e", e=E)
    a_lek = bfv(0, 1)[:, 512:544].rearrange("p (k g) -> p k g", g=NTL)
    a_lrk = bfv(0, 1)[:, 544:576].rearrange("p (k g) -> p k g", g=NTL)
    a_lws = bfv(0, 1)[:, 576:584]
    f_sc = bfv(2, 4).rearrange("p (g e) -> p g e", e=E)
    f_M = bfv(4, 6).rearrange("p (g e) -> p g e", e=E)
    a_scTb = bfv(4, 6)[0:E, 0:TL]
    f_Ww = bfv(6, 8).rearrange("p (g e) -> p g e", e=E)
    f_P = bfv(8, 10).rearrange("p (g e) -> p g e", e=E)
    f_oh = bfv(10, 12).rearrange("p (g e) -> p g e", e=E)
    f_scT3 = Bf[0:E, 4:12, :]
    f_sq = bfv(2, 4)
    f_xstage = bfv(6, 8)
    f_scmy = bfv(10, 11)[:, 0:256].rearrange("p (a b) -> p a b", b=E)
    f_g2 = bfv(10, 11)[:, 512:1024].rearrange("p (g n) -> p g n", n=NG)
    f_gm = bfv(11, 12)[:, 0:512].rearrange("p (g n) -> p g n", n=NG)
    f_t2 = bfv(11, 12)[:, 512:1024].rearrange("p (g n) -> p g n", n=NG)
    l_oh = bfv(10, 11)[:, 256:512].rearrange("p (g e) -> p g e", e=E)
    l_g2 = bfv(11, 12)[:, 0:64].rearrange("p (g n) -> p g n", n=NG)
    l_t2 = bfv(11, 12)[:, 64:128].rearrange("p (g n) -> p g n", n=NG)
    l_gm = bfv(11, 12)[:, 128:192].rearrange("p (g n) -> p g n", n=NG)
    l_sc = bfv(11, 12)[:, 768:1024].rearrange("p (g n) -> p g n", n=E) \
        if False else bfv(2, 3)[:, 0:256].rearrange("p (g e) -> p g e", e=E)

    v_xpT = Df[:, 0:6 * 1024].rearrange("p (a b) -> p a b", b=CPAD)
    v_eg = Df[:, 6 * 1024:9 * 1024].rearrange("p (a b) -> p a b", b=CPAD)[:, 0:8, :]
    v_eu = Df[:, 9 * 1024:12 * 1024].rearrange("p (a b) -> p a b", b=CPAD)[:, 0:8, :]
    v_gb = Df[:, 12 * 1024:16 * 1024].rearrange("p (a b) -> p a b", b=2048)
    v_comb = Df[:, 0:8 * 1024].rearrange("p (a b) -> p a b", b=2048)

    t_xgT = Af[:, 0:16 * 1024].rearrange("p (a b) -> p a b", b=1024)
    t_q0 = Df[:, 0:2048]
    t_k0 = Df[:, 2048:4096]
    t_q1 = Df[:, 4096:6144]
    t_k1 = Df[:, 6144:8192]
    t_whd = Df[:, 8192:10240].rearrange("p (a b) -> p a b", b=128)
    t_whdB = Df[:, 10240:12288].rearrange("p (a b) -> p a b", b=128)
    t_pr = (Df[:, 14336:14720], Df[:, 15104:15488])
    t_prT = (Df[:, 14720:15104], Df[:, 15488:15872])
    t_gpT = Df[:, 10752:11776].rearrange("p (a b) -> p a b", b=64)
    t_ctxs = Df[:, 11776:12288].rearrange("p (a b) -> p a b", b=128)
    t_gq = Df[0:64, 12288:14336]
    t_asm = Df[:, 0:16 * 1024].rearrange("p (a b) -> p a b", b=1024)
    n_norm = bfv(0, 2)[0:8, :]
    n_work = bfv(2, 4)[0:8, :]
    n_r = bfv(4, 6)[0:8, :]
    n_m = bfv(6, 8)[0:8, :]
    n_p0 = bfv(8, 10)[0:1, 0:2048]
    a_rcm = bfv(8, 10)[:, 512:768].rearrange("p (g t n) -> p g t n", t=2, n=8)
    a_wtf = bfv(8, 10)[:, 768:832]
    a_pmy = bfv(10, 11)[:, 0:256].rearrange("p (g l) -> p g l", l=EPC)
    a_wmy = bfv(10, 11)[:, 256:512].rearrange("p (g l) -> p g l", l=EPC)
    c_rwT = bfv(8, 10)[:, 1024:1536].rearrange("p (a b) -> p a b", b=E)
    n_sq = bfv(10, 12)

    s_sy = p.sem("s_sy")
    s_sc = p.sem("s_sc")
    s_po = p.sem("s_po")
    s_te = p.sem("s_te")
    s_ve = p.sem("s_ve")
    s_ac = p.sem("s_ac")
    s_ge = p.sem("s_ge")
    s_cc = p.sem("s_cc")

    def dma(queue, out, in_, wait=(), slow=False):
        inc = {"sync": s_sy, "scalar": s_sc, "gpsimd": s_po}[queue]
        return p.op(queue, lambda e, o=out, i=in_: e.dma_start(
            out=o, in_=i, allow_slow_non_contiguous=slow),
            wait=wait, inc=inc, inc_n=16)

    def dmaT(queue, out, in_, wait=()):
        inc = {"sync": s_sy, "scalar": s_sc}[queue]
        return p.op(queue, lambda e, o=out, i=in_: e.dma_start_transpose(o, i),
                    wait=wait, inc=inc, inc_n=16)

    def vop(fn, wait=(), drain=True):
        return p.op("vector", fn, wait=wait, inc=s_ve, drain=drain)

    def sop(fn, wait=(), drain=True):
        return p.op("scalar", fn, wait=wait, inc=s_ac, drain=drain)

    def top(fn, wait=()):
        return p.op("tensor", fn, wait=wait, inc=s_te)

    def gind(out, in_, idx_ap, wait=()):
        return p.op("gpsimd", lambda e, o=out, i=in_, x=idx_ap: e.indirect_dma_start(
            out=o, out_offset=None, in_=i, in_offset=IndirectOffsetOnAxis(x, 0)),
            wait=wait, inc=s_po, inc_n=16)

    def gscat(out, idx_ap, in_, wait=()):
        return p.op("gpsimd", lambda e, o=out, i=in_, x=idx_ap: e.indirect_dma_start(
            out=o, out_offset=IndirectOffsetOnAxis(x, 0), in_=i, in_offset=None),
            wait=wait, inc=s_po, inc_n=16)

    def coll(ins, outs, groups, wait=()):
        return p.op("gpsimd", lambda e, i=ins, o=outs, g=groups: e.collective_compute(
            "AllGather", AL.bypass, ins=[i], outs=[o], replica_groups=g),
            wait=wait, inc=s_cc)

    G8 = [list(range(8))]
    GP = [[0, 1], [2, 3], [4, 5], [6, 7]]

    def fence():
        vals = [(n, p.sems[n][1]) for n in
                ("s_sy", "s_sc", "s_po", "s_te", "s_ve", "s_ac", "s_ge", "s_cc")]
        for eng in Prog.ENGINES:
            p.op(eng, None, wait=vals, inc=None, drain=False)

    def bc1(ap2d, nrep, m):
        return ap2d.unsqueeze(1).to_broadcast([128, nrep, m])

    def bc2(ap2d, m):
        return ap2d.unsqueeze(2).to_broadcast([128, ap2d.shape[1], m])

    def iota(dst, pattern, base, chm):
        return lambda e, d=dst, pt=pattern, b=base, c=chm: e.iota(
            d, pattern=pt, base=b, channel_multiplier=c,
            allow_small_or_imprecise_dtypes=True)

    for args in ((c_i384[:], [[1, CPAD]], 0, 0), (c_i32[:], [[1, E]], 0, 0),
                 (c_i32m[:], [[1, E]], -32, 0), (c_i8[:], [[1, NG]], 0, 0),
                 (c_i8m[:], [[1, NG]], -8, 0), (c_i64[:], [[1, 64]], 0, 0),
                 (c_ut[:], [[1, 128]], 0, -1), (c_id32[:], [[-1, 128]], 0, 1),
                 (a_lhsT[:, :, 0:1], [[0, NGT]], 0, 1),
                 (a_lhsT[:, :, 1:2], [[1, NGT]], 0, 0),
                 (c_gatok[:, :, 0:1], [[0, 16]], 0, 1),
                 (c_gatok[:, :, 1:2], [[1, 16]], 0, 0)):
        p.op("gpsimd", iota(*args), inc=None)
    g_iotas = p.op("gpsimd", lambda e: None, inc=s_ge, drain=True)

    vop(lambda e: e.memset(c_ones_col[:], 1.0))
    vop(lambda e: e.memset(a_stat[:, 15:16], float(EPS)))
    vop(lambda e: e.memset(c_ones_row[:], 1.0))
    vop(lambda e: e.memset(c_ones16[:], 1.0))
    vop(lambda e: e.memset(c_gatok[:, :, 2:3], 1.0), wait=[(s_ge, g_iotas)])
    vop(lambda e: e.tensor_scalar(out=c_ut[:], in0=c_ut[:], scalar1=0.0,
                                  scalar2=None, op0=AL.is_gt))
    vop(lambda e: e.tensor_scalar(out=c_id32[:], in0=c_id32[:], scalar1=0.0,
                                  scalar2=None, op0=AL.is_equal))
    v_const = vop(lambda e: e.tensor_copy(out=c_id16[:], in_=c_id32[:]))

    dma("sync", c_ebias[:], d_ebias[:])
    dma("sync", c_sel.ap().rearrange("p a b -> p (a b)"), d_sel[:])
    dma("sync", c_cc[:], d_cconst[:])
    dma("sync", c_rmask[:], d_rmask[:])
    dma("sync", c_bqkv[:, 0:HPC].rearrange("p (a o) -> p a o", o=1),
        d_bq.rearrange("(a p) o -> p a o", p=128), slow=True)
    dma("sync", c_bqkv[:, HPC:2 * HPC].rearrange("p (a o) -> p a o", o=1),
        d_bk.rearrange("(a p) o -> p a o", p=128), slow=True)
    ld_bv = dma("gpsimd", c_bv16[:], d_bv[:])
    v_z = vop(lambda e: e.memset(v_gb[:, 0, :], 0.0))
    dma("sync", d_res_ag[DUMMY_ROW:DUMMY_ROW + 128, :], v_gb[:, 0, :],
        wait=[(s_ve, v_z)])
    pad_st = dma("sync", d_ctx_my.rearrange("(h r) d -> h r d", r=2049)[:, 2048, :],
                 v_gb[0:8, 0, 0:DH])

    R = {"xln_my": [], "xln_ag": [], "scT": [], "res": [], "cnt": [],
         "xba": [], "ctx": []}

    def emit_ln_tile(x_src_rows, xb, xlnb, sq_buf, extra_f32_out=None,
                     ld_wait=(), out_wait=()):
        mu, ssq, var, rstd, negb = (a_stat[:, i:i + 1] for i in range(5))
        ld = dma("sync", xb, x_src_rows, wait=list(ld_wait))
        a1 = sop(lambda e: e.activation(out=sq_buf, in_=xb, func=AF.Square,
                                        accum_out=ssq), wait=[(s_sy, ld)])
        vop(lambda e: e.reduce_sum(out=mu, in_=xb, axis=AX.X), wait=[(s_sy, ld)])
        vop(lambda e: e.tensor_scalar_mul(mu, mu, 1.0 / D))
        vop(lambda e: e.tensor_scalar_mul(var, ssq, 1.0 / D), wait=[(s_ac, a1)])
        vop(lambda e: e.tensor_mul(ssq, mu, mu))
        v1 = vop(lambda e: e.tensor_sub(var, var, ssq))
        a2 = sop(lambda e: e.activation(out=rstd, in_=var, func=AF.Sqrt,
                                        bias=a_stat[:, 15:16]),
                 wait=[(s_ve, v1)])
        v2 = vop(lambda e: e.reciprocal(rstd, rstd), wait=[(s_ac, a2)])
        v3 = vop(lambda e: e.scalar_tensor_tensor(
            out=negb, in0=mu, scalar=-1.0, in1=rstd, op0=AL.mult, op1=AL.mult),
            wait=[(s_ve, v2)])
        a3 = sop(lambda e: e.activation(out=xlnb, in_=xb, func=AF.Identity,
                                        bias=negb, scale=rstd),
                 wait=[(s_ve, v3)] + list(out_wait))
        a4 = None
        if extra_f32_out is not None:
            a4 = sop(lambda e: e.activation(out=extra_f32_out, in_=xb,
                                            func=AF.Identity, bias=negb,
                                            scale=rstd), wait=[(s_ve, v3)])
        return a3, a4

    def emit_router(nt, scores_v, sc_v, oh_v, g2_v, gm_v, t2_v, M_v, ek_t,
                    wk_t, ws_t, want_w, wait0):
        vop(lambda e: e.tensor_tensor(out=sc_v[:], in0=scores_v[:],
                                      in1=bc1(c_ebias[:], nt, E), op=AL.add),
            wait=wait0)
        sc4 = sc_v.rearrange("p g (n f) -> p g n f", f=4)
        first = True
        for (i, j) in ((0, 1), (0, 2), (0, 3), (1, 2), (1, 3), (2, 3)):
            vop(lambda e, i=i, j=j: e.tensor_tensor(
                out=t2_v[:], in0=sc4[:, :, :, i], in1=sc4[:, :, :, j], op=AL.add))
            if first:
                vop(lambda e: e.tensor_copy(out=g2_v[:], in_=t2_v[:]))
                first = False
            else:
                vop(lambda e: e.tensor_tensor(out=g2_v[:], in0=g2_v[:], in1=t2_v[:],
                                              op=AL.max))
        vop(lambda e: e.memset(gm_v[:], 0.0))
        ws_g = ws_t[:, 0:nt]
        for r in range(TKG):
            vop(lambda e: e.reduce_max(out=ws_g, in_=g2_v[:], axis=AX.X))
            vop(lambda e: e.tensor_tensor(out=t2_v[:], in0=g2_v[:],
                                          in1=bc2(ws_g, NG), op=AL.is_equal))
            vop(lambda e: e.tensor_tensor(out=t2_v[:], in0=t2_v[:],
                                          in1=bc1(c_i8m[:], nt, NG), op=AL.mult))
            vop(lambda e: e.tensor_reduce(out=ws_g, in_=t2_v[:], axis=AX.X,
                                          op=AL.min))
            vop(lambda e: e.tensor_scalar_add(ws_g, ws_g, 8.0))
            vop(lambda e: e.tensor_tensor(out=t2_v[:], in0=bc1(c_i8[:], nt, NG),
                                          in1=bc2(ws_g, NG), op=AL.is_equal))
            vop(lambda e: e.scalar_tensor_tensor(out=g2_v[:], in0=t2_v[:],
                                                 scalar=NEGBIG, in1=g2_v[:],
                                                 op0=AL.mult, op1=AL.add))
            vop(lambda e: e.tensor_tensor(out=gm_v[:], in0=gm_v[:], in1=t2_v[:],
                                          op=AL.add))
        vop(lambda e: e.tensor_tensor(
            out=sc4[:], in0=sc4[:],
            in1=gm_v.unsqueeze(3).to_broadcast([128, nt, NG, 4]), op=AL.mult))
        vop(lambda e: e.memset(M_v[:], 0.0))
        for k in range(K):
            vop(lambda e: e.reduce_max(out=ws_g, in_=sc_v[:], axis=AX.X))
            vop(lambda e: e.tensor_tensor(out=oh_v[:], in0=sc_v[:],
                                          in1=bc2(ws_g, E), op=AL.is_equal))
            vop(lambda e: e.tensor_tensor(out=oh_v[:], in0=oh_v[:],
                                          in1=bc1(c_i32m[:], nt, E), op=AL.mult))
            vop(lambda e, k=k: e.tensor_reduce(out=ek_t[:, k, 0:nt], in_=oh_v[:],
                                               axis=AX.X, op=AL.min))
            vop(lambda e, k=k: e.tensor_scalar_add(ek_t[:, k, 0:nt],
                                                   ek_t[:, k, 0:nt], 32.0))
            vop(lambda e, k=k: e.tensor_tensor(
                out=oh_v[:], in0=bc1(c_i32[:], nt, E),
                in1=bc2(ek_t[:, k, 0:nt], E), op=AL.is_equal))
            vop(lambda e: e.scalar_tensor_tensor(out=sc_v[:], in0=oh_v[:],
                                                 scalar=NEGBIG, in1=sc_v[:],
                                                 op0=AL.mult, op1=AL.add))
            vop(lambda e: e.tensor_tensor(out=M_v[:], in0=M_v[:], in1=oh_v[:],
                                          op=AL.add))
            if want_w:
                vop(lambda e: e.tensor_tensor(out=oh_v[:], in0=oh_v[:],
                                              in1=scores_v[:], op=AL.mult))
                vop(lambda e, k=k: e.reduce_sum(out=wk_t[:, k, 0:nt], in_=oh_v[:],
                                                axis=AX.X))
        if want_w:
            vop(lambda e: e.tensor_add(ws_g, wk_t[:, 0, 0:nt], wk_t[:, 1, 0:nt]))
            vop(lambda e: e.tensor_add(ws_g, ws_g, wk_t[:, 2, 0:nt]))
            vop(lambda e: e.tensor_add(ws_g, ws_g, wk_t[:, 3, 0:nt]))
            vop(lambda e: e.tensor_scalar_add(ws_g, ws_g, 1.0e-20))
            vop(lambda e: e.reciprocal(ws_g, ws_g))
            vop(lambda e: e.tensor_scalar_mul(ws_g, ws_g, 2.5))
            for k in range(K):
                vop(lambda e, k=k: e.tensor_tensor(out=wk_t[:, k, 0:nt],
                                                   in0=wk_t[:, k, 0:nt],
                                                   in1=ws_g, op=AL.mult))
        return vop(lambda e: None)

    def emit_ffn(x_in, x_out, call):
        ld_rw = dma("sync", c_rwT[:], d_rwT.rearrange("(a p) b -> p a b", p=128))
        sc_ev = []
        ln_st = []
        a4p = {}
        for g in range(NTL):
            xb = a_x32[:, g % 2, :]
            xlnb = v_gb[:, g % 2, :]
            ldw = [(s_ac, a4p[g - 2])] if g >= 2 else []
            outw = [(s_sy, ln_st[g - 2])] if g >= 2 else []
            a3, a4 = emit_ln_tile(x_in[g * 128:(g + 1) * 128, :], xb, xlnb,
                                  f_sq[:], f_xstage[:], ld_wait=ldw,
                                  out_wait=outw)
            a4p[g] = a4
            wmy = R["xln_my"] if g == 0 else []
            ln_st.append(dma("sync", d_xln_my[g * 128:(g + 1) * 128, :], xlnb,
                             wait=[(s_ac, a3)] + wmy))
            evs = []
            for dmt in range(16):
                t1 = top(lambda e, dmt=dmt: e.transpose(
                    ps[1][:, 0:128], f_xstage[:, dmt * 128:(dmt + 1) * 128],
                    c_id32[:]),
                    wait=[(s_ac, a4), (s_ve, v_const)] +
                    ([(s_ve, evs[-1])] if evs else []))
                evs.append(vop(lambda e, dmt=dmt: e.tensor_copy(
                    out=f_sq[:, dmt * 128:(dmt + 1) * 128], in_=ps[1][:, 0:128]),
                    wait=[(s_te, t1)]))
            xtT = f_sq.rearrange("p (a b) -> p a b", b=128)

            def mmsc(e, xtT=xtT):
                for dmt in range(16):
                    e.matmul(ps[0][:, 0:E], xtT[:, dmt, :], c_rwT[:, dmt, :],
                             start=(dmt == 0), stop=(dmt == 15))
            t2 = top(mmsc, wait=[(s_ve, evs[-1]), (s_sy, ld_rw)] +
                     ([(s_ac, sc_ev[-1])] if sc_ev else []))
            sc_ev.append(sop(lambda e, g=g: e.activation(
                out=f_scmy[:, g, :], in_=ps[0][:, 0:E], func=AF.Sigmoid),
                wait=[(s_te, t2)]))
        ln_all = ln_st[-1]
        sc_all = sc_ev[-1]

        ag_x = coll(d_xln_my[:], d_xln_ag[:], G8,
                    wait=[(s_sy, ln_all)] + R["xln_ag"])

        trs = []
        for ts in range(NTL):
            t1 = top(lambda e, ts=ts: e.transpose(
                ps[1][0:E, 0:128], f_scmy[:, ts, :], c_id32[:]),
                wait=[(s_ac, sc_ev[ts])] + ([(s_ve, trs[-1])] if trs else []))
            trs.append(vop(lambda e, ts=ts: e.tensor_copy(
                out=a_scTb[0:E, ts * 128:(ts + 1) * 128], in_=ps[1][0:E, 0:128]),
                wait=[(s_te, t1)]))
        st_sc = dma("sync", d_scT_my[:], a_scTb[:],
                    wait=[(s_ve, trs[-1])] + R["scT"])
        ag_s = coll(d_scT_my[:], d_scT_ag[:], G8, wait=[(s_sy, st_sc)])

        xlT = [dmaT("scalar", v_xlnT[:, dmt, :],
                    d_xln_my[0:TL, dmt * 128:(dmt + 1) * 128],
                    wait=[(s_sy, ln_all)]) for dmt in range(16)]
        xlT_all = xlT[-1]

        v_loc = emit_router(NTL, f_scmy, l_sc, l_oh, l_g2, l_gm, l_t2, a_lM,
                            a_lek, a_lrk, a_lws, False, [(s_ac, sc_all)])
        def mmcnt(e):
            for g in range(NTL):
                e.matmul(ps[2][0:1, 0:E], c_ones_col[:], a_lM[:, g, :],
                         start=(g == 0), stop=(g == NTL - 1),
                         skip_group_check=True)
        t_cnt = top(mmcnt, wait=[(s_ve, v_loc)])
        v_cr = vop(lambda e: e.tensor_copy(out=a_carry[:], in_=ps[2][0:1, 0:E]),
                   wait=[(s_te, t_cnt)])
        st_cnt = dma("sync", d_cnt_my[:], a_carry[:],
                     wait=[(s_ve, v_cr)] + R["cnt"])
        ag_cnt = coll(d_cnt_my[:], d_cnt_ag[:], G8, wait=[(s_sy, st_cnt)])
        ld_cnt = dma("sync", c_cnt[:], d_cnt_ag[:], wait=[(s_cc, ag_cnt)])
        t_cb = top(lambda e: e.matmul(ps[2][0:1, 0:E], c_rmask[:], c_cnt[:],
                                      start=True, stop=True),
                   wait=[(s_sy, ld_cnt)])
        v_cba = vop(lambda e: e.tensor_copy(out=a_carry2[:], in_=ps[2][0:1, 0:E]),
                   wait=[(s_te, t_cb)])
        lp_ev = []
        ev_c = v_cba
        for g in range(NTL):
            pb = ps[3] if (g % 2 == 0) else ps[4]
            w1 = [(s_ve, ev_c)]
            if g >= 2:
                w1.append((s_ac, lp_ev[g - 2]))
            top(lambda e, g=g, pb=pb: e.matmul(
                pb[:, 0:E], c_ut[:], a_lM[:, g, :], start=True, stop=False,
                skip_group_check=True), wait=w1)
            t2 = top(lambda e, pb=pb: e.matmul(
                pb[:, 0:E], c_ones_row[:], a_carry2[:], start=False, stop=True,
                skip_group_check=True))
            t3 = top(lambda e, g=g: e.matmul(
                ps[2][0:1, 0:E], c_ones_col[:], a_lM[:, g, :],
                start=True, stop=True, skip_group_check=True))
            lp_ev.append(sop(lambda e, g=g, pb=pb: e.copy(
                out=a_lP[:, g, :], in_=pb[:, 0:E]), wait=[(s_te, t2)]))
            ev_c = vop(lambda e: e.tensor_add(a_carry2[:], a_carry2[:],
                                              ps[2][0:1, 0:E]),
                       wait=[(s_te, t3)])
        lp_all = lp_ev[-1]
        for k in range(K):
            vop(lambda e, k=k: e.tensor_tensor(
                out=l_oh[:], in0=bc1(c_i32[:], NTL, E),
                in1=bc2(a_lek[:, k, 0:NTL], E), op=AL.is_equal),
                wait=[(s_ac, lp_all)] if k == 0 else ())
            vop(lambda e: e.tensor_tensor(out=l_oh[:], in0=l_oh[:], in1=a_lP[:],
                                          op=AL.mult))
            vop(lambda e, k=k: e.reduce_sum(out=a_lrk[:, k, :], in_=l_oh[:],
                                            axis=AX.X))
            vop(lambda e, k=k: e.scalar_tensor_tensor(
                out=a_lws[:, 0:NTL], in0=a_lek[:, k, 0:NTL], scalar=float(CPAD),
                in1=a_lrk[:, k, :], op0=AL.mult, op1=AL.add))
            vop(lambda e, k=k: e.tensor_scalar(
                out=l_t2.rearrange("p a b -> p (a b)")[:, 0:NTL],
                in0=a_lrk[:, k, :], scalar1=float(CAP), scalar2=None,
                op0=AL.is_lt))
            vop(lambda e: e.tensor_tensor(
                out=a_lws[:, 0:NTL], in0=a_lws[:, 0:NTL],
                in1=l_t2.rearrange("p a b -> p (a b)")[:, 0:NTL], op=AL.mult))
            vop(lambda e: e.tensor_scalar(
                out=l_t2.rearrange("p a b -> p (a b)")[:, 0:NTL],
                in0=l_t2.rearrange("p a b -> p (a b)")[:, 0:NTL],
                scalar1=1.0, scalar2=float(-DUMMY_ROW),
                op0=AL.subtract, op1=AL.mult))
            vop(lambda e: e.tensor_tensor(
                out=a_lws[:, 0:NTL], in0=a_lws[:, 0:NTL],
                in1=l_t2.rearrange("p a b -> p (a b)")[:, 0:NTL], op=AL.add))
            vop(lambda e, k=k: e.tensor_copy(out=a_sgmy[:, :, k],
                                             in_=a_lws[:, 0:NTL]))
        v_sgmy = vop(lambda e: None)

        ld_sct = dma("sync", f_scT3[:, :, :],
                     d_scT_ag.rearrange("(r e) t -> e r t", e=E),
                     wait=[(s_cc, ag_s), (s_ve, v_sgmy)])
        scb = []
        f_scT_flat = f_scT3.rearrange("e a t -> e (a t)")
        for g in range(NGT):
            t1 = top(lambda e, g=g: e.transpose(
                ps[1][:, 0:E], f_scT_flat[:, g * 128:(g + 1) * 128],
                c_id32[0:E, 0:E]),
                wait=[(s_sy, ld_sct)] + ([(s_ve, scb[-1])] if scb else []))
            scb.append(vop(lambda e, g=g: e.tensor_copy(
                out=f_scores[:, g, :], in_=ps[1][:, 0:E]), wait=[(s_te, t1)]))
        v_rt = emit_router(NGT, f_scores, f_sc, f_oh, f_g2, f_gm, f_t2, f_M,
                           a_ek, a_wk, a_ws, True, [(s_ve, scb[-1])])
        vop(lambda e: e.memset(f_Ww[:], 0.0), wait=[(s_ve, v_rt)])
        for k in range(K):
            vop(lambda e, k=k: e.tensor_tensor(
                out=f_oh[:], in0=bc1(c_i32[:], NGT, E),
                in1=bc2(a_ek[:, k, 0:NGT], E), op=AL.is_equal))
            vop(lambda e, k=k: e.tensor_tensor(
                out=f_oh[:], in0=f_oh[:], in1=bc2(a_wk[:, k, 0:NGT], E),
                op=AL.mult))
            vop(lambda e: e.tensor_tensor(out=f_Ww[:], in0=f_Ww[:], in1=f_oh[:],
                                          op=AL.add))
        v_ww = vop(lambda e: None)

        ev_c = vop(lambda e: e.memset(a_carry[:], 0.0), wait=[(s_ve, v_ww)])
        gp_ev = []
        for g in range(NGT):
            pb = ps[3] if (g % 2 == 0) else ps[4]
            w1 = [(s_ve, ev_c)]
            if g >= 2:
                w1.append((s_ac, gp_ev[g - 2]))
            top(lambda e, g=g, pb=pb: e.matmul(
                pb[:, 0:E], c_ut[:], f_M[:, g, :], start=True, stop=False,
                skip_group_check=True), wait=w1)
            t2 = top(lambda e, pb=pb: e.matmul(
                pb[:, 0:E], c_ones_row[:], a_carry[:], start=False, stop=True,
                skip_group_check=True))
            t3 = top(lambda e, g=g: e.matmul(
                ps[2][0:1, 0:E], c_ones_col[:], f_M[:, g, :],
                start=True, stop=True, skip_group_check=True))
            gp_ev.append(sop(lambda e, g=g, pb=pb: e.copy(
                out=f_P[:, g, :], in_=pb[:, 0:E]), wait=[(s_te, t2)]))
            ev_c = vop(lambda e: e.tensor_add(a_carry[:], a_carry[:],
                                              ps[2][0:1, 0:E]),
                       wait=[(s_te, t3)])
        v_ranks = gp_ev[-1]

        vop(lambda e: e.tensor_tensor(out=f_oh[:], in0=f_P[:], in1=f_M[:],
                                      op=AL.mult), wait=[(s_ac, v_ranks)])
        vop(lambda e: e.tensor_scalar_add(f_M[:], f_M[:], -1.0))
        v_pm = vop(lambda e: e.tensor_tensor(out=f_P[:], in0=f_oh[:], in1=f_M[:],
                                             op=AL.add))
        for el in range(EPC):
            vop(lambda e, el=el: e.tensor_tensor(
                out=f_sc[:], in0=f_P[:],
                in1=c_sel[:, el, :].unsqueeze(1).to_broadcast([128, NGT, E]),
                op=AL.mult))
            vop(lambda e, el=el: e.reduce_sum(out=a_pmy[:, :, el], in_=f_sc[:],
                                              axis=AX.X))
            vop(lambda e, el=el: e.tensor_tensor(
                out=f_sc[:], in0=f_Ww[:],
                in1=c_sel[:, el, :].unsqueeze(1).to_broadcast([128, NGT, E]),
                op=AL.mult))
            vop(lambda e, el=el: e.reduce_sum(out=a_wmy[:, :, el], in_=f_sc[:],
                                              axis=AX.X))
        v_pm = vop(lambda e: None)

        idx_ready = []
        prev_mm = None
        prev_evl = None
        prev_sv = None
        for el in range(EPC):
            vl = vop(lambda e, el=el: e.tensor_copy(
                out=a_lhsT[:, :, 2], in_=a_wmy[:, :, el]),
                wait=[(s_ve, v_pm)] + ([(s_te, prev_mm)] if prev_mm else []))
            ohl = []
            mm_last = None
            for g in range(NGT):
                wv = [(s_ve, vl)] if g == 0 else []
                if g >= 2:
                    wv.append((s_te, ohl[g - 2]))
                ohv = vop(lambda e, g=g, el=el: e.tensor_scalar(
                    out=a_oh[:, g % 2, :], in0=c_i384[:],
                    scalar1=a_pmy[:, g, el:el + 1], scalar2=None,
                    op0=AL.is_equal), wait=wv)
                mmw = [(s_ve, ohv)]
                if g == 0 and prev_evl is not None:
                    mmw.append((s_ac, prev_evl))
                mm_last = top(lambda e, g=g: e.matmul(
                    ps[0][0:3, 0:CPAD], a_lhsT[:, g, :], a_oh[:, g % 2, :],
                    start=(g == 0), stop=(g == NGT - 1), skip_group_check=True),
                    wait=mmw)
                ohl.append(mm_last)
            prev_mm = mm_last
            evl = sop(lambda e, el=el: e.copy(
                out=a_sslot[:, el * CPAD:(el + 1) * CPAD], in_=ps[0][0:3, 0:CPAD]),
                wait=[(s_te, mm_last)])
            prev_evl = evl
            for st in range(3):
                t1 = top(lambda e, el=el, st=st: e.transpose(
                    ps[1][:, 0:3],
                    a_sslot[:, el * CPAD + st * 128:el * CPAD + (st + 1) * 128],
                    c_id32[0:3, 0:3]), wait=[(s_ac, evl)] +
                    ([(s_ve, prev_sv)] if prev_sv else []))
                prev_sv = vop(lambda e, el=el, st=st: e.tensor_copy(
                    out=a_slotT[:, el * 3 + st, :], in_=ps[1][:, 0:3]),
                    wait=[(s_te, t1)])
                vop(lambda e, el=el, st=st: None) if False else None
                vop(lambda e, el=el, st=st: e.scalar_tensor_tensor(
                    out=a_slotT[:, el * 3 + st, 1:2],
                    in0=a_slotT[:, el * 3 + st, 1:2], scalar=128.0,
                    in1=a_slotT[:, el * 3 + st, 0:1], op0=AL.mult, op1=AL.add))
                vop(lambda e, el=el, st=st: e.tensor_copy(
                    out=a_idx[:, el * 3 + st:el * 3 + st + 1],
                    in_=a_slotT[:, el * 3 + st, 1:2]))
                idx_ready.append(vop(lambda e, el=el, st=st: e.tensor_copy(
                    out=a_wcol[:, el * 3 + st:el * 3 + st + 1],
                    in_=a_slotT[:, el * 3 + st, 2:3])))
        v_idx = idx_ready[-1]

        xp_stores = {}
        prev_store = {0: None, 1: None}
        for el in range(EPC):
            gts = []
            for st in range(3):
                gw = [(s_ve, v_idx), (s_cc, ag_x)]
                if prev_store[st % 2] is not None:
                    gw.append((s_sy, prev_store[st % 2]))
                gi = gind(v_gb[:, st % 2, :], d_xln_ag[:],
                          a_idx[:, el * 3 + st:el * 3 + st + 1], wait=gw)
                stv = dma("sync", d_xp[el, st * 128:(st + 1) * 128, :],
                          v_gb[:, st % 2, :], wait=[(s_po, gi)])
                gts.append(stv)
                prev_store[st % 2] = stv
            xp_stores[el] = gts[-1]

        sh_st = []
        for tc in range(2):
            t0 = tc * 512
            pass_w = [(s_sy, sh_st[-1])] if sh_st else []
            prev_ev = {3: None, 4: None}
            prev_wl = {0: None, 1: None}
            for phase, (dsrc, dstv, act, pb, pbi) in enumerate(
                    ((d_sgT, v_shG, AF.Silu, ps[3], 3),
                     (d_suT, v_shU, AF.Identity, ps[4], 4))):
                for is_ in range(16):
                    buf = is_ % 2
                    wlw = []
                    if prev_wl[buf] is not None:
                        wlw.append((s_te, prev_wl[buf]))
                    wl = dma("gpsimd",
                             Cc[:, buf * 2:(buf + 1) * 2, :].rearrange(
                                 "p a b -> p (a b)").rearrange(
                                 "p (a b) -> p a b", b=128),
                             dsrc[:, is_ * 128:(is_ + 1) * 128].rearrange(
                                 "(a p) b -> p a b", p=128), wait=wlw)
                    wv = Cf[:, buf * 2048:(buf + 1) * 2048].rearrange(
                        "p (a b) -> p a b", b=128)

                    def mmsh(e, wv=wv, t0=t0, pb=pb):
                        for dmt in range(16):
                            e.matmul(pb[:, 0:512], wv[:, dmt, :],
                                     v_xlnT[:, dmt, t0:t0 + 512],
                                     start=(dmt == 0), stop=(dmt == 15))
                    t1 = top(mmsh, wait=[(s_po, wl), (s_sc, xlT_all)] +
                             pass_w +
                             ([(s_ac, prev_ev[pbi])] if prev_ev[pbi] else []))
                    prev_wl[buf] = t1
                    prev_ev[pbi] = sop(
                        lambda e, is_=is_, dstv=dstv, act=act, pb=pb: e.activation(
                            out=dstv[:, is_, :], in_=pb[:, 0:512], func=act),
                        wait=[(s_te, t1)])
            vhh = vop(lambda e: e.tensor_tensor(
                out=v_shG.rearrange("p a b -> p (a b)"),
                in0=v_shG.rearrange("p a b -> p (a b)"),
                in1=v_shU.rearrange("p a b -> p (a b)"), op=AL.mult),
                wait=[(s_ac, prev_ev[3]), (s_ac, prev_ev[4])])
            a1_prev = [None] * 4
            dn_last = None
            for dc in range(4):
                wl = dma("gpsimd",
                         Cc[:, 8:16, :].rearrange("p a b -> p (a b)").rearrange(
                             "p (a b) -> p a b", b=512),
                         d_sdT[:, dc * 512:(dc + 1) * 512].rearrange(
                             "(a p) b -> p a b", p=128),
                         wait=[(s_te, dn_last)] if dn_last else ())
                wv = Cf[:, 8 * 1024:16 * 1024].rearrange("p (a b) -> p a b", b=512)
                for tt in range(4):
                    pb = ps[3 + tt]

                    def mmdn(e, tt=tt, wv=wv, pb=pb):
                        for it in range(16):
                            e.matmul(pb[:, 0:512],
                                     v_shG[:, it, tt * 128:(tt + 1) * 128],
                                     wv[:, it, :], start=(it == 0), stop=(it == 15))
                    t1 = top(mmdn, wait=[(s_ve, vhh), (s_po, wl)] +
                             ([(s_ac, a1_prev[tt])] if a1_prev[tt] else []))
                    dn_last = t1
                    a1 = sop(lambda e, dc=dc, tt=tt, pb=pb: e.activation(
                        out=v_shY[:, tt, dc * 512:(dc + 1) * 512],
                        in_=pb[:, 0:512], func=AF.Identity), wait=[(s_te, t1)])
                    a1_prev[tt] = a1
                    if dc == 3:
                        sh_st.append(dma("sync",
                                         d_shared[t0 + tt * 128:
                                                  t0 + (tt + 1) * 128, :],
                                         v_shY[:, tt, :],
                                         wait=[(s_ac, a1)]))
        sh_all = sh_st[-1]

        y_st = []
        y_prev = {0: None, 1: None}
        ev_prev2 = None
        wd_prev = {0: None, 1: None}
        for el in range(EPC):
            xls = [dmaT("scalar", v_xpT[:, dmt, :],
                        d_xp[el, 0:CPAD, dmt * 128:(dmt + 1) * 128],
                        wait=[(s_sy, xp_stores[el])]) for dmt in range(16)]
            pe_prev = {0: None, 1: None}
            we_prev = {}
            for phase, (dsrc, dstv, act, pb, cbase) in enumerate(
                    ((d_wg, v_eg, AF.Silu, ps[0], 0),
                     (d_wu, v_eu, AF.Identity, ps[1], 4))):
                for is_ in range(8):
                    buf = cbase + (is_ % 2) * 2
                    wlw = [(s_te, we_prev[buf])] if buf in we_prev else []
                    wl = dma("gpsimd",
                             Cc[:, buf:buf + 2, :].rearrange(
                                 "p a b -> p (a b)").rearrange(
                                 "p (a b) -> p a b", b=128),
                             dsrc[el, :, is_ * 128:(is_ + 1) * 128].rearrange(
                                 "(a p) b -> p a b", p=128), wait=wlw)
                    wv = Cf[:, buf * 1024:(buf + 2) * 1024].rearrange(
                        "p (a b) -> p a b", b=128)

                    def mme(e, wv=wv, pb=pb):
                        for dmt in range(16):
                            e.matmul(pb[:, 0:CPAD], wv[:, dmt, :], v_xpT[:, dmt, :],
                                     start=(dmt == 0), stop=(dmt == 15))
                    t1 = top(mme, wait=[(s_po, wl), (s_sc, xls[-1])] +
                             ([(s_ac, pe_prev[phase])] if pe_prev[phase] else []))
                    we_prev[buf] = t1
                    pe_prev[phase] = sop(
                        lambda e, is_=is_, dstv=dstv, act=act, pb=pb: e.activation(
                            out=dstv[:, is_, :], in_=pb[:, 0:CPAD], func=act),
                        wait=[(s_te, t1)])
            vhh = vop(lambda e: e.tensor_tensor(
                out=v_eg.rearrange("p a b -> p (a b)"),
                in0=v_eg.rearrange("p a b -> p (a b)"),
                in1=v_eu.rearrange("p a b -> p (a b)"), op=AL.mult),
                wait=[(s_ac, pe_prev[0]), (s_ac, pe_prev[1])])
            for dc in range(4):
                buf = 8 + (dc % 2) * 4
                wlw = [(s_te, wd_prev[dc % 2])] if wd_prev[dc % 2] else []
                wl = dma("gpsimd",
                         Cc[:, buf:buf + 4, :].rearrange(
                             "p a b -> p (a b)").rearrange(
                             "p (a b) -> p a b", b=512),
                         d_wd[el, :, dc * 512:(dc + 1) * 512].rearrange(
                             "(a p) b -> p a b", p=128), wait=wlw)
                wv = Cf[:, buf * 1024:(buf + 4) * 1024].rearrange(
                    "p (a b) -> p a b", b=512)
                for st in range(3):
                    def mmd(e, st=st, wv=wv):
                        for it in range(8):
                            e.matmul(ps[2][:, 0:512],
                                     v_eg[:, it, st * 128:(st + 1) * 128],
                                     wv[:, it, :], start=(it == 0), stop=(it == 7))
                    t1 = top(mmd, wait=[(s_ve, vhh), (s_po, wl)] +
                             ([(s_ve, ev_prev2)] if ev_prev2 else []))
                    wd_prev[dc % 2] = t1
                    evw = [(s_te, t1)]
                    if y_prev[st % 2] is not None:
                        evw.append((s_sy, y_prev[st % 2]))
                    ev = vop(lambda e, st=st, dc=dc, el=el: e.tensor_scalar(
                        out=a_y[:, st % 2, :], in0=ps[2][:, 0:512],
                        scalar1=a_wcol[:, el * 3 + st:el * 3 + st + 1],
                        scalar2=None, op0=AL.mult), wait=evw)
                    ev_prev2 = ev
                    yv = dma("sync",
                             d_res_my[el * CPAD + st * 128:
                                      el * CPAD + (st + 1) * 128,
                                      dc * 512:(dc + 1) * 512],
                             a_y[:, st % 2, :], wait=[(s_ve, ev)])
                    y_st.append(yv)
                    y_prev[st % 2] = yv
        ag_r = coll(d_res_my[:], d_res_ag[0:8 * SEG, :], G8,
                    wait=[(s_sy, y_st[-1])] + R["res"])

        out_st = []
        last_g = None
        v1p = None
        v4p = None
        for g in range(NTL):
            gsems = []
            for k in range(K):
                gw = [(s_cc, ag_r)]
                if v1p is not None:
                    gw.append((s_ve, v1p))
                gi = gind(v_comb[:, k, :], d_res_ag[:],
                          a_sgmy[:, g, k:k + 1], wait=gw)
                gsems.append(gi)
            last_g = gsems[-1]
            xw = [(s_sy, out_st[g - 2])] if g >= 2 else []
            ldx = dma("sync", a_x32[:, g % 2, :], x_in[g * 128:(g + 1) * 128, :],
                      wait=xw)
            lds = dma("scalar", v_gb[:, 1, :], d_shared[g * 128:(g + 1) * 128, :],
                      wait=[(s_sy, sh_all)] + ([(s_ve, v4p)] if v4p else []))
            v1 = vop(lambda e: e.tensor_tensor(out=f_sq[:], in0=v_comb[:, 0, :],
                                               in1=v_comb[:, 1, :], op=AL.add),
                     wait=[(s_po, gsems[-1])])
            vop(lambda e: e.tensor_tensor(out=f_xstage[:], in0=v_comb[:, 2, :],
                                          in1=v_comb[:, 3, :], op=AL.add))
            vop(lambda e: e.tensor_tensor(out=f_sq[:], in0=f_sq[:],
                                          in1=f_xstage[:], op=AL.add))
            v4p = vop(lambda e: e.tensor_tensor(out=f_sq[:], in0=f_sq[:],
                                                in1=v_gb[:, 1, :], op=AL.add),
                      wait=[(s_sc, lds)])
            v5 = vop(lambda e, g=g: e.scalar_tensor_tensor(
                out=a_x32[:, g % 2, :], in0=f_sq[:], scalar=0.5,
                in1=a_x32[:, g % 2, :], op0=AL.mult, op1=AL.add),
                wait=[(s_sy, ldx)])
            v1p = v5
            out_st.append(dma("sync", x_out[g * 128:(g + 1) * 128, :],
                              a_x32[:, g % 2, :], wait=[(s_ve, v5)]))
        R["xln_my"] = [(s_sc, xlT_all)]
        R["xln_ag"] = [(s_sy, xp_stores[EPC - 1])]
        R["scT"] = [(s_sy, ld_sct)]
        R["res"] = [(s_po, last_g)]
        R["cnt"] = [(s_sy, ld_cnt)]
        return out_st[-1]

    def emit_attn(x_in, x_out):
        def sub_exit(w0):
            prev = None
            for q in range(4):
                w = list(w0) if q == 0 else []
                prev = dma("sync", x_out[q * 256:(q + 1) * 256, :],
                           x_in[q * 256:(q + 1) * 256, :], wait=w)
            return prev
        ln_st = []
        a3p = {}
        for g in range(NTL):
            xb = a_x32[:, g % 2, :]
            xlnb = v_gb[:, g % 2, :]
            ldw = [(s_ac, a3p[g - 2])] if g >= 2 else []
            outw = [(s_sy, ln_st[g - 2])] if g >= 2 else []
            a3, _ = emit_ln_tile(x_in[g * 128:(g + 1) * 128, :], xb, xlnb,
                                 n_sq[:], None, ld_wait=ldw, out_wait=outw)
            a3p[g] = a3
            wmy = R["xln_my"] if g == 0 else []
            ln_st.append(dma("sync", d_xln_my[g * 128:(g + 1) * 128, :], xlnb,
                             wait=[(s_ac, a3)] + wmy))
        ln_all = ln_st[-1]
        ag_b = coll(d_xln_my[:], d_xba[:], GP, wait=[(s_sy, ln_all)] + R["xba"])

        xlT = [dmaT("scalar", A[:, dmt, :],
                    d_xba[0:S, dmt * 128:(dmt + 1) * 128],
                    wait=[(s_cc, ag_b)]) for dmt in range(16)]
        xlT_all = xlT[-1]
        axlnT = A

        wvl = []
        for ch in range(2):
            wvl.append(dma("gpsimd",
                           Dd[:, ch * 8:(ch + 1) * 8, :].rearrange(
                               "p a b -> p (a b)").rearrange(
                               "p (a b) -> p a b", b=512),
                           d_wv[:, ch * 512:(ch + 1) * 512].rearrange(
                               "(a p) b -> p a b", p=128)))
        v_ev = []
        for ch in range(2):
            wv = Df[:, ch * 8192:(ch + 1) * 8192].rearrange(
                "p (a b) -> p a b", b=512)
            for ts in range(16):
                def mmv(e, ch=ch, ts=ts, wv=wv):
                    for dmt in range(16):
                        e.matmul(ps[6][:, 0:512],
                                 axlnT[:, dmt, ts * 128:(ts + 1) * 128],
                                 wv[:, dmt, :], start=(dmt == 0), stop=False,
                                 skip_group_check=True)
                    e.matmul(ps[6][:, 0:512], c_ones16[:],
                             c_bv16[:, ch * 512:(ch + 1) * 512],
                             start=False, stop=True, skip_group_check=True)
                t1 = top(mmv, wait=[(s_po, wvl[ch]), (s_sc, xlT_all),
                                    (s_po, ld_bv)] +
                         ([(s_ac, v_ev[-1])] if v_ev else []))
                v_ev.append(sop(lambda e, ch=ch, ts=ts: e.activation(
                    out=Cc[:, ts, ch * 512:(ch + 1) * 512], in_=ps[6][:, 0:512],
                    func=AF.Identity), wait=[(s_te, t1)]))
        v_all = v_ev[-1]
        if ASUB == 1:
            return sub_exit([(s_ac, v_all)])

        qk_sp = []
        nrm_dma = []
        whd_user = {0: None, 1: None}
        prev_spill = {0: None, 1: None}
        q_ev_pp = None
        k_ev_pp = None
        nev_pp = None
        for h in range(HPC):
            hb = h % 2
            qdst = t_q0 if hb == 0 else t_q1
            kdst = t_k0 if hb == 0 else t_k1
            wqw = ([(s_ac, v_all)] if h == 0 else
                   [(s_te, whd_user[0])])
            if prev_spill[hb] is not None:
                wqw.append((s_sy, prev_spill[hb]))
            wql = dma("gpsimd",
                      Dd[:, 8:10, :].rearrange("p a b -> p (a b)").rearrange(
                          "p (a b) -> p a b", b=128),
                      d_wq[:, h * 128:(h + 1) * 128].rearrange(
                          "(a p) b -> p a b", p=128),
                      wait=wqw)
            qev = []
            for tc in range(4):
                def mmq(e, tc=tc):
                    for dmt in range(16):
                        e.matmul(ps[0][:, 0:512], t_whd[:, dmt, :],
                                 axlnT[:, dmt, tc * 512:(tc + 1) * 512],
                                 start=(dmt == 0), stop=(dmt == 15))
                t1 = top(mmq, wait=[(s_po, wql)] +
                         ([(s_ac, qev[-1])] if qev else
                          ([(s_ac, q_ev_pp)] if q_ev_pp else [])))
                qev.append(sop(lambda e, tc=tc, h=h, qdst=qdst: e.activation(
                    out=qdst[:, tc * 512:(tc + 1) * 512], in_=ps[0][:, 0:512],
                    func=AF.Identity, bias=c_bqkv[:, h:h + 1]),
                    wait=[(s_te, t1)]))
            whd_user[0] = p.val("s_te")
            wkw = ([(s_ac, v_all)] if h == 0 else
                   [(s_te, whd_user[1])])
            wkl = dma("gpsimd",
                      Dd[:, 10:12, :].rearrange("p a b -> p (a b)").rearrange(
                          "p (a b) -> p a b", b=128),
                      d_wk[:, h * 128:(h + 1) * 128].rearrange(
                          "(a p) b -> p a b", p=128),
                      wait=wkw)
            kev = []
            for tc in range(4):
                def mmk(e, tc=tc):
                    for dmt in range(16):
                        e.matmul(ps[1][:, 0:512], t_whdB[:, dmt, :],
                                 axlnT[:, dmt, tc * 512:(tc + 1) * 512],
                                 start=(dmt == 0), stop=(dmt == 15))
                t1 = top(mmk, wait=[(s_po, wkl)] +
                         ([(s_ac, kev[-1])] if kev else
                          ([(s_ac, k_ev_pp)] if k_ev_pp else [])))
                kev.append(sop(lambda e, tc=tc, h=h, kdst=kdst: e.activation(
                    out=kdst[:, tc * 512:(tc + 1) * 512], in_=ps[1][:, 0:512],
                    func=AF.Identity, bias=c_bqkv[:, HPC + h:HPC + h + 1]),
                    wait=[(s_te, t1)]))
            whd_user[1] = p.val("s_te")
            aq = sop(lambda e, qdst=qdst: e.activation(
                out=n_sq[:], in_=qdst, func=AF.Square), wait=[(s_ac, qev[-1])])
            nevs = []
            for chk in range(4):
                nw = [(s_ac, aq)]
                if nevs:
                    nw.append((s_ve, nevs[-1]))
                elif nev_pp is not None:
                    nw.append((s_ve, nev_pp))
                t1 = top(lambda e, chk=chk: e.matmul(
                    ps[2][0:1, 0:512], c_ones_col[:],
                    n_sq[:, chk * 512:(chk + 1) * 512], start=True, stop=True,
                    skip_group_check=True), wait=nw)
                cw = [(s_te, t1)]
                if not nevs and nrm_dma:
                    cw.append((s_sy, nrm_dma[-1]))
                nevs.append(vop(lambda e, chk=chk: e.tensor_copy(
                    out=n_p0[:, chk * 512:(chk + 1) * 512], in_=ps[2][0:1, 0:512]),
                    wait=cw))
            nrm_dma.append(dma("sync", n_norm[h:h + 1, :], n_p0[:],
                               wait=[(s_ve, nevs[-1])]))
            qk_sp.append(dma("sync", d_qk[h * 128:(h + 1) * 128, :], qdst,
                             wait=[(s_ac, aq)]))
            ksp = dma("sync", d_qk[1024 + h * 128:1024 + (h + 1) * 128, :],
                      kdst, wait=[(s_ac, kev[-1])])
            qk_sp.append(ksp)
            prev_spill[hb] = ksp
            q_ev_pp = qev[-1]
            k_ev_pp = kev[-1]
            nev_pp = nevs[-1]
        qk_all = qk_sp[-1]
        if ASUB == 2:
            return sub_exit([(s_sy, qk_all)])

        v0 = vop(lambda e: e.tensor_copy(out=n_work[:], in_=n_norm[:]),
                 wait=[(s_sy, nrm_dma[-1])])
        for rnd in range(8):
            vop(lambda e: e.max(out=a_gsum[0:8, :], in_=n_work[:]))
            if rnd == 7:
                vop(lambda e: e.memset(a_gsum[0:8, 6:8], -1.0))
            vop(lambda e: e.match_replace(out=n_work[:], in_to_replace=a_gsum[0:8, :],
                                          in_values=n_work[:], imm_value=-1.0))
        vop(lambda e: e.tensor_tensor(out=n_m[:], in0=n_norm[:], in1=n_work[:],
                                      op=AL.not_equal))
        vop(lambda e: e.memset(n_m[0:8, 0:1], 1.0))
        vop(lambda e: e.memset(n_m[0:8, 2047:2048], 1.0))
        vop(lambda e: e.tensor_tensor_scan(out=n_r[:], data0=n_m[:], data1=n_m[:],
                                           initial=0.0, op0=AL.add, op1=AL.bypass))
        v_scan = vop(lambda e: e.tensor_sub(n_r[:], n_r[:], n_m[:]))
        rc_ev = []
        for g in range(16):
            t1 = top(lambda e, g=g: e.transpose(
                ps[2][:, 0:8], n_r[:, g * 128:(g + 1) * 128], c_id32[0:8, 0:8]),
                wait=[(s_ve, v_scan)] + ([(s_ve, rc_ev[-1])] if rc_ev else []))
            rc_ev.append(vop(lambda e, g=g: e.tensor_copy(
                out=a_rcm[:, g, 0, :], in_=ps[2][:, 0:8]), wait=[(s_te, t1)]))
            t2 = top(lambda e, g=g: e.transpose(
                ps[2][:, 0:8], n_m[:, g * 128:(g + 1) * 128], c_id32[0:8, 0:8]),
                wait=[(s_ve, rc_ev[-1])])
            rc_ev.append(vop(lambda e, g=g: e.tensor_copy(
                out=a_rcm[:, g, 1, :], in_=ps[2][:, 0:8]), wait=[(s_te, t2)]))
        v_rcm = rc_ev[-1]
        if ASUB == 3:
            return sub_exit([(s_ve, v_rcm)])

        xg_st = []
        wx_prev = None
        gi_prev = None
        for h in range(HPC):
            mml = None
            for g in range(16):
                w0l = []
                if g == 0 and h == 0:
                    w0l = [(s_ve, v_rcm)]
                elif g == 0:
                    w0l = [(s_ve, wx_prev)]
                else:
                    w0l = [(s_te, mml)]
                v1 = vop(lambda e, g=g, h=h: e.tensor_scalar(
                    out=a_wtf[:], in0=c_i64[:],
                    scalar1=a_rcm[:, g, 0, h:h + 1], scalar2=None,
                    op0=AL.is_equal), wait=w0l)
                v2 = vop(lambda e, g=g, h=h: e.tensor_scalar(
                    out=a_wtf[:], in0=a_wtf[:],
                    scalar1=a_rcm[:, g, 1, h:h + 1], scalar2=None, op0=AL.mult))
                mml = top(lambda e, g=g: e.matmul(
                    ps[3][0:64, 0:3], a_wtf[:, 0:64], c_gatok[:, g, :],
                    start=(g == 0), stop=(g == 15), skip_group_check=True),
                    wait=[(s_ve, v2)])
            ev = vop(lambda e, h=h: e.tensor_copy(out=a_gidx[:, 0:3],
                                                  in_=ps[3][0:64, 0:3]),
                     wait=[(s_te, mml)])
            vop(lambda e: e.scalar_tensor_tensor(
                out=a_gidx[:, 1:2], in0=a_gidx[:, 1:2], scalar=128.0,
                in1=a_gidx[:, 0:1], op0=AL.mult, op1=AL.add))
            vop(lambda e: e.tensor_scalar(out=a_gidx[:, 3:4], in0=a_gidx[:, 2:3],
                                          scalar1=0.0, scalar2=None, op0=AL.is_gt))
            vop(lambda e: e.tensor_tensor(out=a_gidx[:, 4:5], in0=a_gidx[:, 1:2],
                                          in1=a_gidx[:, 3:4], op=AL.mult))
            vop(lambda e: e.tensor_tensor(out=a_gidx[:, 5:6], in0=a_gidx[:, 4:5],
                                          in1=a_gidx[:, 3:4], op=AL.mult)
                if False else e.tensor_copy(out=a_gidx[:, 5:6], in_=a_gidx[:, 4:5]))
            vop(lambda e: e.tensor_scalar(out=a_gidx[:, 6:7], in0=a_gidx[:, 3:4],
                                          scalar1=1.0, scalar2=-2048.0,
                                          op0=AL.subtract, op1=AL.mult))
            vop(lambda e: e.tensor_tensor(out=a_gidx[:, 5:6], in0=a_gidx[:, 5:6],
                                          in1=a_gidx[:, 6:7], op=AL.add))
            vz = vop(lambda e: e.memset(a_gi32[:], 0),
                     wait=[(s_po, gi_prev)] if gi_prev else ())
            v3 = vop(lambda e: e.tensor_copy(out=a_gi32[0:64, 0:1],
                                             in_=a_gidx[:, 4:5]))
            v4 = vop(lambda e, h=h: e.tensor_scalar(
                out=a_sci[0:64, h:h + 1], in0=a_gidx[:, 5:6],
                scalar1=float(h * 2049), scalar2=None, op0=AL.add))
            wx_prev = v4
            giw = [(s_ve, v4), (s_cc, ag_b)]
            if xg_st:
                giw.append((s_sy, xg_st[-1]))
            gi = gind(Dd[:, 12:14, :].rearrange("p a b -> p (a b)"),
                      d_xba[:], a_gi32[:, 0:1], wait=giw)
            gi_prev = gi
            xg_st.append(dma("sync", d_xg[h * 128:(h + 1) * 128, :],
                             Dd[:, 12:14, :].rearrange("p a b -> p (a b)"),
                             wait=[(s_po, gi)]))
        xgT = [dmaT("scalar", t_xgT[:, dmt, :],
                    d_xg[0:1024, dmt * 128:(dmt + 1) * 128],
                    wait=[(s_sy, xg_st[-1]), (s_sc, xlT_all)])
               for dmt in range(16)]
        xgT_all = xgT[-1]
        if ASUB == 4:
            return sub_exit([(s_sc, xgT_all)])

        vop(lambda e: e.memset(t_pr[0][:, 0:64], 0.0), wait=[(s_ac, v_all)])
        vz_pad = vop(lambda e: e.memset(t_pr[1][:, 0:64], 0.0))
        scat_po = []
        prev_core_te = {0: 0, 1: 0}
        gev_pp = None
        evg_pp = None
        pp_a1 = {0: None, 1: None}
        pp_mxv = {0: None, 1: None}
        pp_trs_t = {0: None, 1: None}
        pp_trs_c = {0: None, 1: None}
        pp_ev = {0: None, 1: None}
        for h in range(HPC):
            hb = h % 2
            qb = t_q0 if hb == 0 else t_q1
            kb = t_k0 if hb == 0 else t_k1
            lqw = [(s_sy, qk_all), (s_sc, xgT_all)]
            if prev_core_te[hb]:
                lqw.append((s_te, prev_core_te[hb]))
            lq = dma("sync", qb, d_qk[h * 128:(h + 1) * 128, :], wait=lqw)
            lk = dma("sync", kb, d_qk[1024 + h * 128:1024 + (h + 1) * 128, :])
            st_blocks = []

            def emit_scores(b, h=h, qb=qb, kb=kb, lk=lk):
                w0 = max(b * 128 - 64, 0)
                W = min(b * 128 + 192, S) - w0
                par = b % 2
                sb = ps[0] if par == 0 else ps[7]
                smw = [(s_sy, lk)]
                if h == 0 and b <= 1:
                    smw.append((s_ve, vz_pad))
                if pp_a1[par] is not None:
                    smw += [(s_ac, pp_a1[par]), (s_ve, pp_mxv[par])]
                return top(lambda e, b=b, w0=w0, W=W, sb=sb: e.matmul(
                    sb[:, 0:W], qb[:, b * 128:(b + 1) * 128], kb[:, w0:w0 + W],
                    start=True, stop=True), wait=smw)

            t1s = {0: emit_scores(0)}
            for b in range(NB):
                if b + 1 < NB:
                    t1s[b + 1] = emit_scores(b + 1)
                par = b % 2
                sb = ps[0] if par == 0 else ps[7]
                tb = ps[1] if par == 0 else ps[6]
                cb = ps[2] if par == 0 else ps[5]
                w0 = max(b * 128 - 64, 0)
                W = min(b * 128 + 192, S) - w0
                mx = a_stat[:, 5 + 3 * par:6 + 3 * par]
                sm = a_stat[:, 6 + 3 * par:7 + 3 * par]
                rs = a_stat[:, 7 + 3 * par:8 + 3 * par]
                mxw = [(s_te, t1s[b])]
                if pp_a1[par] is not None:
                    mxw.append((s_ac, pp_a1[par]))
                mxv = vop(lambda e, W=W, mx=mx, sb=sb: e.reduce_max(
                    out=mx, in_=sb[:, 0:W], axis=AX.X), wait=mxw)
                pp_mxv[par] = mxv
                v2 = vop(lambda e, mx=mx: e.tensor_scalar_mul(mx, mx, -ISQ))
                a1w = [(s_ve, v2)]
                if pp_trs_t[par] is not None:
                    a1w.append((s_te, pp_trs_t[par]))
                a1 = sop(lambda e, W=W, mx=mx, sm=sm, sb=sb, par=par:
                         e.activation(
                             out=t_pr[par][:, 64:64 + W], in_=sb[:, 0:W],
                             func=AF.Exp, bias=mx, scale=ISQ, accum_out=sm),
                         wait=a1w)
                pp_a1[par] = a1
                v3 = vop(lambda e, sm=sm, rs=rs: e.reciprocal(rs, sm),
                         wait=[(s_ac, a1)])
                if b == 0:
                    segs = [(64, 128, 0, 0), (192, 64, 1, 1)]
                elif b == NB - 1:
                    segs = [(0, 128, 0, b - 1), (128, 128, 1, b)]
                else:
                    segs = [(0, 128, 0, b - 1), (128, 128, 1, b),
                            (256, 64, 2, b + 1)]
                trs = []
                for (c0, n, reg, ttile) in segs:
                    t2w = [(s_ac, a1)]
                    if trs:
                        t2w.append((s_ve, trs[-1][1]))
                    elif pp_trs_c[par] is not None:
                        t2w.append((s_ve, pp_trs_c[par]))
                    t2 = top(lambda e, c0=c0, n=n, tb=tb, par=par: e.transpose(
                        tb.ap().bitcast(BF16)[0:n, 0:128],
                        t_pr[par][:, c0:c0 + n], c_id16[:]), wait=t2w)
                    trs.append((t2, vop(
                        lambda e, n=n, reg=reg, tb=tb, par=par: e.tensor_copy(
                            out=t_prT[par][0:n, reg * 128:reg * 128 + 128],
                            in_=tb.ap().bitcast(BF16)[0:n, 0:128]),
                        wait=[(s_te, t2)])))
                pp_trs_t[par] = trs[-1][0]
                pp_trs_c[par] = trs[-1][1]

                def ctx_mms(e, h=h, segs=segs, cb=cb, par=par):
                    for i, (c0, n, reg, ttile) in enumerate(segs):
                        e.matmul(cb[:, 0:DH],
                                 t_prT[par][0:n, reg * 128:reg * 128 + 128],
                                 Cc[0:n, ttile, h * 128:(h + 1) * 128],
                                 start=(i == 0), stop=(i == len(segs) - 1),
                                 skip_group_check=True)
                t3w = [(s_ve, trs[-1][1])]
                if pp_ev[par] is not None:
                    t3w.append((s_ve, pp_ev[par]))
                if b == 1 and evg_pp is not None:
                    t3w.append((s_ve, evg_pp))
                t3 = top(ctx_mms, wait=t3w)
                evw2 = [(s_te, t3), (s_ve, v3)]
                if b >= 4:
                    evw2.append((s_sy, st_blocks[b - 4]))
                ev = vop(lambda e, b=b, rs=rs, cb=cb: e.tensor_scalar(
                    out=t_ctxs[:, b % 4, :], in0=cb[:, 0:DH],
                    scalar1=rs, scalar2=None, op0=AL.mult), wait=evw2)
                pp_ev[par] = ev
                st_blocks.append(dma("sync",
                                     d_ctx_my[h * 2049 + b * 128:
                                              h * 2049 + (b + 1) * 128, :],
                                     t_ctxs[:, b % 4, :], wait=[(s_ve, ev)]))
            if ASUB == 44:
                prev_core_te[hb] = p.val("s_te")
                continue
            wq2 = dma("gpsimd",
                      Dd[:, 8:10, :].rearrange("p a b -> p (a b)").rearrange(
                          "p (a b) -> p a b", b=128),
                      d_wq[:, h * 128:(h + 1) * 128].rearrange(
                          "(a p) b -> p a b", p=128),
                      wait=[(s_te, whd_user[0])])

            def mmqg(e, h=h):
                for dmt in range(16):
                    e.matmul(ps[3][:, 0:64], t_whd[:, dmt, :],
                             t_xgT[:, dmt, h * 128:h * 128 + 64],
                             start=(dmt == 0), stop=(dmt == 15))
            tqg = top(mmqg, wait=[(s_po, wq2)] +
                      ([(s_ac, gev_pp)] if gev_pp else []))
            whd_user[0] = p.val("s_te")
            aqg = sop(lambda e, h=h: e.activation(
                out=a_qgT[:], in_=ps[3][:, 0:64], func=AF.Identity,
                bias=c_bqkv[:, h:h + 1]), wait=[(s_te, tqg)])
            if ASUB == 4411:
                prev_core_te[hb] = p.val("s_te")
                gev_pp = aqg
                continue
            gevs = []
            gmx_prev = 0
            for chk in range(4):
                t1 = top(lambda e, chk=chk, kb=kb: e.matmul(
                    ps[3][0:64, 0:512], a_qgT[:], kb[:, chk * 512:(chk + 1) * 512],
                    start=True, stop=True),
                    wait=[(s_ac, aqg)] + ([(s_ac, gevs[-1]),
                                          (s_ve, gmx_prev)] if gevs else []))
                gevs.append(sop(lambda e, chk=chk: e.activation(
                    out=t_gq[:, chk * 512:(chk + 1) * 512], in_=ps[3][0:64, 0:512],
                    func=AF.Identity), wait=[(s_te, t1)]))
                gmx_prev = vop(lambda e, chk=chk: e.reduce_max(
                    out=a_gsum[:, chk:chk + 1],
                    in_=t_gq[:, chk * 512:(chk + 1) * 512],
                    axis=AX.X), wait=[(s_ac, gevs[-1])])
            if ASUB == 4412:
                prev_core_te[hb] = p.val("s_te")
                gev_pp = gevs[-1]
                continue
            vop(lambda e: e.tensor_tensor(out=a_gsum[:, 4:5], in0=a_gsum[:, 0:1],
                                          in1=a_gsum[:, 1:2], op=AL.max),
                wait=[(s_ac, gevs[-1])])
            vop(lambda e: e.tensor_tensor(out=a_gsum[:, 4:5], in0=a_gsum[:, 4:5],
                                          in1=a_gsum[:, 2:3], op=AL.max))
            vop(lambda e: e.tensor_tensor(out=a_gsum[:, 4:5], in0=a_gsum[:, 4:5],
                                          in1=a_gsum[:, 3:4], op=AL.max))
            vmx = vop(lambda e: e.tensor_scalar_mul(a_gsum[:, 4:5],
                                                    a_gsum[:, 4:5], -ISQ))
            asum = sop(lambda e: e.activation(
                out=t_gq[:], in_=t_gq[:],
                func=AF.Exp, bias=a_gsum[0:64, 4:5], scale=ISQ,
                accum_out=a_gsum[0:64, 5:6]), wait=[(s_ve, vmx)])
            vrs = vop(lambda e: e.reciprocal(a_gsum[:, 5:6], a_gsum[:, 5:6]),
                      wait=[(s_ac, asum)])
            if ASUB == 441:
                prev_core_te[hb] = p.val("s_te")
                gev_pp = gevs[-1]
                continue
            gtr = []
            for j in range(16):
                t1 = top(lambda e, j=j: e.transpose(
                    ps[4].ap().bitcast(BF16)[:, 0:64],
                    t_gq[:, j * 128:(j + 1) * 128], c_id16[0:64, 0:64]),
                    wait=[(s_ac, asum)] + ([(s_ve, gtr[-1])] if gtr else []))
                gtr.append(vop(lambda e, j=j: e.tensor_copy(
                    out=t_gpT[:, j, :], in_=ps[4].ap().bitcast(BF16)[:, 0:64]),
                    wait=[(s_te, t1)]))

            if ASUB == 442:
                prev_core_te[hb] = p.val("s_te")
                gev_pp = gevs[-1]
                continue

            def mmgc(e, h=h):
                for j in range(16):
                    e.matmul(ps[5][0:64, 0:DH], t_gpT[:, j, :],
                             Cc[:, j, h * 128:(h + 1) * 128],
                             start=(j == 0), stop=(j == 15))
            t2 = top(mmgc, wait=[(s_ve, gtr[-1])] +
                     ([(s_ve, evg_pp)] if evg_pp else []))
            evgw = [(s_te, t2), (s_ve, vrs)]
            if scat_po:
                evgw.append((s_po, scat_po[-1]))
            evg = vop(lambda e: e.tensor_scalar(
                out=a_gctx[:], in0=ps[5][0:64, 0:DH],
                scalar1=a_gsum[0:64, 5:6], scalar2=None, op0=AL.mult),
                wait=evgw)
            if ASUB == 443:
                prev_core_te[hb] = p.val("s_te")
                gev_pp = gevs[-1]
                evg_pp = evg
                continue
            v5 = vop(lambda e, h=h: e.tensor_copy(out=a_gi32[0:64, 1:2],
                                                  in_=a_sci[0:64, h:h + 1]))
            scat_po.append(gscat(d_ctx_my[:], a_gi32[0:64, 1:2], a_gctx[:],
                                 wait=[(s_ve, evg), (s_ve, v5),
                                       (s_sy, st_blocks[-1])]))
            prev_core_te[hb] = p.val("s_te")
            gev_pp = gevs[-1]
            evg_pp = evg
        if ASUB == 44:
            return sub_exit([(s_sy, st_blocks[-1])])
        if ASUB in (4411, 4412, 441, 442, 443):
            return sub_exit([(s_te, p.val("s_te")), (s_ve, p.val("s_ve"))])
        if ASUB == 45:
            return sub_exit([(s_po, scat_po[-1])])
        ag_c = coll(d_ctx_my[:], d_ctx_pag[:], GP,
                    wait=[(s_po, scat_po[-1]), (s_sy, pad_st)] + R["ctx"])
        if ASUB == 5:
            return sub_exit([(s_cc, ag_c)])

        out_st = []
        asm_lds = []
        v2p = None
        for g in range(NTL):
            lds = []
            for hg in range(16):
                seg = (hg // 8) * (HPC * 2049) + (hg % 8) * 2049
                for hf in range(2):
                    ldw = [(s_cc, ag_c)]
                    if v2p is not None:
                        ldw.append((s_ve, v2p))
                    lds.append(dma("scalar",
                                   Dd[:, hg, hf * 512:hf * 512 + 128],
                                   d_ctx_pag[seg + hf * 1024 + g * 128:
                                             seg + hf * 1024 + (g + 1) * 128, :],
                                   wait=ldw))
            asm_lds = lds
            xw = [(s_sy, out_st[g - 2])] if g >= 2 else []
            ldx = dma("sync", a_x32[:, g % 2, :], x_in[g * 128:(g + 1) * 128, :],
                      wait=xw)
            scr = Cc[:, 0:2, :].rearrange("p a b -> p (a b)").rearrange(
                "p (a b) -> p a b", b=128)
            v1 = vop(lambda e: e.tensor_tensor(
                out=scr[:], in0=Dd[:, 0:16, 512:640],
                in1=Dd[:, 0:16, 0:128], op=AL.subtract),
                wait=[(s_sc, lds[-1])])
            v2 = vop(lambda e: e.scalar_tensor_tensor(
                out=scr[:], in0=scr[:],
                scalar=c_cc[:, 1:2], in1=Dd[:, 0:16, 0:128],
                op0=AL.mult, op1=AL.add))
            v3 = vop(lambda e, g=g: e.tensor_tensor(
                out=a_x32[:, g % 2, :], in0=a_x32[:, g % 2, :],
                in1=scr.rearrange("p a b -> p (a b)"), op=AL.add),
                wait=[(s_sy, ldx)])
            v2p = v2
            out_st.append(dma("sync", x_out[g * 128:(g + 1) * 128, :],
                              a_x32[:, g % 2, :], wait=[(s_ve, v3)]))
        R["xln_my"] = [(s_sc, xlT_all)]
        R["xba"] = [(s_sc, xgT_all), (s_po, p.val("s_po"))]
        R["ctx"] = [(s_sc, asm_lds[-1])]
        return out_st[-1]

    if STAGE == 1:
        emit_ffn(d_x0, d_out, 1)
    elif STAGE == 2:
        emit_ffn(d_x0, d_x1, 1)
        fence()
        emit_attn(d_x1, d_out)
    else:
        emit_ffn(d_x0, d_x1, 1)
        fence()
        emit_attn(d_x1, d_x2)
        fence()
        emit_ffn(d_x2, d_out, 2)

    p.op("gpsimd", None, wait=[(s_sy, p.val("s_sy")), (s_sc, p.val("s_sc")),
                               (s_po, p.val("s_po"))], inc=None, drain=False)
    p.streams["gpsimd"].append(
        (None, ((s_sy, p.val("s_sy")),), None, 0, False))

    p.build_block()
    p.ctx.close()
    return nc


def kernel(**inputs):
    x = np.ascontiguousarray(np.asarray(inputs["L0_attn_hidden_states"],
                                        dtype=np.float32).reshape(T, D))
    wq = np.asarray(inputs["L0_attn_Wq"], dtype=np.float32)
    wk = np.asarray(inputs["L0_attn_Wk"], dtype=np.float32)
    wv = np.asarray(inputs["L0_attn_Wv"], dtype=np.float32)
    bq = np.asarray(inputs["L0_attn_bq"], dtype=np.float32).reshape(D, 1)
    bk = np.asarray(inputs["L0_attn_bk"], dtype=np.float32).reshape(D, 1)
    bv = np.asarray(inputs["L0_attn_bv"], dtype=np.float32).reshape(1, D)
    rw = np.asarray(inputs["L0_ffn_router_w"], dtype=np.float32)
    ebias = np.asarray(inputs["L0_ffn_e_bias"], dtype=np.float32)
    wg = np.asarray(inputs["L0_ffn_W_gate"], dtype=np.float32)
    wu = np.asarray(inputs["L0_ffn_W_up"], dtype=np.float32)
    wd = np.asarray(inputs["L0_ffn_W_down"], dtype=np.float32)
    sg = np.asarray(inputs["L0_ffn_shared_gate"], dtype=np.float32)
    su = np.asarray(inputs["L0_ffn_shared_up"], dtype=np.float32)
    sd = np.asarray(inputs["L0_ffn_shared_down"], dtype=np.float32)

    sgT = np.ascontiguousarray(sg.T)
    suT = np.ascontiguousarray(su.T)
    sdT = np.ascontiguousarray(sd.T)
    rwT = np.ascontiguousarray(rw.T)

    nc = build_program()
    in_maps = []
    for c in range(8):
        hs = slice((c % 2) * HPC * DH, ((c % 2) + 1) * HPC * DH)
        sel = np.zeros((EPC, E), np.float32)
        for el in range(EPC):
            sel[el, EPC * c + el] = 1.0
        sel = np.ascontiguousarray(
            np.broadcast_to(sel[None], (128, EPC, E)).reshape(128, EPC * E))
        cconst = np.zeros((128, 4), np.float32)
        cconst[:, 0] = EPC * c
        cconst[:, 1] = c % 2
        rmask = np.zeros((8, 1), np.float32)
        rmask[:c, 0] = 1.0
        in_maps.append({
            "x0": x[c * TL:(c + 1) * TL],
            "wq": np.ascontiguousarray(wq[:, hs]),
            "wk": np.ascontiguousarray(wk[:, hs]),
            "wv": np.ascontiguousarray(wv[:, hs]),
            "bq": np.ascontiguousarray(bq[hs]),
            "bk": np.ascontiguousarray(bk[hs]),
            "bv": np.ascontiguousarray(bv[:, hs]),
            "rwT": rwT,
            "ebias": np.ascontiguousarray(
                np.broadcast_to(ebias[None, :], (128, E))),
            "wg": np.ascontiguousarray(wg[EPC * c:EPC * (c + 1)]),
            "wu": np.ascontiguousarray(wu[EPC * c:EPC * (c + 1)]),
            "wd": np.ascontiguousarray(wd[EPC * c:EPC * (c + 1)]),
            "sgT": sgT, "suT": suT, "sdT": sdT,
            "sel": sel, "cconst": cconst, "rmask": rmask,
        })
    res = run_bass_kernel_spmd(nc, in_maps, core_ids=list(range(8)))
    out = np.concatenate([res.results[c]["xout"] for c in range(8)], axis=0)
    return out.reshape(B, S, D)
